# version 44
# baseline (speedup 1.0000x reference)
import numpy as np
from contextlib import ExitStack

import concourse.bass as bass
import concourse.mybir as mybir
from concourse.bass_utils import run_bass_kernel_spmd

F32 = mybir.dt.float32
BF16 = mybir.dt.bfloat16
AX = mybir.AxisListType.X
AF = mybir.ActivationFunctionType
ALU = mybir.AluOpType

MAGIC = 12582912.0
D = 1024
KB = 8
H = 16
C = 128
NT = 4096
NCH = 32
EPS = 1e-5
QMAX = 127.0


class Plan:

    ENGINES = ("sync", "scalar", "vector", "tensor", "gpsimd")
    OWN = {"vector": "v", "scalar": "s", "gpsimd": "g"}

    def __init__(self):
        self.steps = {e: [] for e in self.ENGINES}
        self.cnt = {k: 0 for k in ("dL0", "dL1", "dL2", "dL3", "dL4", "dS0", "dS1", "dS2", "dGwq", "dGwk", "dGwv", "dGwo", "dGr", "dR2", "dW0", "dW1", "dGv0", "dGv1", "dGm", "v", "s", "t", "g", "c", "dbg")}

    def step(self, eng, fn, deps=(), inc=None, inc_by=1):
        chain = None
        if eng in self.OWN and inc is None:
            inc = self.OWN[eng]
        if eng in self.OWN and inc == self.OWN[eng]:
            chain = (self.OWN[eng], self.cnt[self.OWN[eng]])
        val = None
        if inc is not None:
            self.cnt[inc] += inc_by
            val = self.cnt[inc]
        self.steps[eng].append((tuple(deps), fn, inc, inc_by, chain))
        return val

    def run(self, eng_name, eng, sems):
        watermark = {}
        for deps, fn, inc, inc_by, chain in self.steps[eng_name]:
            alldeps = list(deps)
            if chain is not None and chain[1] > 0:
                alldeps.append(chain)
            for s, v in alldeps:
                if v and watermark.get(s, -1) < v:
                    eng.wait_ge(sems[s], v)
                    watermark[s] = v
            inst = fn(eng)
            if inc is not None:
                inst.then_inc(sems[inc], inc_by)


def build_nc(upto='full', alvl=10, blvl=9, dbg_dump=False):
    nc = bass.Bass("TRN2", target_bir_lowering=False)

    xs_h = nc.dram_tensor("xs", [NT, D], F32, kind="ExternalInput").ap()
    xh_h = nc.dram_tensor("xh", [1, D], F32, kind="ExternalInput").ap()
    cm_h = nc.dram_tensor("cm", [1, 1], F32, kind="ExternalInput").ap()
    wq_h = nc.dram_tensor("wq", [D, D], F32, kind="ExternalInput").ap()
    wk_h = nc.dram_tensor("wk", [D, D], F32, kind="ExternalInput").ap()
    wv_h = nc.dram_tensor("wv", [D, D], F32, kind="ExternalInput").ap()
    wo_h = nc.dram_tensor("wo", [D, D], F32, kind="ExternalInput").ap()
    bq_h = nc.dram_tensor("bq", [1, D], F32, kind="ExternalInput").ap()
    bk_h = nc.dram_tensor("bk", [1, D], F32, kind="ExternalInput").ap()
    bv_h = nc.dram_tensor("bv", [1, D], F32, kind="ExternalInput").ap()
    bo_h = nc.dram_tensor("bo", [1, D], F32, kind="ExternalInput").ap()
    dec_h = nc.dram_tensor("dec", [1, H], F32, kind="ExternalInput").ap()
    wf_h = nc.dram_tensor("wf", [D, D], F32, kind="ExternalInput").ap()
    gam_h = nc.dram_tensor("gam", [1, D], F32, kind="ExternalInput").ap()
    bet_h = nc.dram_tensor("bet", [1, D], F32, kind="ExternalInput").ap()
    y_h = nc.dram_tensor("y", [NT, D], F32, kind="ExternalOutput").ap()
    x2_h = nc.dram_tensor("x2sp", [NT, D], F32).ap()
    cc_in = nc.dram_tensor("cc_in", [1], F32).ap()
    cc_out = nc.dram_tensor("cc_out", [8], F32).ap()
    dmp = {}
    if dbg_dump:
        for nmm, shp, dt in [("d_ln1bf", [128, 1024], BF16), ("d_lnT", [128, 2048], BF16),
                             ("d_qT", [128, 2048], BF16), ("d_kT", [128, 2064], BF16),
                             ("d_vsb", [128, 2048], BF16), ("d_vc", [1, 1024], BF16),
                             ("d_psb", [128, 2064], BF16), ("d_pT", [128, 2048], BF16),
                             ("d_pcT", [1, 2048], BF16), ("d_nm", [128, 16], F32),
                             ("d_dn", [128, 16], F32), ("d_rc", [128, 16], F32),
                             ("d_ato", [128, 1024], BF16), ("d_atoT", [128, 1024], BF16),
                             ("d_xin0", [128, 1024], F32), ("d_st1", [128, 9], F32)]:
            dmp[nmm] = nc.dram_tensor(nmm, shp, dt, kind="ExternalOutput").ap()

    p = Plan()

    with ExitStack() as st:
        e = st.enter_context

        wq = e(nc.sbuf_tensor("wq_sb", [128, KB * D], BF16))
        wk = e(nc.sbuf_tensor("wk_sb", [128, KB * D], BF16))
        wv = e(nc.sbuf_tensor("wv_sb", [128, KB * D], BF16))
        wo = e(nc.sbuf_tensor("wo_sb", [128, KB * D], BF16))
        bqb = e(nc.sbuf_tensor("bqb", [1, D], BF16))
        bkb = e(nc.sbuf_tensor("bkb", [1, D], BF16))
        bvb = e(nc.sbuf_tensor("bvb", [1, D], BF16))
        bob = e(nc.sbuf_tensor("bob", [1, D], BF16))
        gam_r = e(nc.sbuf_tensor("gam_r", [1, D], F32))
        bet_r = e(nc.sbuf_tensor("bet_r", [1, D], F32))
        dec_r = e(nc.sbuf_tensor("dec_r", [1, H], F32))
        cm_sb = e(nc.sbuf_tensor("cm_sb", [1, 1], F32))
        xh_sb = e(nc.sbuf_tensor("xh_sb", [1, D], F32))
        gamb = e(nc.sbuf_tensor("gamb", [128, D], F32))
        betb = e(nc.sbuf_tensor("betb", [128, D], F32))
        dect = e(nc.sbuf_tensor("dect", [128, 2048], F32))
        d8rep = e(nc.sbuf_tensor("d8rep", [128, 128], F32))
        dpp = e(nc.sbuf_tensor("dpp", [128, 16], F32))
        idn = e(nc.sbuf_tensor("idn", [128, 128], BF16))
        idnf = e(nc.sbuf_tensor("idnf", [128, 128], F32))
        onesb = e(nc.sbuf_tensor("onesb", [1, 256], BF16))
        onesf = e(nc.sbuf_tensor("onesf", [1, 128], F32))
        onecf = e(nc.sbuf_tensor("onecf", [128, 1], F32))
        cmpp = e(nc.sbuf_tensor("cmpp", [128, 1], F32))
        lnhT = e(nc.sbuf_tensor("lnhT", [128, KB], BF16))
        bqT = e(nc.sbuf_tensor("bqT", [128, KB], F32))
        bkT = e(nc.sbuf_tensor("bkT", [128, KB], F32))
        lnh_bf = e(nc.sbuf_tensor("lnh_bf", [1, D], BF16))
        xin = [e(nc.sbuf_tensor(f"xin{i}", [128, D], F32)) for i in range(5)]
        ln1bf = [e(nc.sbuf_tensor(f"ln1bf{i}", [128, D], BF16)) for i in range(2)]
        ln1T = [e(nc.sbuf_tensor(f"ln1T{i}", [128, 2048], BF16)) for i in range(2)]
        qT = [e(nc.sbuf_tensor(f"qT{i}", [128, 2048], BF16)) for i in range(2)]
        kT = [e(nc.sbuf_tensor(f"kT{i}", [128, 2064], BF16)) for i in range(2)]
        vsb = [e(nc.sbuf_tensor(f"vsb{i}", [128, 2048], BF16)) for i in range(2)]
        vcar = [e(nc.sbuf_tensor(f"vcar{i}", [1, D], BF16)) for i in range(2)]
        psb = e(nc.sbuf_tensor("psb", [128, H * 129], BF16))
        pTs = e(nc.sbuf_tensor("pTs", [128, H * 128], BF16))
        pcT = e(nc.sbuf_tensor("pcT", [1, H * 128], BF16))
        dnm = [e(nc.sbuf_tensor(f"dnm{i}", [128, H], F32)) for i in range(2)]
        nmx = [e(nc.sbuf_tensor(f"nmx{i}", [128, H], F32)) for i in range(2)]
        rcp = [e(nc.sbuf_tensor(f"rcp{i}", [128, H], F32)) for i in range(2)]
        ato = e(nc.sbuf_tensor("ato", [128, D], BF16))
        atoT = e(nc.sbuf_tensor("atoT", [128, D], BF16))
        x2st = [e(nc.sbuf_tensor(f"x2st{i}", [128, D], F32)) for i in range(3)]
        sqs = e(nc.sbuf_tensor("sqs", [128, D], F32))
        xh2 = [e(nc.sbuf_tensor("xh2", [128, D], F32))]
        xng = e(nc.sbuf_tensor("xng", [128, D], F32))
        st1 = e(nc.sbuf_tensor("st1", [128, 18], F32))
        epsc = e(nc.sbuf_tensor("epsc", [128, 1], F32))
        st2 = e(nc.sbuf_tensor("st2", [128, 8], F32))
        nm2a = e(nc.sbuf_tensor("nm2a", [128, NCH], F32))
        rs2a = e(nc.sbuf_tensor("rs2a", [128, NCH], F32))
        b2a = e(nc.sbuf_tensor("b2a", [128, NCH], F32))
        amr = e(nc.sbuf_tensor("amr", [128, 1], F32))
        amt = e(nc.sbuf_tensor("amt", [128, 1], F32))
        gmx = e(nc.sbuf_tensor("gmx", [1, 8], F32))
        sca = e(nc.sbuf_tensor("sca", [1, 4], F32))
        ivpp = e(nc.sbuf_tensor("ivpp", [128, 1], F32))
        sapp = e(nc.sbuf_tensor("sapp", [128, 1], F32))
        kcs = e(nc.sbuf_tensor("kcs", [128, 8], BF16))
        hsc = e(nc.sbuf_tensor("hsc", [1, 8], F32))
        csab = e(nc.sbuf_tensor("csab", [128, 8], F32))
        cnta = e(nc.sbuf_tensor("cnta", [128, 8], F32))
        suma = e(nc.sbuf_tensor("suma", [128, 8], F32))
        tots = e(nc.sbuf_tensor("tots", [1, 8], F32))
        dlpp = e(nc.sbuf_tensor("dlpp", [128, 1], F32))

        wT = wq
        tbf = wk
        rT = [wv[:, 0:1024], wv[:, 1024:2048]]
        rbf = [wv[:, 2048:3072], wv[:, 3072:4096]]
        sgnb = wv[:, 4096:5120]
        wfs = wo[:, 0:2048].bitcast(F32)
        absb = wo[:, 2048:4096].bitcast(F32)
        mskb = wo[:, 4096:6144].bitcast(F32)
        rpre = [qT[0][:, :].bitcast(F32), qT[1][:, :].bitcast(F32)]
        yt = [ln1T[0][:, :].bitcast(F32), ln1T[1][:, :].bitcast(F32)]
        ysb = [vsb[0][:, :].bitcast(F32), vsb[1][:, :].bitcast(F32)]

        Pm = e(nc.psum_tensor("Pm", [128, 1024], BF16))
        Pq = e(nc.psum_tensor("Pq", [128, 512], F32))
        Pk = Pq
        Pv1 = e(nc.psum_tensor("Pv1", [128, 512], F32))
        Pv = [Pv1, Pv1]
        Pat = e(nc.psum_tensor("Pat", [128, 512], F32))
        Psc = [e(nc.psum_tensor(f"Psc{i}", [128, 512], F32)) for i in range(4)]

        sems = {}
        for sname in ("dL0", "dL1", "dL2", "dL3", "dL4", "dS0", "dS1", "dS2", "dGwq", "dGwk", "dGwv", "dGwo", "dGr", "dR2", "dW0", "dW1", "dGv0", "dGv1", "dGm", "v", "s", "t", "g", "c", "dbg"):
            sems[sname] = e(nc.semaphore(sname))
        block = e(nc.Block())

        g1 = p.step("gpsimd", lambda g: g.memset(idn[:, :], 0.0), inc="g")
        p.step("gpsimd", lambda g: g.affine_select(
            out=idn[:, :], in_=idn[:, :], compare_op=ALU.not_equal, fill=1.0,
            base=0, pattern=[[-1, 128]], channel_multiplier=1),
            deps=[("g", g1)], inc="g")
        g2 = p.step("gpsimd", lambda g: g.memset(idnf[:, :], 0.0), inc="g")
        p.step("gpsimd", lambda g: g.affine_select(
            out=idnf[:, :], in_=idnf[:, :], compare_op=ALU.not_equal, fill=1.0,
            base=0, pattern=[[-1, 128]], channel_multiplier=1),
            deps=[("g", g2)], inc="g")
        p.step("gpsimd", lambda g: g.memset(onesb[:, :], 1.0), inc="g")
        p.step("gpsimd", lambda g: g.memset(onesf[:, :], 1.0), inc="g")
        g_const = p.step("gpsimd", lambda g: g.memset(onecf[:, :], 1.0), inc="g")
        g_eps = p.step("gpsimd", lambda g: g.memset(epsc[:, :], EPS), inc="g")

        for dst, src in [(gam_r, gam_h), (bet_r, bet_h), (dec_r, dec_h),
                         (cm_sb, cm_h), (xh_sb, xh_h)]:
            p.step("sync", (lambda d=dst, s_=src: (lambda s2:
                s2.dma_start(out=d[0:1, :], in_=s_)))(), inc="dR2", inc_by=16)
        mk_small = ("dR2", p.cnt["dR2"])
        for dst, src in [(bqb, bq_h), (bkb, bk_h), (bvb, bv_h), (bob, bo_h)]:
            p.step("gpsimd", (lambda d=dst, s_=src: (lambda g:
                g.dma_start(out=d[0:1, :], in_=s_)))(), inc="dGr", inc_by=16)
        mk_bias = ("dGr", p.cnt["dGr"])
        for lane, w_h2, w_sb in (("dGwq", wq_h, wq), ("dGwk", wk_h, wk),
                                 ("dGwv", wv_h, wv), ("dGwo", wo_h, wo)):
            for kb in range(KB):
                p.step("gpsimd", (lambda wh=w_h2, ws=w_sb, kb=kb: (lambda g:
                    g.dma_start(out=ws[:, kb * D:(kb + 1) * D],
                                in_=wh[kb * 128:(kb + 1) * 128, :])))(),
                    inc=lane, inc_by=16)
        mk_wq = ("dGwq", p.cnt["dGwq"])
        mk_wk = ("dGwk", p.cnt["dGwk"])
        mk_wv = ("dGwv", p.cnt["dGwv"])
        mk_wo = ("dGwo", p.cnt["dGwo"])

        dL_x = {}
        for c in range(5):
            lane = "dL%d" % c
            dL_x[c] = (lane, p.step("sync", (lambda c=c: (lambda s_:
                s_.dma_start(out=xin[c][:, :], in_=xs_h[c * 128:(c + 1) * 128, :])))(),
                inc=lane, inc_by=16))

        t_cm = p.step("tensor", lambda t: t.matmul(
            Pq[:, 0:1], onesf[0:1, :], cm_sb[0:1, 0:1], start=True, stop=True),
            deps=[mk_small, ("g", g_const)], inc="t")
        v_cm = p.step("vector", lambda v: v.tensor_copy(cmpp[:, 0:1], Pq[:, 0:1]),
                      deps=[("t", t_cm)], inc="v")
        v_prev = None
        for dst, srcrow in ((gamb, gam_r), (betb, bet_r)):
            for half in range(2):
                dep = [("v", v_prev)] if v_prev else [mk_small]
                t_b = p.step("tensor", (lambda h=half, r=srcrow: (lambda t: t.matmul(
                    Pv[0][:, 0:512], onesf[0:1, :], r[0:1, h * 512:(h + 1) * 512],
                    start=True, stop=True)))(), deps=dep, inc="t")
                v_prev = p.step("vector", (lambda h=half, d=dst: (lambda v: v.tensor_copy(
                    d[:, h * 512:(h + 1) * 512], Pv[0][:, :])))(),
                    deps=[("t", t_b)], inc="v")
        v_bb2 = v_prev
        t_de = p.step("tensor", lambda t: t.matmul(
            Pk[:, 0:8], onesf[0:1, :],
            dec_r[0:1, :].rearrange("p (a b) -> p a b", b=2)[:, :, 0],
            start=True, stop=True), deps=[mk_small], inc="t")
        t_do = p.step("tensor", lambda t: t.matmul(
            Pk[:, 8:16], onesf[0:1, :],
            dec_r[0:1, :].rearrange("p (a b) -> p a b", b=2)[:, :, 1],
            start=True, stop=True), inc="t")
        p.step("vector", lambda v: v.tensor_copy(dpp[0:64, 0:16:2], Pk[0:64, 0:8]),
               deps=[("t", t_do)], inc="v")
        p.step("vector", lambda v: v.tensor_copy(dpp[64:128, 0:16:2], Pk[64:128, 8:16]),
               inc="v")
        p.step("vector", lambda v: v.tensor_copy(dpp[0:64, 1:16:2], Pk[0:64, 0:8]),
               inc="v")
        v_d4 = p.step("vector", lambda v: v.tensor_copy(dpp[64:128, 1:16:2], Pk[64:128, 8:16]),
                      inc="v")
        p.step("vector", lambda v: v.memset(
            dect[:, :].rearrange("p (a b) -> p a b", b=128)[:, :, 0], 0.125))

        dview = dect[:, :].rearrange("p (a b) -> p a b", b=128)
        for tp in range(1, 8):
            p.step("vector", (lambda tp=tp: (lambda v: v.tensor_tensor(
                dview[:, :, tp], dview[:, :, tp - 1], dpp[:, :], op=ALU.mult)))())
        for _ in range(3):
            p.step("vector", lambda v: v.tensor_tensor(
                dpp[:, :], dpp[:, :], dpp[:, :], op=ALU.mult))
        r8 = d8rep[:, :].rearrange("p (a k) -> p a k", k=8)
        p.step("vector", lambda v: v.tensor_copy(r8[:, :, 0:1], dpp[:, :].rearrange(
            "p (a k) -> p a k", k=1)))
        p.step("vector", lambda v: v.tensor_copy(r8[:, :, 1:2], r8[:, :, 0:1]))
        p.step("vector", lambda v: v.tensor_copy(r8[:, :, 2:4], r8[:, :, 0:2]))
        p.step("vector", lambda v: v.tensor_copy(r8[:, :, 4:8], r8[:, :, 0:4]))
        dblk = dect[:, :].rearrange("p (a c b) -> p a c b", a=16, c=16)
        v_dect = None
        for m in range(1, 16):
            v_dect = p.step("vector", (lambda m=m: (lambda v: v.tensor_tensor(
                dblk[:, :, m, :], dblk[:, :, m - 1, :], r8[:, :, :],
                op=ALU.mult)))(), inc="v" if m == 15 else None)

        t_bt = None
        for i, brow in enumerate((bqb, bkb)):
            for kb in range(KB):
                dep = [mk_bias, ("g", g_const)] if (i == 0 and kb == 0) else ()
                t_bt = p.step("tensor", (lambda brow=brow, i=i, kb=kb: (lambda t: t.transpose(
                    Pm[:, 32 + 2 * (KB * i + kb):32 + 2 * (KB * i + kb) + 1],
                    brow[0:1, kb * 128:(kb + 1) * 128],
                    onesb[0:1, 0:1])))(),
                    deps=dep, inc="t" if (i == 1 and kb == KB - 1) else None)
        p.step("vector", lambda v: v.tensor_copy(bqT[:, :], Pm[:, 32:32 + 2 * KB:2]),
               deps=[("t", t_bt)], inc="v")
        v_bt2 = p.step("vector", lambda v: v.tensor_copy(
            bkT[:, :], Pm[:, 32 + 2 * KB:32 + 4 * KB:2]), inc="v")
        free = {
            "Pm": [("v", v_bt2)],
            "Pq": [("v", v_d4)],
            "Pv": [("v", v_bb2)],
            "Pat": [],
        }
        ln1T_free = [[], []]
        qT_free = [[], []]
        kT_free = [[], []]
        vsb_free = [[], []]
        vcar_free = [[], []]
        xin_free = {i: [] for i in range(5)}
        psb_free = []
        pTs_free = []
        ato_free_box = [[]]
        atoT_free = []
        x2st_free = [[], [], []]
        xh2_free = [[]]
        ln1bf_free = [[], []]

        s_ln1 = {}
        v_lnT = {}
        s_kev = {}
        qk_t = {}
        psc_free = [[], []]
        xng_free = [[]]
        v_kcar = {}
        s_vev = {}
        dG_vcar = {}
        s_aev = {}
        v_resid = {}
        dS_x2 = {}
        g_xn = {}
        v_b2 = {}

        def _halo_late():
            p.step("vector", lambda v: v.reduce_sum(hsc[0:1, 0:1], xh_sb[0:1, :], axis=AX),
                   deps=[mk_small], inc="v")
            _ = 0
            p.step("vector", lambda v: v.tensor_scalar(hsc[0:1, 1:2], hsc[0:1, 0:1],
                                                       -1.0 / D, None, op0=ALU.mult))
            s_h1 = p.step("scalar", lambda s_: s_.activation(
                sqs[0:1, :], xh_sb[0:1, :], AF.Square),
                deps=[mk_small, sqs_free_box[0]], inc="s")
            v_h2 = p.step("vector", lambda v: v.reduce_sum(hsc[0:1, 2:3], sqs[0:1, :], axis=AX),
                          deps=[("s", s_h1)], inc="v")
            p.step("vector", lambda v: v.tensor_scalar(hsc[0:1, 3:4], hsc[0:1, 1:2],
                                                       hsc[0:1, 1:2], None, op0=ALU.mult))
            p.step("vector", lambda v: v.tensor_scalar(hsc[0:1, 4:5], hsc[0:1, 2:3],
                                                       1.0 / D, EPS, op0=ALU.mult, op1=ALU.add))
            p.step("vector", lambda v: v.tensor_tensor(hsc[0:1, 5:6], hsc[0:1, 4:5],
                                                       hsc[0:1, 3:4], op=ALU.subtract))
            v_h3 = p.step("vector", lambda v: v.tensor_copy(hsc[0:1, 6:7], hsc[0:1, 5:6]),
                          inc="v")
            s_hl = p.step("scalar", lambda s_: s_.activation(
                hsc[0:1, 6:7], hsc[0:1, 5:6], AF.Ln), deps=[("v", v_h3)], inc="s")
            s_h2 = p.step("scalar", lambda s_: s_.activation(
                hsc[0:1, 7:8], hsc[0:1, 6:7], AF.Exp, scale=-0.5),
                deps=[("s", s_hl)], inc="s")
            v_h4 = p.step("vector", lambda v: v.tensor_scalar(
                hsc[0:1, 3:4], hsc[0:1, 1:2], hsc[0:1, 7:8], None, op0=ALU.mult),
                deps=[("s", s_h2)], inc="v")
            s_h3 = p.step("scalar", lambda s_: s_.activation(
                lnh_bf[0:1, :], xh_sb[0:1, :], AF.Identity,
                bias=hsc[0:1, 3:4], scale=hsc[0:1, 7:8]), deps=[("v", v_h4)], inc="s")
            t_lnhT = None
            for kb in range(KB):
                dep = [("s", s_h3)] + free["Pm"] if kb == 0 else ()
                t_lnhT = p.step("tensor", (lambda kb=kb: (lambda t: t.transpose(
                    Pm[:, 2 * kb:2 * kb + 1], lnh_bf[0:1, kb * 128:(kb + 1) * 128],
                    onesb[0:1, 0:1])))(),
                    deps=dep, inc="t" if kb == KB - 1 else None)
            v_lnhT = p.step("vector", lambda v: v.tensor_copy(lnhT[:, :], Pm[:, 0:2 * KB:2]),
                            deps=[("t", t_lnhT)], inc="v")
            t_kh = None
            for mb in range(KB):
                for kb in range(KB):
                    dep = []
                    if mb == 0 and kb == 0:
                        dep = [("v", v_lnhT), mk_wk, mk_small, mk_bias, ("v", v_cm)] + free["Pq"]
                    p.step("tensor", (lambda mb=mb, kb=kb: (lambda t: t.matmul(
                        Pq[:, mb:mb + 1], wk[:, kb * D + mb * 128:kb * D + (mb + 1) * 128],
                        lnhT[:, kb:kb + 1], start=(kb == 0), stop=False)))(), deps=dep)
                t_kh = p.step("tensor", (lambda mb=mb: (lambda t: t.matmul(
                    Pq[:, mb:mb + 1], bkb[0:1, mb * 128:(mb + 1) * 128],
                    onesb[0:1, 0:1], start=False, stop=True)))(),
                    inc="t" if mb == KB - 1 else None)
            v_kh = p.step("vector", lambda v: v.tensor_scalar(
                kcs[:, :], Pq[:, 0:KB], cmpp[:, 0:1], None, op0=ALU.mult),
                deps=[("t", t_kh)], inc="v")
            v_kh2 = p.step("vector", lambda v: v.tensor_copy(
                kT[0][:, :].rearrange("p (a b) -> p a b", b=258)[:, :, 0], kcs[:, :]),
                deps=[("v", v_kh)], inc="v")
            t_vh = None
            for half in range(2):
                for kb in range(KB):
                    dep = ([("v", v_kh), mk_wv] + free["Pv"]) if (half == 0 and kb == 0) else ()
                    p.step("tensor", (lambda half=half, kb=kb: (lambda t: t.matmul(
                        Pv[half][0:1, :], lnhT[:, kb:kb + 1],
                        wv[:, kb * D + half * 512:kb * D + (half + 1) * 512],
                        start=(kb == 0), stop=False)))(), deps=dep)
                t_vh = p.step("tensor", (lambda half=half: (lambda t: t.matmul(
                    Pv[half][0:1, :], onesb[0:1, 0:1],
                    bvb[0:1, half * 512:(half + 1) * 512], start=False, stop=True)))(),
                    inc="t" if half == 1 else None)
            v_vh = None
            for half in range(2):
                v_vh = p.step("vector", (lambda half=half: (lambda v: v.tensor_scalar(
                    vcar[0][0:1, half * 512:(half + 1) * 512], Pv[half][0:1, :],
                    cm_sb[0:1, 0:1], None, op0=ALU.mult)))(), deps=[("t", t_vh)], inc="v")
            p.step("vector", lambda v: v.memset(amr[:, :], 0.0))


            free["Pm"] = [("v", v_lnhT)]
            free["Pq"] = [("v", v_kh)]
            free["Pv"] = [("v", v_vh)]
            kT_free[0] = [("v", v_kh2)]
            dG_vcar[0] = [("v", v_vh)]
            sqs_free_box[0] = ("v", v_h2)

        if upto == "pro":
            return _emit(p, block, sems, nc)
        sqs_free_box = [mk_small]
        ln1v_free = [[], []]

        def ln1_stats(cc):
            cp = cc % 2
            slot = cc % 5
            o = cp * 8
            depx = [dL_x[cc]]
            v0 = p.step("vector", (lambda slot=slot, o=o: (lambda v: v.reduce_sum(
                st1[:, o:o + 1], xin[slot][:, :], axis=AX)))(),
                deps=depx + ln1v_free[cp], inc="v")
            ln1v_free[cp] = []
            s1 = p.step("scalar", (lambda slot=slot, o=o: (lambda s_: s_.activation(
                sqs[:, :], xin[slot][:, :], AF.Square,
                accum_out=st1[:, o + 2:o + 3])))(),
                deps=depx + [sqs_free_box[0]], inc="s")
            sqs_free_box[0] = ("s", s1)
            sm1 = p.step("scalar", (lambda o=o: (lambda s_: s_.activation(
                st1[:, o + 1:o + 2], st1[:, o:o + 1], AF.Identity,
                scale=-1.0 / D)))(), deps=[("v", v0)], inc="s")
            ln1v_free[cp] = [("s", sm1)]
            p.step("scalar", (lambda o=o: (lambda s_: s_.activation(
                st1[:, o + 3:o + 4], st1[:, o + 1:o + 2], AF.Square)))())
            p.step("scalar", (lambda o=o: (lambda s_: s_.activation(
                st1[:, o + 4:o + 5], st1[:, o + 2:o + 3], AF.Identity,
                scale=1.0 / D, bias=epsc[:, 0:1])))(), deps=[("g", g_eps)])
            p.step("scalar", (lambda o=o: (lambda s_: s_.activation(
                st1[:, o + 5:o + 6], st1[:, o + 3:o + 4], AF.Identity,
                scale=-1.0, bias=st1[:, o + 4:o + 5])))())
            p.step("scalar", (lambda o=o: (lambda s_: s_.activation(
                st1[:, o + 6:o + 7], st1[:, o + 5:o + 6], AF.Ln)))())
            p.step("scalar", (lambda o=o: (lambda s_: s_.activation(
                st1[:, o + 7:o + 8], st1[:, o + 6:o + 7], AF.Exp, scale=-0.5)))())
            p.step("scalar", (lambda cp=cp, o=o: (lambda s_: s_.activation(
                st1[:, 16 + cp:17 + cp], st1[:, o + 1:o + 2], AF.Identity,
                scale=st1[:, o + 7:o + 8])))())
            s3 = p.step("scalar", (lambda slot=slot, cp=cp, o=o: (lambda s_: s_.activation(
                ln1bf[cp][:, :], xin[slot][:, :], AF.Identity,
                bias=st1[:, 16 + cp:17 + cp], scale=st1[:, o + 7:o + 8])))(),
                deps=ln1bf_free[cp], inc="s")
            s_ln1[cc] = s3
            if dbg_dump and cc == 0:
                p.step("sync", lambda s_: s_.dma_start(out=dmp["d_ln1bf"], in_=ln1bf[0][:, :]),
                       deps=[("s", s3)], inc="dbg", inc_by=16)
                p.step("sync", lambda s_: s_.dma_start(out=dmp["d_xin0"], in_=xin[0][:, :]),
                       deps=[("s", s3)], inc="dbg", inc_by=16)
                p.step("sync", lambda s_: s_.dma_start(out=dmp["d_st1"], in_=st1[:, 0:8]),
                       deps=[("s", s3)], inc="dbg", inc_by=16)
                p.step("sync", lambda s_: s_.dma_start(out=dmp["d_vc"], in_=vcar[0][0:1, :]),
                       deps=[("s", s3)], inc="dbg", inc_by=16)

        def ln1_trans(cc):
            cp = cc % 2
            gpx = (cc // 2) % 2
            cgx = cc % 2
            s3 = s_ln1[cc]
            t_lnT = None
            for kb in range(KB):
                dep = [("s", s3)] + free["Pm"] if kb == 0 else ()
                t_lnT = p.step("tensor", (lambda kb=kb, cp=cp: (lambda t: t.transpose(
                    Pm[:, kb * 128:(kb + 1) * 128], ln1bf[cp][:, kb * 128:(kb + 1) * 128],
                    idn[:, :])))(), deps=dep, inc="t" if kb == KB - 1 else None)

            def lnT_ev(v, gp=gpx, cg=cgx):
                dst = ln1T[gp][:, :].rearrange("p (a b c) -> p a b c", a=KB, b=2)[:, :, cg, :]
                src = Pm[:, 0:1024].rearrange("p (a b) -> p a b", b=128)
                return v.tensor_copy(dst, src)
            dep = [("t", t_lnT)]
            if cgx == 0:
                dep += ln1T_free[gpx]
            vv = p.step("vector", lnT_ev, deps=dep, inc="v")
            v_lnT[cc] = vv
            free["Pm"] = [("v", vv)]
            ln1bf_free[cp] = [("v", vv)]

        def emit_ln1(cc):
            ln1_stats(cc)
            ln1_trans(cc)


        def qk_mm(gq, mb):
            gpq = gq % 2

            def run():
                chq = 2 * gq + 1
                for kb in range(KB):
                    dep = []
                    if kb == 0:
                        dep = [("v", v_lnT[chq]), ("v", v_lnT[chq - 1])] + free["Pq"]
                        if gq == 0 and mb == 0:
                            dep += [mk_wq, mk_wk]
                    p.step("tensor", (lambda mb=mb, kb=kb, gpq=gpq: (lambda t: t.matmul(
                        Pq[:, 0:256], wq[:, kb * D + mb * 128:kb * D + (mb + 1) * 128],
                        ln1T[gpq][:, kb * 256:(kb + 1) * 256],
                        start=(kb == 0), stop=(kb == KB - 1))))(), deps=dep)
                tk = None
                for kb in range(KB):
                    tk = p.step("tensor", (lambda mb=mb, kb=kb, gpq=gpq: (lambda t: t.matmul(
                        Pq[:, 256:512], wk[:, kb * D + mb * 128:kb * D + (mb + 1) * 128],
                        ln1T[gpq][:, kb * 256:(kb + 1) * 256],
                        start=(kb == 0), stop=(kb == KB - 1))))(),
                        inc="t" if kb == KB - 1 else None)
                qk_t[(gq, mb)] = tk
            return run

        def qk_ev(gq, mb):
            gpq = gq % 2

            def run():
                dep = [("t", qk_t[(gq, mb)])]
                if mb == 0:
                    dep += [("v", v_dect)] + qT_free[gpq]
                vq = p.step("vector", (lambda mb=mb, gpq=gpq: (lambda v: v.scalar_tensor_tensor(
                    qT[gpq][:, mb * 256:(mb + 1) * 256], Pq[:, 0:256],
                    bqT[:, mb:mb + 1],
                    dect[:, mb * 256:(mb + 1) * 256],
                    op0=ALU.add, op1=ALU.mult)))(),
                    deps=dep, inc="v")
                dep = []
                if mb == 0:
                    dep += kT_free[gpq]

                def kev(v_, mb=mb, gpq=gpq):
                    dst = kT[gpq][:, :].rearrange(
                        "p (m c k) -> p m c k", m=KB, c=2)[:, mb, :, 1:129]
                    return v_.tensor_scalar(dst, Pq[:, 256:512].rearrange(
                        "p (a b) -> p a b", b=128), bkT[:, mb:mb + 1],
                        None, op0=ALU.add)
                sk = p.step("vector", kev, deps=dep, inc="v")
                free["Pq"] = [("v", sk)]
                s_kev[(gq, mb)] = ("v", sk)
            return run

        def kcar_bundle(gq):
            gpq = gq % 2

            def run():
                kdep = [s_kev[(gq, KB - 1)]]

                def kcar0(v, gpq=gpq):
                    src = kT[1 - gpq][:, :].rearrange("p (a b) -> p a b", b=258)[:, :, 257]
                    dst = kT[gpq][:, :].rearrange("p (a b) -> p a b", b=258)[:, :, 0]
                    return v.tensor_copy(dst, src)

                def kcar1(v, gpq=gpq):
                    src = kT[gpq][:, :].rearrange("p (a b) -> p a b", b=258)[:, :, 128]
                    dst = kT[gpq][:, :].rearrange("p (a b) -> p a b", b=258)[:, :, 129]
                    return v.tensor_copy(dst, src)
                if gq > 0:
                    p.step("vector", kcar0, deps=kdep, inc="v")
                v_kcar[gq] = p.step("vector", kcar1, deps=kdep, inc="v")
            return run

        def v_bundle(cv, half):
            gpv = (cv // 2) % 2
            cgv = cv % 2

            def run():
                Pb = Pv1 if half == 0 else Pq
                bank = "Pv" if half == 0 else "Pq"
                for kb in range(KB):
                    dep = []
                    if kb == 0:
                        dep = list(free[bank])
                        if cgv == 0 and half == 0:
                            dep += vsb_free[gpv]
                        if cv <= 1 and half == 0:
                            dep += [mk_wv]
                    p.step("tensor", (lambda Pb=Pb, kb=kb, gpv=gpv, cgv=cgv, half=half:
                        (lambda t: t.matmul(
                            Pb[:, 0:512],
                            ln1T[gpv][:, kb * 256 + cgv * 128:kb * 256 + (cgv + 1) * 128],
                            wv[:, kb * D + half * 512:kb * D + (half + 1) * 512],
                            start=(kb == 0), stop=False)))(), deps=dep)
                tv = p.step("tensor", (lambda Pb=Pb, half=half: (lambda t: t.matmul(
                    Pb[:, 0:512], onesb[0:1, 0:128],
                    bvb[0:1, half * 512:(half + 1) * 512],
                    start=False, stop=True)))(), inc="t")
                sv_ = p.step("scalar", (lambda Pb=Pb, half=half, gpv=gpv, cgv=cgv:
                    (lambda s_: s_.activation(
                        vsb[gpv][:, cgv * 1024 + half * 512:cgv * 1024 + (half + 1) * 512],
                        Pb[:, 0:512], AF.Identity)))(),
                    deps=[("t", tv)], inc="s")
                free[bank] = [("s", sv_)]
                if half == 1:
                    s_vev[cv] = sv_
            return run

        ln2m_box = [[]]

        def ln2_bundle(ca):
            cpa = ca % 2
            cp3 = ca % 3
            box = {}

            def run1():
                vres = v_resid[ca]
                _ = cpa
                v0b = p.step("vector", (lambda cp3=cp3: (lambda v: v.reduce_sum(
                    st2[:, 0:1], x2st[cp3][:, :], axis=AX)))(),
                    deps=[("v", vres)] + ln2m_box[0], inc="v")
                s1b = p.step("scalar", (lambda cp3=cp3, ca=ca: (lambda s_: s_.activation(
                    sqs[:, :], x2st[cp3][:, :], AF.Square,
                    accum_out=st2[:, 2:3])))(),
                    deps=[("v", vres), sqs_free_box[0]], inc="s")
                sqs_free_box[0] = ("s", s1b)
                sm1b = p.step("scalar", (lambda ca=ca: (lambda s_: s_.activation(
                    nm2a[:, ca:ca + 1], st2[:, 0:1], AF.Identity,
                    scale=-1.0 / D)))(), deps=[("v", v0b)], inc="s")
                ln2m_box[0] = [("s", sm1b)]
                p.step("scalar", (lambda ca=ca: (lambda s_: s_.activation(
                    st2[:, 3:4], nm2a[:, ca:ca + 1], AF.Square)))())
                p.step("scalar", lambda s_: s_.activation(
                    st2[:, 4:5], st2[:, 2:3], AF.Identity,
                    scale=1.0 / D, bias=epsc[:, 0:1]), deps=[("g", g_eps)])
                p.step("scalar", lambda s_: s_.activation(
                    st2[:, 5:6], st2[:, 3:4], AF.Identity,
                    scale=-1.0, bias=st2[:, 4:5]))
                p.step("scalar", (lambda ca=ca: (lambda s_: s_.activation(
                    st2[:, 6:7], st2[:, 5:6], AF.Ln)))())
                s2b = p.step("scalar", (lambda ca=ca: (lambda s_: s_.activation(
                    rs2a[:, ca:ca + 1], st2[:, 6:7], AF.Exp, scale=-0.5)))(), inc="s")
                s4b = p.step("scalar", (lambda ca=ca: (lambda s_: s_.activation(
                    b2a[:, ca:ca + 1], nm2a[:, ca:ca + 1], AF.Identity,
                    scale=rs2a[:, ca:ca + 1])))(), inc="s")
                _ = s2b
                v_b2[ca] = ("s", s4b)
                box["s1b"] = s1b
                box["s4b"] = s4b

            def run2a():
                s1b = box["s1b"]
                s3b = p.step("scalar", (lambda ca=ca, cpa=cpa, cp3=cp3: (lambda s_: s_.activation(
                    xh2[0][:, :], x2st[cp3][:, :], AF.Identity,
                    bias=b2a[:, ca:ca + 1], scale=rs2a[:, ca:ca + 1])))(),
                    deps=[("s", box["s4b"])] + xh2_free[0], inc="s")
                x2st_free[cp3] = [dS_x2[ca], ("s", s1b), ("s", s3b)]
                dep = [("s", s3b), ("v", v_bb2)] + xng_free[0]
                gx1 = p.step("gpsimd", (lambda cpa=cpa: (lambda g: g.tensor_tensor(
                    xng[:, :], xh2[0][:, :], gamb[:, :], op=ALU.mult)))(),
                    deps=dep, inc="g")
                gx = p.step("gpsimd", lambda g: g.tensor_tensor(
                    xng[:, :], xng[:, :], betb[:, :], op=ALU.add),
                    deps=[("g", gx1)], inc="g")
                xh2_free[0] = [("g", gx)]
                box["gx"] = gx

            def run2b():
                gx = box["gx"]
                p.step("vector", lambda v: v.reduce_max(
                    amt[:, 0:1], xng[:, :], axis=AX, apply_absolute_value=True),
                    deps=[("g", gx)])
                vam = p.step("vector", lambda v: v.tensor_tensor(
                    amr[:, 0:1], amr[:, 0:1], amt[:, 0:1], op=ALU.max), inc="v")
                g_xn[ca] = vam
                xng_free[0] = [("v", vam)]
            return [run1, run2a, run2b]

        emit_ln1(0)
        emit_ln1(1)
        emit_ln1(2)
        emit_ln1(3)
        for mb in range(KB):
            qk_mm(0, mb)()
            qk_ev(0, mb)()
        _halo_late()
        kcar_bundle(0)()
        for cv in (0, 1):
            for half in (0, 1):
                v_bundle(cv, half)()

        def mk_chunk(ca):
            g2 = ca // 2
            gp = g2 % 2
            cga = ca % 2
            cpa = ca % 2
            s_exp = {}
            t_sc = {}
            vn_d = {}
            t_pT = {}
            vp2_d = {}
            sa_mid = {}
            sa_steps = {}

            def emit_S(j):
                if j == 0:
                    dep = [("v", v_kcar[g2]), ("s", s_vev[ca])] + psc_free[0]
                elif j == 1:
                    dep = [("s", s_vev[ca])] + psc_free[1]
                else:
                    dep = [("s", s_exp[j - 2])]
                bk0 = 2 * (j % 2)
                for h2 in range(2):
                    t_sc[j] = p.step("tensor", (lambda j=j, h2=h2, gp=gp, cga=cga, bk0=bk0:
                        (lambda t: t.matmul(
                            Psc[bk0 + h2][:, 0:129],
                            qT[gp][h2 * 64:(h2 + 1) * 64,
                                   j * 256 + cga * 128:j * 256 + (cga + 1) * 128],
                            kT[gp][h2 * 64:(h2 + 1) * 64,
                                   j * 258 + cga * 129:j * 258 + (cga + 1) * 129],
                            start=True, stop=True,
                            tile_position=(h2 * 64, 0))))(),
                        deps=dep if h2 == 0 else (),
                        inc="t" if h2 == 1 else None)

            def emit_X(j):
                bk0 = 2 * (j % 2)
                p.step("vector", (lambda j=j, cpa=cpa, bk0=bk0: (lambda v: v.reduce_max(
                    nmx[cpa][:, 2 * j:2 * j + 1],
                    Psc[bk0][:, 0:129], axis=AX, negate=True)))(),
                    deps=[("t", t_sc[j])], inc="v")
                vn_d[j] = p.step("vector", (lambda j=j, cpa=cpa, bk0=bk0:
                    (lambda v: v.reduce_max(
                        nmx[cpa][:, 2 * j + 1:2 * j + 2],
                        Psc[bk0 + 1][:, 0:129], axis=AX, negate=True)))(), inc="v")
                for h2 in range(2):
                    h = 2 * j + h2
                    dep2 = [("v", vn_d[j])]
                    if j == 0 and h2 == 0:
                        dep2 += psb_free
                    bk0 = 2 * (j % 2)
                    s_exp[j] = p.step("scalar", (lambda h=h, h2=h2, cpa=cpa, bk0=bk0:
                        (lambda s_: s_.activation(
                            psb[:, h * 129:(h + 1) * 129],
                            Psc[bk0 + h2][:, 0:129], AF.Exp,
                            bias=nmx[cpa][:, h:h + 1],
                            accum_out=dnm[cpa][:, h:h + 1])))(),
                        deps=dep2, inc="s")

            def emit_T(j):
                for h2 in range(2):
                    h = 2 * j + h2
                    dep2 = [("s", s_exp[j])] + free["Pm"] if h2 == 0 else ()
                    p.step("tensor", (lambda h=h, h2=h2: (lambda t: t.transpose(
                        Pm[:, h2 * 128:(h2 + 1) * 128],
                        psb[:, h * 129 + 1:(h + 1) * 129], idn[:, :])))(),
                        deps=dep2)
                    t_pT[j] = p.step("tensor", (lambda h=h, h2=h2: (lambda t: t.transpose(
                        Pm[0:1, 256 + h2 * 128:256 + (h2 + 1) * 128],
                        psb[:, h * 129:h * 129 + 1], idn[:, :])))(),
                        inc="t" if h2 == 1 else None)

            def emit_E(j):
                dep2 = [("t", t_pT[j])]
                if j == 0:
                    dep2 += pTs_free
                p.step("vector", (lambda j=j: (lambda v: v.tensor_copy(
                    pTs[:, j * 256:(j + 1) * 256], Pm[:, 0:256])))(),
                    deps=dep2, inc="v")
                vp2_d[j] = p.step("vector", (lambda j=j: (lambda v: v.tensor_copy(
                    pcT[0:1, j * 256:(j + 1) * 256], Pm[0:1, 256:512])))(),
                    inc="v")
                free["Pm"] = [("v", vp2_d[j])]

            def emit_A(j):
                for h2 in range(2):
                    h = 2 * j + h2
                    col = (h % 8) * 64
                    dep2 = [("v", vp2_d[j])]
                    if j == 0 and h2 == 0:
                        dep2 += free["Pat"] + dG_vcar[ca]
                    if j == 4 and h2 == 0:
                        dep2 += [("s", sa_mid[0])]
                    p.step("tensor", (lambda h=h, col=col, gp=gp, cga=cga:
                        (lambda t: t.matmul(
                            Pat[:, col:col + 64],
                            pTs[:, h * 128:(h + 1) * 128],
                            vsb[gp][:, cga * 1024 + h * 64:cga * 1024 + (h + 1) * 64],
                            start=True, stop=False)))(), deps=dep2)
                    p.step("tensor", (lambda h=h, col=col, ca=ca:
                        (lambda t: t.matmul(
                            Pat[:, col:col + 64],
                            pcT[0:1, h * 128:(h + 1) * 128],
                            vcar[ca % 2][0:1, h * 64:(h + 1) * 64],
                            start=False, stop=True)))(),
                        inc="t" if h2 == 1 else None)

            def emit_half_evac(hh, t_mark=None):
                t_now = p.cnt["t"] if t_mark is None else t_mark
                vr = p.step("vector", (lambda cpa=cpa, hh=hh: (lambda v: v.reciprocal(
                    rcp[cpa][:, 8 * hh:8 * hh + 8],
                    dnm[cpa][:, 8 * hh:8 * hh + 8])))(),
                    deps=[("s", s_exp[4 * hh + 3])], inc="v")
                sa = None
                for i in range(8):
                    h = 8 * hh + i
                    dep = [("v", vr)]
                    if i == 0:
                        dep += [("t", t_now)]
                        if hh == 0:
                            dep += ato_free_box[0]
                    sa = p.step("scalar", (lambda h=h, i=i, cpa=cpa:
                        (lambda s_: s_.activation(
                            ato[:, h * 64:(h + 1) * 64],
                            Pat[:, i * 64:(i + 1) * 64], AF.Identity,
                            scale=rcp[cpa][:, h:h + 1])))(), deps=dep, inc="s")
                    sa_steps[(hh, i)] = sa
                sa_mid[hh] = sa

            return dict(S=emit_S, X=emit_X, T=emit_T, E=emit_E, A=emit_A,
                        H=emit_half_evac, s_exp=s_exp, sa_mid=sa_mid,
                        sa_steps=sa_steps)

        ctx_box = [None]
        late2 = []

        for g_ in range(NCH // 2):
            gp = g_ % 2
            nxt = []
            if 2 * g_ + 3 < NCH:
                for mb in range(KB):
                    nxt.append(qk_mm(g_ + 1, mb))
                    nxt.append(qk_ev(g_ + 1, mb))
                nxt.append(kcar_bundle(g_ + 1))

            def drip(n=1, late_ok=False):
                for _ in range(n):
                    if nxt:
                        nxt.pop(0)()

            for cga in range(2):
                ca = g_ * 2 + cga
                cpa = ca % 2
                if ctx_box[0] is not None:
                    C = ctx_box[0]
                else:
                    C = mk_chunk(ca)
                    C["S"](0)
                    C["X"](0)
                    C["S"](1)
                    C["X"](1)
                emit_T, emit_E, emit_S, emit_X, emit_A = (
                    C["T"], C["E"], C["S"], C["X"], C["A"])
                emit_half_evac = C["H"]
                s_exp = C["s_exp"]
                sa_mid = C["sa_mid"]

                for j in range(8):
                    if j != 0 or not C.get("TE0"):
                        emit_T(j)
                        emit_E(j)
                    if j + 2 < 8:
                        emit_S(j + 2)
                        emit_X(j + 2)
                    drip(1)
                    emit_A(j)
                    if j == 3:
                        emit_half_evac(0)
                    drip(1, late_ok=(j >= 3))
                    if j in (0, 1, 5) and late2:
                        late2.pop(0)()
                t_attn_done = p.cnt["t"]
                psb_free = [("t", t_attn_done)]
                pTs_free = [("t", t_attn_done)]
                vcar_free[ca % 2] = [("t", t_attn_done)]
                psc_free = [[("s", s_exp[6])], [("s", s_exp[7])]]
                if ca + 1 < NCH:
                    Cn = mk_chunk(ca + 1)
                    Cn["S"](0)
                    ctx_box[0] = Cn
                else:
                    ctx_box[0] = None
                t_aT0 = None
                for kb in range(4):
                    dep = [("s", sa_mid[0])] + free["Pm"] if kb == 0 else ()
                    t_aT0 = p.step("tensor", (lambda kb=kb: (lambda t: t.transpose(
                        Pm[:, 512 + kb * 128:512 + (kb + 1) * 128],
                        ato[:, kb * 128:(kb + 1) * 128], idn[:, :])))(),
                        deps=dep, inc="t" if kb == 3 else None)
                vaT0 = p.step("vector", lambda v: v.tensor_copy(
                    atoT[:, 0:512], Pm[:, 512:1024]),
                    deps=[("t", t_aT0)] + atoT_free, inc="v")
                free["Pm"] = free["Pm"] + [("v", vaT0)]
                if ctx_box[0] is not None:
                    ctx_box[0]["S"](1)
                wo_dep0 = [("v", vaT0)] + free["Pv"]
                if g_ == 0 and cga == 0:
                    wo_dep0 += [mk_wo]
                for kb in range(4):
                    p.step("tensor", (lambda kb=kb: (lambda t: t.matmul(
                        Pv[0][:, :], atoT[:, kb * 128:(kb + 1) * 128],
                        wo[:, kb * D:kb * D + 512],
                        start=(kb == 0), stop=False)))(),
                        deps=wo_dep0 if kb == 0 else ())
                emit_half_evac(1, t_mark=t_attn_done)
                sa = sa_mid[1]
                s_aev[ca] = sa
                free["Pat"] = [("s", sa)]
                ato_free_box[0] = []
                if ca + 1 < NCH:
                    lane = "dGv%d" % ((ca + 1) % 2)
                    dgv = p.step("gpsimd", (lambda gp=gp, cga=cga, ca=ca:
                        (lambda g: g.dma_start(
                            out=vcar[(ca + 1) % 2][0:1, :],
                            in_=vsb[gp][127:128, cga * 1024:(cga + 1) * 1024])))(),
                        deps=[("s", s_vev[ca])] + vcar_free[(ca + 1) % 2],
                        inc=lane, inc_by=16)
                    dG_vcar[ca + 1] = [(lane, dgv)]

                t_aT = None
                sa_steps = C["sa_steps"]
                for kb in range(4, KB):
                    dep = [("s", sa_steps[(1, 2 * (kb - 4) + 1)])]
                    if kb == 4:
                        dep += free["Pm"]
                    t_aT = p.step("tensor", (lambda kb=kb: (lambda t: t.transpose(
                        Pm[:, (kb - 4) * 128:(kb - 3) * 128],
                        ato[:, kb * 128:(kb + 1) * 128], idn[:, :])))(),
                        deps=dep, inc="t" if kb == KB - 1 else None)
                vaT = p.step("vector", lambda v: v.tensor_copy(
                    atoT[:, 512:1024], Pm[:, 0:512]),
                    deps=[("t", t_aT)], inc="v")
                free["Pm"] = [("v", vaT)]
                ato_free_box[0] = [("v", vaT0), ("v", vaT)]
                if ctx_box[0] is not None:
                    ctx_box[0]["X"](0)
                    ctx_box[0]["X"](1)

                vres = None
                for half in range(2):
                    Pb = Pv1 if half == 0 else Pat
                    kbs = range(4, KB) if half == 0 else range(KB)
                    for kb in kbs:
                        dep = []
                        if half == 0 and kb == 4:
                            dep = [("v", vaT)]
                        if half == 1 and kb == 0:
                            dep = [("s", sa)]
                        p.step("tensor", (lambda Pb=Pb, half=half, kb=kb: (lambda t: t.matmul(
                            Pb[:, 0:512], atoT[:, kb * 128:(kb + 1) * 128],
                            wo[:, kb * D + half * 512:kb * D + (half + 1) * 512],
                            start=(half == 1 and kb == 0), stop=False)))(), deps=dep)
                    tw = p.step("tensor", (lambda Pb=Pb, half=half: (lambda t: t.matmul(
                        Pb[:, 0:512], onesb[0:1, 0:128],
                        bob[0:1, half * 512:(half + 1) * 512],
                        start=False, stop=True)))(), inc="t")
                    dep = [("t", tw)]
                    if half == 0:
                        dep += x2st_free[ca % 3]
                    vres = p.step("vector", (lambda Pb=Pb, half=half, ca=ca:
                        (lambda v: v.tensor_tensor(
                            x2st[ca % 3][:, half * 512:(half + 1) * 512],
                            Pb[:, 0:512],
                            xin[ca % 5][:, half * 512:(half + 1) * 512],
                            op=ALU.add)))(), deps=dep, inc="v")
                    if half == 0:
                        free["Pv"] = [("v", vres)]
                        if ctx_box[0] is not None:
                            ctx_box[0]["T"](0)
                            ctx_box[0]["E"](0)
                            ctx_box[0]["TE0"] = True
                    else:
                        free["Pat"] = [("v", vres)]
                v_resid[ca] = vres
                xin_free[ca % 5] = [("v", vres)]
                atoT_free = [("t", p.cnt["t"])]
                if ca + 5 < NCH:
                    cc2 = ca + 5
                    lane = "dL%d" % (cc2 % 5)
                    dL_x[cc2] = (lane, p.step("sync", (lambda cc2=cc2: (lambda s_: s_.dma_start(
                        out=xin[cc2 % 5][:, :],
                        in_=xs_h[cc2 * 128:(cc2 + 1) * 128, :])))(),
                        deps=[("v", vres)], inc=lane, inc_by=16))

                dS_x2[ca] = ("dS%d" % (ca % 3), p.step("scalar", (lambda ca=ca:
                    (lambda s_: s_.dma_start(out=x2_h[ca * 128:(ca + 1) * 128, :],
                                 in_=x2st[ca % 3][:, :])))(),
                    deps=[("v", vres)], inc="dS%d" % (ca % 3), inc_by=16))

                if ca + 1 < NCH:
                    late2.extend(ln2_bundle(ca))
                else:
                    for _f in ln2_bundle(ca):
                        _f()

                if cga == 0 and 2 * g_ + 3 < NCH:
                    for cv in (2 * g_ + 2, 2 * g_ + 3):
                        for half in (0, 1):
                            nxt.append(v_bundle(cv, half))
                    for c4 in (2 * g_ + 4, 2 * g_ + 5):
                        if c4 < NCH:
                            nxt.append((lambda c4=c4: ln1_stats(c4)))
                    for c4 in (2 * g_ + 4, 2 * g_ + 5):
                        if c4 < NCH:
                            nxt.append((lambda c4=c4: ln1_trans(c4)))

            while nxt:
                nxt.pop(0)()
            ln1T_free[gp] = [("t", p.cnt["t"])]
            qT_free[gp] = [("t", p.cnt["t"])]
            kT_free[gp] = [("t", p.cnt["t"])]
            vsb_free[gp] = [("t", p.cnt["t"])]

        while late2:
            late2.pop(0)()
        tA_end = p.cnt["t"]

        if upto == "A":
            return _emit(p, block, sems, nc)
        v_last = g_xn[NCH - 1]
        t_am = p.step("tensor", lambda t: t.transpose(
            Pq[0:1, 0:128], amr[:, 0:1], idnf[:, :]),
            deps=[("v", v_last)] + free["Pq"] + psc_free[0] + psc_free[1], inc="t")
        v_am = p.step("vector", lambda v: v.reduce_max(
            sca[0:1, 0:1], Pq[0:1, 0:128], axis=AX), deps=[("t", t_am)], inc="v")
        free["Pq"] = [("v", v_am)]
        dg1 = p.step("gpsimd", lambda g: g.dma_start(out=cc_in[:], in_=sca[0:1, 0:1]),
                     deps=[("v", v_am)], inc="dGm", inc_by=16)
        c1 = p.step("gpsimd", lambda g: g.collective_compute(
            "AllGather", ALU.bypass, replica_groups=[list(range(8))],
            ins=[cc_in[:]], outs=[cc_out[:]]), deps=[("dGm", dg1)], inc="c")
        dg2 = p.step("gpsimd", lambda g: g.dma_start(out=gmx[0:1, :], in_=cc_out[:]),
                     deps=[("c", c1)], inc="dGm", inc_by=16)

        wfs2 = wo[:, 6144:8192].bitcast(F32)
        wfsb = [wfs, wfs2]
        absb2 = mskb
        WL = ("dW0", "dW1")
        dgw = {}
        for ob in range(2):
            dgw[ob] = p.step("sync", (lambda ob=ob: (lambda s_: s_.dma_start(
                out=wfsb[ob][:, :], in_=wf_h[ob * 128:(ob + 1) * 128, :])))(),
                deps=[("t", tA_end)], inc=WL[ob % 2], inc_by=16)
        vcs = None
        sab_d = {}
        vcs_d = {}
        for ob in range(KB):
            ab = absb if ob % 2 == 0 else absb2
            dep = [(WL[ob % 2], dgw[ob]), ("t", tA_end)]
            if ob >= 2:
                dep += [vcs_d[ob - 2]]
            sab = p.step("scalar", (lambda ob=ob, ab=ab: (lambda s_: s_.activation(
                ab[:, :], wfsb[ob % 2][:, :], AF.Abs)))(), deps=dep, inc="s")
            sab_d[ob] = ("s", sab)
            if ob + 2 < KB:
                dgw[ob + 2] = p.step("sync", (lambda ob=ob: (lambda s_: s_.dma_start(
                    out=wfsb[ob % 2][:, :],
                    in_=wf_h[(ob + 2) * 128:(ob + 3) * 128, :])))(),
                    deps=[("s", sab)], inc=WL[ob % 2], inc_by=16)
            vcs = p.step("vector", (lambda ob=ob, ab=ab: (lambda v: v.reduce_sum(
                csab[:, ob:ob + 1], ab[:, :], axis=AX)))(), deps=[("s", sab)], inc="v")
            vcs_d[ob] = ("v", vcs)
        v_cs = p.step("vector", lambda v: v.reduce_sum(amt[:, 0:1], csab[:, :], axis=AX),
                      deps=[("v", vcs)], inc="v")
        t_tot = p.step("tensor", lambda t: t.matmul(
            Pq[0:1, 0:1], amt[:, 0:1], onecf[:, 0:1], start=True, stop=True),
            deps=[("v", v_cs)] + free["Pq"], inc="t")
        v_dl = p.step("vector", lambda v: v.tensor_scalar(
            tots[0:1, 0:1], Pq[0:1, 0:1], 0.7 / (D * D), None, op0=ALU.mult),
            deps=[("t", t_tot)], inc="v")
        free["Pq"] = [("v", v_dl)]
        t_dl = p.step("tensor", lambda t: t.matmul(
            Pk[:, 0:1], onesf[0:1, :], tots[0:1, 0:1], start=True, stop=True),
            deps=[("v", v_dl)] + free["Pq"], inc="t")
        v_dl2 = p.step("vector", lambda v: v.tensor_copy(dlpp[:, 0:1], Pk[:, 0:1]),
                       deps=[("t", t_dl)], inc="v")
        free["Pq"] = [("v", v_dl2)]
        vt = None
        vs2 = None
        ssg_d = {}
        vt_d = {}
        for ob in range(KB):
            dg = p.step("sync", (lambda ob=ob: (lambda s_: s_.dma_start(
                out=wfsb[ob % 2][:, :], in_=wf_h[ob * 128:(ob + 1) * 128, :])))(),
                deps=[ssg_d[ob - 2]] if ob >= 2 else [sab_d[KB - 2 + ob]],
                inc=WL[ob % 2], inc_by=16)
            dep = [(WL[ob % 2], dg), ("v", v_dl2)]
            if ob >= 1:
                dep += [vt_d[ob - 1]]
            if ob == 0:
                dep += [vcs_d[KB - 1]]
            sab = p.step("scalar", (lambda ob=ob: (lambda s_: s_.activation(
                absb[:, :], wfsb[ob % 2][:, :], AF.Abs)))(), deps=dep, inc="s")
            ssg = p.step("scalar", (lambda ob=ob: (lambda s_: s_.activation(
                sgnb, wfsb[ob % 2][:, :], AF.Sign)))(), inc="s")
            ssg_d[ob] = ("s", ssg)
            vm = p.step("vector", (lambda ob=ob: (lambda v: v.tensor_scalar(
                mskb, absb[:, :], dlpp[:, 0:1], None, op0=ALU.is_gt, op1=ALU.add,
                accum_out=cnta[:, ob:ob + 1])))(),
                deps=[("s", sab)] + ([vt_d[ob - 1]] if ob >= 1 else []), inc="v")
            vs2 = p.step("vector", (lambda ob=ob: (lambda v: v.scalar_tensor_tensor(
                absb[:, :], absb[:, :], 1.0, mskb, op0=ALU.mult, op1=ALU.mult,
                accum_out=suma[:, ob:ob + 1])))(), deps=[("v", vm)], inc="v")
            vt = p.step("vector", (lambda ob=ob: (lambda v: v.tensor_tensor(
                tbf[:, ob * D:(ob + 1) * D], sgnb, mskb, op=ALU.mult)))(),
                deps=[("s", ssg)] + ([("t", tA_end)] if ob == 0 else []), inc="v")
            vt_d[ob] = ("v", vt)
        v_c1 = p.step("vector", lambda v: v.reduce_sum(amt[:, 0:1], cnta[:, :], axis=AX),
                      deps=[("v", vs2)], inc="v")
        t_c1 = p.step("tensor", lambda t: t.matmul(
            Pq[0:1, 0:1], amt[:, 0:1], onecf[:, 0:1], start=True, stop=True),
            deps=[("v", v_c1)] + free["Pq"], inc="t")
        v_c2 = p.step("vector", lambda v: v.tensor_scalar(
            tots[0:1, 1:2], Pq[0:1, 0:1], 1.0, None, op0=ALU.max),
            deps=[("t", t_c1)], inc="v")
        v_s1 = p.step("vector", lambda v: v.reduce_sum(amt[:, 0:1], suma[:, :], axis=AX),
                      inc="v")
        t_s1 = p.step("tensor", lambda t: t.matmul(
            Pq[0:1, 1:2], amt[:, 0:1], onecf[:, 0:1], start=True, stop=True),
            deps=[("v", v_s1)], inc="t")
        v_al = p.step("vector", lambda v: v.reciprocal(tots[0:1, 2:3], tots[0:1, 1:2]),
                      deps=[("v", v_c2)], inc="v")
        p.step("vector", lambda v: v.tensor_tensor(
            tots[0:1, 3:4], Pq[0:1, 1:2], tots[0:1, 2:3], op=ALU.mult),
            deps=[("t", t_s1), ("v", v_al)], inc="v")

        v_wT = None
        for ib in range(KB):
            t_w = None
            for ob in range(KB):
                dep = []
                if ob == 0:
                    dep = list(free["Pm"])
                    if ib == 0:
                        dep += [("v", vt)]
                t_w = p.step("tensor", (lambda ib=ib, ob=ob: (lambda t: t.transpose(
                    Pm[:, ob * 128:(ob + 1) * 128],
                    tbf[:, ob * D + ib * 128:ob * D + (ib + 1) * 128], idn[:, :])))(),
                    deps=dep, inc="t" if ob == KB - 1 else None)
            v_wT = p.step("vector", (lambda ib=ib: (lambda v: v.tensor_copy(
                wT[:, ib * D:(ib + 1) * D], Pm[:, 0:1024])))(), deps=[("t", t_w)], inc="v")
            free["Pm"] = [("v", v_wT)]

        v_gm = p.step("vector", lambda v: v.reduce_max(sca[0:1, 0:1], gmx[0:1, :], axis=AX),
                      deps=[("dGm", dg2)], inc="v")
        p.step("vector", lambda v: v.tensor_scalar(
            sca[0:1, 1:2], sca[0:1, 0:1], 1e-8, 1.0 / QMAX, op0=ALU.max, op1=ALU.mult))
        v_iv = p.step("vector", lambda v: v.reciprocal(sca[0:1, 2:3], sca[0:1, 1:2]),
                      inc="v")
        t_iv = p.step("tensor", lambda t: t.matmul(
            Pq[:, 0:1], onesf[0:1, :], sca[0:1, 2:3], start=True, stop=True),
            deps=[("v", v_iv)] + free["Pq"], inc="t")
        v_iv2 = p.step("vector", lambda v: v.tensor_copy(ivpp[:, 0:1], Pq[:, 0:1]),
                       deps=[("t", t_iv)], inc="v")
        free["Pq"] = [("v", v_iv2)]
        v_G = p.step("vector", lambda v: v.tensor_scalar(
            gamb[:, :], gamb[:, :], ivpp[:, 0:1], None, op0=ALU.mult),
            deps=[("v", v_iv2)], inc="v")
        v_B = p.step("vector", lambda v: v.tensor_scalar(
            betb[:, :], betb[:, :], ivpp[:, 0:1], None, op0=ALU.mult), inc="v")
        v_sa = p.step("vector", lambda v: v.tensor_tensor(
            tots[0:1, 4:5], tots[0:1, 3:4], sca[0:1, 1:2], op=ALU.mult), inc="v")
        t_sa = p.step("tensor", lambda t: t.matmul(
            Pk[:, 0:1], onesf[0:1, :], tots[0:1, 4:5], start=True, stop=True),
            deps=[("v", v_sa)] + free["Pq"], inc="t")
        v_sa2 = p.step("vector", lambda v: v.tensor_copy(sapp[:, 0:1], Pk[:, 0:1]),
                       deps=[("t", t_sa)], inc="v")

        if upto == "bar":
            return _emit(p, block, sems, nc)
        x2done = [("dS0", p.cnt["dS0"]), ("dS1", p.cnt["dS1"]), ("dS2", p.cnt["dS2"])]
        x2in_free = dict(xin_free)
        dL_b = {}
        for c in range(3):
            lane = "dL%d" % (c % 5)
            dL_b[c] = (lane, p.step("sync", (lambda c=c: (lambda s_: s_.dma_start(
                out=xin[c % 5][:, :], in_=x2_h[c * 128:(c + 1) * 128, :])))(),
                deps=x2in_free[c % 5] + x2done + [("v", v_resid[NCH - 1])],
                inc=lane, inc_by=16))
        rpre_free = [[("t", tA_end)], [("t", tA_end)]]
        rbf_free = [[], []]
        rT_free = [[], []]
        yt_free = [[], []]
        ysb_free = [[], []]
        dS_y = {}
        vy = None
        def b_stage1(c):
            cp = c % 2
            slot = c % 5
            sx = p.step("scalar", (lambda c=c, slot=slot: (lambda s_: s_.activation(
                xh2[0][:, :], xin[slot][:, :], AF.Identity,
                bias=b2a[:, c:c + 1], scale=rs2a[:, c:c + 1])))(),
                deps=[dL_b[c], v_b2[c]] + xh2_free[0], inc="s")
            v1 = p.step("vector", (lambda cp=cp: (lambda v: v.tensor_tensor(
                rpre[cp], xh2[0][:, :], gamb[:, :], op=ALU.mult)))(),
                deps=[("s", sx), ("v", v_G)] + rpre_free[cp], inc="v")
            xh2_free[0] = [("v", v1)]
            p.step("vector", (lambda cp=cp: (lambda v: v.tensor_tensor(
                rpre[cp], rpre[cp], betb[:, :], op=ALU.add)))(),
                deps=[("v", v_B)], inc="v")
            v3 = p.step("vector", (lambda cp=cp: (lambda v: v.tensor_scalar(
                rbf[cp], rpre[cp], MAGIC, MAGIC,
                op0=ALU.add, op1=ALU.subtract)))(), deps=rbf_free[cp], inc="v")
            rpre_free[cp] = [("v", v3)]
            return v3

        PmB = Psc[0][:, :].bitcast(BF16)
        free["PmB"] = list(psc_free[0]) + list(psc_free[1])

        def b_trans(c, v3):
            cp = c % 2
            dst = Pm if cp == 0 else PmB
            key = "Pm" if cp == 0 else "PmB"
            t_rT = None
            for kb in range(KB):
                dep = []
                if kb == 0:
                    dep = [("v", v3), ("v", v_wT)] + free[key]
                t_rT = p.step("tensor", (lambda kb=kb, cp=cp, dst=dst: (lambda t: t.transpose(
                    dst[:, kb * 128:(kb + 1) * 128],
                    rbf[cp][:, kb * 128:(kb + 1) * 128], idn[:, :])))(),
                    deps=dep, inc="t" if kb == KB - 1 else None)
            vrT = p.step("vector", (lambda cp=cp, dst=dst: (lambda v: v.tensor_copy(
                rT[cp], dst[:, 0:1024])))(), deps=[("t", t_rT)] + rT_free[cp], inc="v")
            free[key] = [("v", vrT)]
            rbf_free[cp] = [("v", vrT)]
            return vrT

        free["Pat"] = list(free["Pat"])
        v3n = b_stage1(0)
        vrTn = b_trans(0, v3n)
        for c in range(NCH):
            cp = c % 2
            slot = c % 5
            vrT = vrTn
            if c + 3 < NCH:
                c2 = c + 3
                lane = "dL%d" % (c2 % 5)
                dL_b[c2] = (lane, p.step("sync", (lambda c2=c2: (lambda s_: s_.dma_start(
                    out=xin[c2 % 5][:, :], in_=x2_h[c2 * 128:(c2 + 1) * 128, :])))(),
                    deps=x2in_free[c2 % 5], inc=lane, inc_by=16))
            v3n = b_stage1(c + 1) if c + 1 < NCH else None
            tf = {}
            for half in range(2):
                Pb = Pv1 if half == 0 else Pat
                bank = "Pv" if half == 0 else "Pat"
                t_f = None
                for kb in range(KB):
                    dep = []
                    if kb == 0:
                        dep = [("v", vrT)] + free[bank]
                    t_f = p.step("tensor", (lambda Pb=Pb, half=half, kb=kb, cp=cp:
                        (lambda t: t.matmul(
                            Pb[:, 0:512], rT[cp][:, kb * 128:(kb + 1) * 128],
                            wT[:, kb * D + half * 512:kb * D + (half + 1) * 512],
                            start=(kb == 0), stop=(kb == KB - 1))))(),
                        deps=dep, inc="t" if kb == KB - 1 else None)
                tf[half] = t_f
            vy = None
            for half in range(2):
                Pb = Pv1 if half == 0 else Pat
                bank = "Pv" if half == 0 else "Pat"
                dep = [("t", tf[half]), ("v", v_sa2)]
                if half == 0:
                    dep += yt_free[cp]
                sy = p.step("scalar", (lambda Pb=Pb, half=half, cp=cp:
                    (lambda s_: s_.activation(
                        yt[cp][:, half * 512:(half + 1) * 512], Pb[:, 0:512],
                        AF.Identity, scale=sapp[:, 0:1])))(), deps=dep, inc="s")
                free[bank] = [("s", sy)]
                dep = [("s", sy)]
                if half == 0:
                    dep += ysb_free[cp]
                vy = p.step("vector", (lambda half=half, cp=cp, slot=slot:
                    (lambda v: v.tensor_tensor(
                        ysb[cp][:, half * 512:(half + 1) * 512],
                        yt[cp][:, half * 512:(half + 1) * 512],
                        xin[slot][:, half * 512:(half + 1) * 512], op=ALU.add)))(),
                    deps=dep, inc="v")
            yt_free[cp] = [("v", vy)]
            x2in_free[slot] = [("v", vy)]
            rT_free[cp] = [("t", p.cnt["t"])]
            if v3n is not None:
                vrTn = b_trans(c + 1, v3n)
            dS_y[c] = ("dS%d" % cp, p.step("scalar", (lambda c=c, cp=cp: (lambda s_:
                s_.dma_start(out=y_h[c * 128:(c + 1) * 128, :],
                             in_=ysb[cp][:, :])))(),
                deps=[("v", vy)], inc="dS%d" % cp, inc_by=16))
            ysb_free[cp] = [dS_y[c]]

        fin_deps = [("dS0", p.cnt["dS0"]), ("dS1", p.cnt["dS1"])]
        if p.cnt["dbg"]:
            fin_deps.append(("dbg", p.cnt["dbg"]))
        p.step("sync", lambda s_: s_.nop(), deps=fin_deps)
        return _emit(p, block, sems, nc)


def _emit(p, block, sems, nc):
    @block.sync
    def _(eng):
        p.run("sync", eng, sems)

    @block.scalar
    def _(eng):
        p.run("scalar", eng, sems)

    @block.vector
    def _(eng):
        p.run("vector", eng, sems)

    @block.tensor
    def _(eng):
        p.run("tensor", eng, sems)

    @block.gpsimd
    def _(eng):
        p.run("gpsimd", eng, sems)
    return nc


_NC_CACHE = None


def kernel(L0_attn_x, L0_attn_Wq, L0_attn_bq, L0_attn_Wk, L0_attn_bk,
           L0_attn_Wv, L0_attn_bv, L0_attn_Wo, L0_attn_bo, L0_attn_decay,
           L0_ffn_weight, L0_ffn_gamma, L0_ffn_beta):
    global _NC_CACHE
    x = np.asarray(L0_attn_x)
    B, L, _ = x.shape
    assert (B, L) == (2, 16384)
    if _NC_CACHE is None:
        _NC_CACHE = build_nc()
    nc = _NC_CACHE

    def row(a):
        return np.ascontiguousarray(np.asarray(a, dtype=np.float32).reshape(1, -1))

    shared = {
        "wq": np.ascontiguousarray(L0_attn_Wq, dtype=np.float32),
        "wk": np.ascontiguousarray(L0_attn_Wk, dtype=np.float32),
        "wv": np.ascontiguousarray(L0_attn_Wv, dtype=np.float32),
        "wo": np.ascontiguousarray(L0_attn_Wo, dtype=np.float32),
        "bq": row(L0_attn_bq), "bk": row(L0_attn_bk),
        "bv": row(L0_attn_bv), "bo": row(L0_attn_bo),
        "dec": row(L0_attn_decay),
        "wf": np.ascontiguousarray(L0_ffn_weight, dtype=np.float32),
        "gam": row(L0_ffn_gamma), "bet": row(L0_ffn_beta),
    }
    in_maps = []
    for i in range(8):
        b = i // 4
        t0 = (i % 4) * NT
        m = dict(shared)
        m["xs"] = np.ascontiguousarray(x[b, t0:t0 + NT, :], dtype=np.float32)
        if i % 4 == 0:
            m["xh"] = np.ascontiguousarray(x[b, 0:1, :], dtype=np.float32)
            m["cm"] = np.zeros((1, 1), dtype=np.float32)
        else:
            m["xh"] = np.ascontiguousarray(x[b, t0 - 1:t0, :], dtype=np.float32)
            m["cm"] = np.ones((1, 1), dtype=np.float32)
        in_maps.append(m)

    res = run_bass_kernel_spmd(nc, in_maps, core_ids=list(range(8)))
    out = np.empty((2, 16384, D), dtype=np.float32)
    for i in range(8):
        b = i // 4
        t0 = (i % 4) * NT
        out[b, t0:t0 + NT, :] = np.asarray(res.results[i]["y"], dtype=np.float32)
    return out


# version 45
# speedup vs baseline: 1.0029x; 1.0029x over previous
import numpy as np
from contextlib import ExitStack

import concourse.bass as bass
import concourse.mybir as mybir
from concourse.bass_utils import run_bass_kernel_spmd

F32 = mybir.dt.float32
BF16 = mybir.dt.bfloat16
AX = mybir.AxisListType.X
AF = mybir.ActivationFunctionType
ALU = mybir.AluOpType

MAGIC = 12582912.0
D = 1024
KB = 8
H = 16
C = 128
NT = 4096
NCH = 32
EPS = 1e-5
QMAX = 127.0


class Plan:

    ENGINES = ("sync", "scalar", "vector", "tensor", "gpsimd")
    OWN = {"vector": "v", "scalar": "s", "gpsimd": "g"}

    def __init__(self):
        self.steps = {e: [] for e in self.ENGINES}
        self.cnt = {k: 0 for k in ("dL0", "dL1", "dL2", "dL3", "dL4", "dS0", "dS1", "dS2", "dGwq", "dGwk", "dGwv", "dGwo", "dGr", "dR2", "dW0", "dW1", "dGv0", "dGv1", "dGm", "v", "s", "t", "g", "c", "dbg")}

    def step(self, eng, fn, deps=(), inc=None, inc_by=1):
        chain = None
        if eng in self.OWN and inc is None:
            inc = self.OWN[eng]
        if eng in self.OWN and inc == self.OWN[eng]:
            chain = (self.OWN[eng], self.cnt[self.OWN[eng]])
        val = None
        if inc is not None:
            self.cnt[inc] += inc_by
            val = self.cnt[inc]
        self.steps[eng].append((tuple(deps), fn, inc, inc_by, chain))
        return val

    def run(self, eng_name, eng, sems):
        watermark = {}
        for deps, fn, inc, inc_by, chain in self.steps[eng_name]:
            alldeps = list(deps)
            if chain is not None and chain[1] > 0:
                alldeps.append(chain)
            for s, v in alldeps:
                if v and watermark.get(s, -1) < v:
                    eng.wait_ge(sems[s], v)
                    watermark[s] = v
            inst = fn(eng)
            if inc is not None:
                inst.then_inc(sems[inc], inc_by)


def build_nc(upto='full', alvl=10, blvl=9, dbg_dump=False):
    nc = bass.Bass("TRN2", target_bir_lowering=False)

    xs_h = nc.dram_tensor("xs", [NT, D], F32, kind="ExternalInput").ap()
    xh_h = nc.dram_tensor("xh", [1, D], F32, kind="ExternalInput").ap()
    cm_h = nc.dram_tensor("cm", [1, 1], F32, kind="ExternalInput").ap()
    wq_h = nc.dram_tensor("wq", [D, D], F32, kind="ExternalInput").ap()
    wk_h = nc.dram_tensor("wk", [D, D], F32, kind="ExternalInput").ap()
    wv_h = nc.dram_tensor("wv", [D, D], F32, kind="ExternalInput").ap()
    wo_h = nc.dram_tensor("wo", [D, D], F32, kind="ExternalInput").ap()
    bq_h = nc.dram_tensor("bq", [1, D], F32, kind="ExternalInput").ap()
    bk_h = nc.dram_tensor("bk", [1, D], F32, kind="ExternalInput").ap()
    bv_h = nc.dram_tensor("bv", [1, D], F32, kind="ExternalInput").ap()
    bo_h = nc.dram_tensor("bo", [1, D], F32, kind="ExternalInput").ap()
    dec_h = nc.dram_tensor("dec", [1, H], F32, kind="ExternalInput").ap()
    wf_h = nc.dram_tensor("wf", [D, D], F32, kind="ExternalInput").ap()
    gam_h = nc.dram_tensor("gam", [1, D], F32, kind="ExternalInput").ap()
    bet_h = nc.dram_tensor("bet", [1, D], F32, kind="ExternalInput").ap()
    y_h = nc.dram_tensor("y", [NT, D], F32, kind="ExternalOutput").ap()
    x2_h = nc.dram_tensor("x2sp", [NT, D], F32).ap()
    cc_in = nc.dram_tensor("cc_in", [1], F32).ap()
    cc_out = nc.dram_tensor("cc_out", [8], F32).ap()
    dmp = {}
    if dbg_dump:
        for nmm, shp, dt in [("d_ln1bf", [128, 1024], BF16), ("d_lnT", [128, 2048], BF16),
                             ("d_qT", [128, 2048], BF16), ("d_kT", [128, 2064], BF16),
                             ("d_vsb", [128, 2048], BF16), ("d_vc", [1, 1024], BF16),
                             ("d_psb", [128, 2064], BF16), ("d_pT", [128, 2048], BF16),
                             ("d_pcT", [1, 2048], BF16), ("d_nm", [128, 16], F32),
                             ("d_dn", [128, 16], F32), ("d_rc", [128, 16], F32),
                             ("d_ato", [128, 1024], BF16), ("d_atoT", [128, 1024], BF16),
                             ("d_xin0", [128, 1024], F32), ("d_st1", [128, 9], F32)]:
            dmp[nmm] = nc.dram_tensor(nmm, shp, dt, kind="ExternalOutput").ap()

    p = Plan()

    with ExitStack() as st:
        e = st.enter_context

        wq = e(nc.sbuf_tensor("wq_sb", [128, KB * D], BF16))
        wk = e(nc.sbuf_tensor("wk_sb", [128, KB * D], BF16))
        wv = e(nc.sbuf_tensor("wv_sb", [128, KB * D], BF16))
        wo = e(nc.sbuf_tensor("wo_sb", [128, KB * D], BF16))
        bqb = e(nc.sbuf_tensor("bqb", [1, D], BF16))
        bkb = e(nc.sbuf_tensor("bkb", [1, D], BF16))
        bvb = e(nc.sbuf_tensor("bvb", [1, D], BF16))
        bob = e(nc.sbuf_tensor("bob", [1, D], BF16))
        gam_r = e(nc.sbuf_tensor("gam_r", [1, D], F32))
        bet_r = e(nc.sbuf_tensor("bet_r", [1, D], F32))
        dec_r = e(nc.sbuf_tensor("dec_r", [1, H], F32))
        cm_sb = e(nc.sbuf_tensor("cm_sb", [1, 1], F32))
        xh_sb = e(nc.sbuf_tensor("xh_sb", [1, D], F32))
        gamb = e(nc.sbuf_tensor("gamb", [128, D], F32))
        betb = e(nc.sbuf_tensor("betb", [128, D], F32))
        dect = e(nc.sbuf_tensor("dect", [128, 2048], F32))
        d8rep = e(nc.sbuf_tensor("d8rep", [128, 128], F32))
        dpp = e(nc.sbuf_tensor("dpp", [128, 16], F32))
        idn = e(nc.sbuf_tensor("idn", [128, 128], BF16))
        idnf = e(nc.sbuf_tensor("idnf", [128, 128], F32))
        onesb = e(nc.sbuf_tensor("onesb", [1, 256], BF16))
        onesf = e(nc.sbuf_tensor("onesf", [1, 128], F32))
        onecf = e(nc.sbuf_tensor("onecf", [128, 1], F32))
        cmpp = e(nc.sbuf_tensor("cmpp", [128, 1], F32))
        lnhT = e(nc.sbuf_tensor("lnhT", [128, KB], BF16))
        bqT = e(nc.sbuf_tensor("bqT", [128, KB], F32))
        bkT = e(nc.sbuf_tensor("bkT", [128, KB], F32))
        lnh_bf = e(nc.sbuf_tensor("lnh_bf", [1, D], BF16))
        xin = [e(nc.sbuf_tensor(f"xin{i}", [128, D], F32)) for i in range(5)]
        ln1bf = [e(nc.sbuf_tensor(f"ln1bf{i}", [128, D], BF16)) for i in range(2)]
        ln1T = [e(nc.sbuf_tensor(f"ln1T{i}", [128, 2048], BF16)) for i in range(2)]
        qT = [e(nc.sbuf_tensor(f"qT{i}", [128, 2048], BF16)) for i in range(2)]
        kT = [e(nc.sbuf_tensor(f"kT{i}", [128, 2064], BF16)) for i in range(2)]
        vsb = [e(nc.sbuf_tensor(f"vsb{i}", [128, 2048], BF16)) for i in range(2)]
        vcar = [e(nc.sbuf_tensor(f"vcar{i}", [1, D], BF16)) for i in range(2)]
        psb = e(nc.sbuf_tensor("psb", [128, H * 129], BF16))
        pTs = e(nc.sbuf_tensor("pTs", [128, H * 128], BF16))
        pcT = e(nc.sbuf_tensor("pcT", [1, H * 128], BF16))
        dnm = [e(nc.sbuf_tensor(f"dnm{i}", [128, H], F32)) for i in range(2)]
        nmx = [e(nc.sbuf_tensor(f"nmx{i}", [128, H], F32)) for i in range(2)]
        rcp = [e(nc.sbuf_tensor(f"rcp{i}", [128, H], F32)) for i in range(2)]
        ato = e(nc.sbuf_tensor("ato", [128, D], BF16))
        atoT = e(nc.sbuf_tensor("atoT", [128, D], BF16))
        x2st = [e(nc.sbuf_tensor(f"x2st{i}", [128, D], F32)) for i in range(3)]
        sqs = e(nc.sbuf_tensor("sqs", [128, D], F32))
        xh2 = [e(nc.sbuf_tensor("xh2", [128, D], F32))]
        xng = e(nc.sbuf_tensor("xng", [128, D], F32))
        st1 = e(nc.sbuf_tensor("st1", [128, 18], F32))
        epsc = e(nc.sbuf_tensor("epsc", [128, 1], F32))
        st2 = e(nc.sbuf_tensor("st2", [128, 8], F32))
        nm2a = e(nc.sbuf_tensor("nm2a", [128, NCH], F32))
        rs2a = e(nc.sbuf_tensor("rs2a", [128, NCH], F32))
        b2a = e(nc.sbuf_tensor("b2a", [128, NCH], F32))
        amr = e(nc.sbuf_tensor("amr", [128, 1], F32))
        amt = e(nc.sbuf_tensor("amt", [128, 1], F32))
        gmx = e(nc.sbuf_tensor("gmx", [1, 8], F32))
        sca = e(nc.sbuf_tensor("sca", [1, 4], F32))
        ivpp = e(nc.sbuf_tensor("ivpp", [128, 1], F32))
        sapp = e(nc.sbuf_tensor("sapp", [128, 1], F32))
        kcs = e(nc.sbuf_tensor("kcs", [128, 8], BF16))
        hsc = e(nc.sbuf_tensor("hsc", [1, 8], F32))
        csab = e(nc.sbuf_tensor("csab", [128, 8], F32))
        cnta = e(nc.sbuf_tensor("cnta", [128, 8], F32))
        suma = e(nc.sbuf_tensor("suma", [128, 8], F32))
        tots = e(nc.sbuf_tensor("tots", [1, 8], F32))
        dlpp = e(nc.sbuf_tensor("dlpp", [128, 1], F32))

        wT = wq
        tbf = wk
        rT = [wv[:, 0:1024], wv[:, 1024:2048]]
        rbf = [wv[:, 2048:3072], wv[:, 3072:4096]]
        sgnb = wv[:, 4096:5120]
        wfs = wo[:, 0:2048].bitcast(F32)
        absb = wo[:, 2048:4096].bitcast(F32)
        mskb = wo[:, 4096:6144].bitcast(F32)
        rpre = [qT[0][:, :].bitcast(F32), qT[1][:, :].bitcast(F32)]
        yt = [ln1T[0][:, :].bitcast(F32), ln1T[1][:, :].bitcast(F32)]
        ysb = [vsb[0][:, :].bitcast(F32), vsb[1][:, :].bitcast(F32)]

        Pm = e(nc.psum_tensor("Pm", [128, 1024], BF16))
        Pq = e(nc.psum_tensor("Pq", [128, 512], F32))
        Pk = Pq
        Pv1 = e(nc.psum_tensor("Pv1", [128, 512], F32))
        Pv = [Pv1, Pv1]
        Pat = e(nc.psum_tensor("Pat", [128, 512], F32))
        Psc = [e(nc.psum_tensor(f"Psc{i}", [128, 512], F32)) for i in range(4)]

        sems = {}
        for sname in ("dL0", "dL1", "dL2", "dL3", "dL4", "dS0", "dS1", "dS2", "dGwq", "dGwk", "dGwv", "dGwo", "dGr", "dR2", "dW0", "dW1", "dGv0", "dGv1", "dGm", "v", "s", "t", "g", "c", "dbg"):
            sems[sname] = e(nc.semaphore(sname))
        block = e(nc.Block())

        g1 = p.step("gpsimd", lambda g: g.memset(idn[:, :], 0.0), inc="g")
        p.step("gpsimd", lambda g: g.affine_select(
            out=idn[:, :], in_=idn[:, :], compare_op=ALU.not_equal, fill=1.0,
            base=0, pattern=[[-1, 128]], channel_multiplier=1),
            deps=[("g", g1)], inc="g")
        g2 = p.step("gpsimd", lambda g: g.memset(idnf[:, :], 0.0), inc="g")
        p.step("gpsimd", lambda g: g.affine_select(
            out=idnf[:, :], in_=idnf[:, :], compare_op=ALU.not_equal, fill=1.0,
            base=0, pattern=[[-1, 128]], channel_multiplier=1),
            deps=[("g", g2)], inc="g")
        p.step("gpsimd", lambda g: g.memset(onesb[:, :], 1.0), inc="g")
        p.step("gpsimd", lambda g: g.memset(onesf[:, :], 1.0), inc="g")
        g_const = p.step("gpsimd", lambda g: g.memset(onecf[:, :], 1.0), inc="g")
        g_eps = p.step("gpsimd", lambda g: g.memset(epsc[:, :], EPS), inc="g")

        for dst, src in [(gam_r, gam_h), (bet_r, bet_h), (dec_r, dec_h),
                         (cm_sb, cm_h), (xh_sb, xh_h)]:
            p.step("sync", (lambda d=dst, s_=src: (lambda s2:
                s2.dma_start(out=d[0:1, :], in_=s_)))(), inc="dR2", inc_by=16)
        mk_small = ("dR2", p.cnt["dR2"])
        for dst, src in [(bqb, bq_h), (bkb, bk_h), (bvb, bv_h), (bob, bo_h)]:
            p.step("gpsimd", (lambda d=dst, s_=src: (lambda g:
                g.dma_start(out=d[0:1, :], in_=s_)))(), inc="dGr", inc_by=16)
        mk_bias = ("dGr", p.cnt["dGr"])
        for lane, w_h2, w_sb in (("dGwq", wq_h, wq), ("dGwk", wk_h, wk),
                                 ("dGwv", wv_h, wv), ("dGwo", wo_h, wo)):
            for kb in range(KB):
                p.step("gpsimd", (lambda wh=w_h2, ws=w_sb, kb=kb: (lambda g:
                    g.dma_start(out=ws[:, kb * D:(kb + 1) * D],
                                in_=wh[kb * 128:(kb + 1) * 128, :])))(),
                    inc=lane, inc_by=16)
        mk_wq = ("dGwq", p.cnt["dGwq"])
        mk_wk = ("dGwk", p.cnt["dGwk"])
        mk_wv = ("dGwv", p.cnt["dGwv"])
        mk_wo = ("dGwo", p.cnt["dGwo"])

        dL_x = {}
        for c in range(5):
            lane = "dL%d" % c
            dL_x[c] = (lane, p.step("sync", (lambda c=c: (lambda s_:
                s_.dma_start(out=xin[c][:, :], in_=xs_h[c * 128:(c + 1) * 128, :])))(),
                inc=lane, inc_by=16))

        t_cm = p.step("tensor", lambda t: t.matmul(
            Pq[:, 0:1], onesf[0:1, :], cm_sb[0:1, 0:1], start=True, stop=True),
            deps=[mk_small, ("g", g_const)], inc="t")
        v_cm = p.step("vector", lambda v: v.tensor_copy(cmpp[:, 0:1], Pq[:, 0:1]),
                      deps=[("t", t_cm)], inc="v")
        v_prev = None
        for dst, srcrow in ((gamb, gam_r), (betb, bet_r)):
            for half in range(2):
                dep = [("v", v_prev)] if v_prev else [mk_small]
                t_b = p.step("tensor", (lambda h=half, r=srcrow: (lambda t: t.matmul(
                    Pv[0][:, 0:512], onesf[0:1, :], r[0:1, h * 512:(h + 1) * 512],
                    start=True, stop=True)))(), deps=dep, inc="t")
                v_prev = p.step("vector", (lambda h=half, d=dst: (lambda v: v.tensor_copy(
                    d[:, h * 512:(h + 1) * 512], Pv[0][:, :])))(),
                    deps=[("t", t_b)], inc="v")
        v_bb2 = v_prev
        t_de = p.step("tensor", lambda t: t.matmul(
            Pk[:, 0:8], onesf[0:1, :],
            dec_r[0:1, :].rearrange("p (a b) -> p a b", b=2)[:, :, 0],
            start=True, stop=True), deps=[mk_small], inc="t")
        t_do = p.step("tensor", lambda t: t.matmul(
            Pk[:, 8:16], onesf[0:1, :],
            dec_r[0:1, :].rearrange("p (a b) -> p a b", b=2)[:, :, 1],
            start=True, stop=True), inc="t")
        p.step("vector", lambda v: v.tensor_copy(dpp[0:64, 0:16:2], Pk[0:64, 0:8]),
               deps=[("t", t_do)], inc="v")
        p.step("vector", lambda v: v.tensor_copy(dpp[64:128, 0:16:2], Pk[64:128, 8:16]),
               inc="v")
        p.step("vector", lambda v: v.tensor_copy(dpp[0:64, 1:16:2], Pk[0:64, 0:8]),
               inc="v")
        v_d4 = p.step("vector", lambda v: v.tensor_copy(dpp[64:128, 1:16:2], Pk[64:128, 8:16]),
                      inc="v")
        p.step("vector", lambda v: v.memset(
            dect[:, :].rearrange("p (a b) -> p a b", b=128)[:, :, 0], 0.125))

        dview = dect[:, :].rearrange("p (a b) -> p a b", b=128)
        for tp in range(1, 8):
            p.step("vector", (lambda tp=tp: (lambda v: v.tensor_tensor(
                dview[:, :, tp], dview[:, :, tp - 1], dpp[:, :], op=ALU.mult)))())
        for _ in range(3):
            p.step("vector", lambda v: v.tensor_tensor(
                dpp[:, :], dpp[:, :], dpp[:, :], op=ALU.mult))
        r8 = d8rep[:, :].rearrange("p (a k) -> p a k", k=8)
        p.step("vector", lambda v: v.tensor_copy(r8[:, :, 0:1], dpp[:, :].rearrange(
            "p (a k) -> p a k", k=1)))
        p.step("vector", lambda v: v.tensor_copy(r8[:, :, 1:2], r8[:, :, 0:1]))
        p.step("vector", lambda v: v.tensor_copy(r8[:, :, 2:4], r8[:, :, 0:2]))
        p.step("vector", lambda v: v.tensor_copy(r8[:, :, 4:8], r8[:, :, 0:4]))
        dblk = dect[:, :].rearrange("p (a c b) -> p a c b", a=16, c=16)
        v_dect = None
        for m in range(1, 16):
            v_dect = p.step("vector", (lambda m=m: (lambda v: v.tensor_tensor(
                dblk[:, :, m, :], dblk[:, :, m - 1, :], r8[:, :, :],
                op=ALU.mult)))(), inc="v" if m == 15 else None)

        t_bt = None
        for i, brow in enumerate((bqb, bkb)):
            for kb in range(KB):
                dep = [mk_bias, ("g", g_const)] if (i == 0 and kb == 0) else ()
                t_bt = p.step("tensor", (lambda brow=brow, i=i, kb=kb: (lambda t: t.transpose(
                    Pm[:, 32 + 2 * (KB * i + kb):32 + 2 * (KB * i + kb) + 1],
                    brow[0:1, kb * 128:(kb + 1) * 128],
                    onesb[0:1, 0:1])))(),
                    deps=dep, inc="t" if (i == 1 and kb == KB - 1) else None)
        p.step("vector", lambda v: v.tensor_copy(bqT[:, :], Pm[:, 32:32 + 2 * KB:2]),
               deps=[("t", t_bt)], inc="v")
        v_bt2 = p.step("vector", lambda v: v.tensor_copy(
            bkT[:, :], Pm[:, 32 + 2 * KB:32 + 4 * KB:2]), inc="v")
        free = {
            "Pm": [("v", v_bt2)],
            "Pq": [("v", v_d4)],
            "Pv": [("v", v_bb2)],
            "Pat": [],
        }
        ln1T_free = [[], []]
        qT_free = [[], []]
        kT_free = [[], []]
        vsb_free = [[], []]
        vcar_free = [[], []]
        xin_free = {i: [] for i in range(5)}
        psb_free = []
        pTs_free = []
        ato_free_box = [[]]
        atoT_free = []
        x2st_free = [[], [], []]
        xh2_free = [[]]
        ln1bf_free = [[], []]

        s_ln1 = {}
        v_lnT = {}
        s_kev = {}
        qk_t = {}
        psc_free = [[], []]
        xng_free = [[]]
        v_kcar = {}
        s_vev = {}
        dG_vcar = {}
        s_aev = {}
        v_resid = {}
        dS_x2 = {}
        g_xn = {}
        v_b2 = {}

        def _halo_late():
            p.step("vector", lambda v: v.reduce_sum(hsc[0:1, 0:1], xh_sb[0:1, :], axis=AX),
                   deps=[mk_small], inc="v")
            _ = 0
            p.step("vector", lambda v: v.tensor_scalar(hsc[0:1, 1:2], hsc[0:1, 0:1],
                                                       -1.0 / D, None, op0=ALU.mult))
            s_h1 = p.step("scalar", lambda s_: s_.activation(
                sqs[0:1, :], xh_sb[0:1, :], AF.Square),
                deps=[mk_small, sqs_free_box[0]], inc="s")
            v_h2 = p.step("vector", lambda v: v.reduce_sum(hsc[0:1, 2:3], sqs[0:1, :], axis=AX),
                          deps=[("s", s_h1)], inc="v")
            p.step("vector", lambda v: v.tensor_scalar(hsc[0:1, 3:4], hsc[0:1, 1:2],
                                                       hsc[0:1, 1:2], None, op0=ALU.mult))
            p.step("vector", lambda v: v.tensor_scalar(hsc[0:1, 4:5], hsc[0:1, 2:3],
                                                       1.0 / D, EPS, op0=ALU.mult, op1=ALU.add))
            p.step("vector", lambda v: v.tensor_tensor(hsc[0:1, 5:6], hsc[0:1, 4:5],
                                                       hsc[0:1, 3:4], op=ALU.subtract))
            v_h3 = p.step("vector", lambda v: v.tensor_copy(hsc[0:1, 6:7], hsc[0:1, 5:6]),
                          inc="v")
            s_hl = p.step("scalar", lambda s_: s_.activation(
                hsc[0:1, 6:7], hsc[0:1, 5:6], AF.Ln), deps=[("v", v_h3)], inc="s")
            s_h2 = p.step("scalar", lambda s_: s_.activation(
                hsc[0:1, 7:8], hsc[0:1, 6:7], AF.Exp, scale=-0.5),
                deps=[("s", s_hl)], inc="s")
            v_h4 = p.step("vector", lambda v: v.tensor_scalar(
                hsc[0:1, 3:4], hsc[0:1, 1:2], hsc[0:1, 7:8], None, op0=ALU.mult),
                deps=[("s", s_h2)], inc="v")
            s_h3 = p.step("scalar", lambda s_: s_.activation(
                lnh_bf[0:1, :], xh_sb[0:1, :], AF.Identity,
                bias=hsc[0:1, 3:4], scale=hsc[0:1, 7:8]), deps=[("v", v_h4)], inc="s")
            t_lnhT = None
            for kb in range(KB):
                dep = [("s", s_h3)] + free["Pm"] if kb == 0 else ()
                t_lnhT = p.step("tensor", (lambda kb=kb: (lambda t: t.transpose(
                    Pm[:, 2 * kb:2 * kb + 1], lnh_bf[0:1, kb * 128:(kb + 1) * 128],
                    onesb[0:1, 0:1])))(),
                    deps=dep, inc="t" if kb == KB - 1 else None)
            v_lnhT = p.step("vector", lambda v: v.tensor_copy(lnhT[:, :], Pm[:, 0:2 * KB:2]),
                            deps=[("t", t_lnhT)], inc="v")
            t_kh = None
            for mb in range(KB):
                for kb in range(KB):
                    dep = []
                    if mb == 0 and kb == 0:
                        dep = [("v", v_lnhT), mk_wk, mk_small, mk_bias, ("v", v_cm)] + free["Pq"]
                    p.step("tensor", (lambda mb=mb, kb=kb: (lambda t: t.matmul(
                        Pq[:, mb:mb + 1], wk[:, kb * D + mb * 128:kb * D + (mb + 1) * 128],
                        lnhT[:, kb:kb + 1], start=(kb == 0), stop=False)))(), deps=dep)
                t_kh = p.step("tensor", (lambda mb=mb: (lambda t: t.matmul(
                    Pq[:, mb:mb + 1], bkb[0:1, mb * 128:(mb + 1) * 128],
                    onesb[0:1, 0:1], start=False, stop=True)))(),
                    inc="t" if mb == KB - 1 else None)
            v_kh = p.step("vector", lambda v: v.tensor_scalar(
                kcs[:, :], Pq[:, 0:KB], cmpp[:, 0:1], None, op0=ALU.mult),
                deps=[("t", t_kh)], inc="v")
            v_kh2 = p.step("vector", lambda v: v.tensor_copy(
                kT[0][:, :].rearrange("p (a b) -> p a b", b=258)[:, :, 0], kcs[:, :]),
                deps=[("v", v_kh)], inc="v")
            t_vh = None
            for half in range(2):
                for kb in range(KB):
                    dep = ([("v", v_kh), mk_wv] + free["Pv"]) if (half == 0 and kb == 0) else ()
                    p.step("tensor", (lambda half=half, kb=kb: (lambda t: t.matmul(
                        Pv[half][0:1, :], lnhT[:, kb:kb + 1],
                        wv[:, kb * D + half * 512:kb * D + (half + 1) * 512],
                        start=(kb == 0), stop=False)))(), deps=dep)
                t_vh = p.step("tensor", (lambda half=half: (lambda t: t.matmul(
                    Pv[half][0:1, :], onesb[0:1, 0:1],
                    bvb[0:1, half * 512:(half + 1) * 512], start=False, stop=True)))(),
                    inc="t" if half == 1 else None)
            v_vh = None
            for half in range(2):
                v_vh = p.step("vector", (lambda half=half: (lambda v: v.tensor_scalar(
                    vcar[0][0:1, half * 512:(half + 1) * 512], Pv[half][0:1, :],
                    cm_sb[0:1, 0:1], None, op0=ALU.mult)))(), deps=[("t", t_vh)], inc="v")
            p.step("vector", lambda v: v.memset(amr[:, :], 0.0))


            free["Pm"] = [("v", v_lnhT)]
            free["Pq"] = [("v", v_kh)]
            free["Pv"] = [("v", v_vh)]
            kT_free[0] = [("v", v_kh2)]
            dG_vcar[0] = [("v", v_vh)]
            sqs_free_box[0] = ("v", v_h2)

        if upto == "pro":
            return _emit(p, block, sems, nc)
        sqs_free_box = [mk_small]
        ln1v_free = [[], []]

        def ln1_stats(cc):
            cp = cc % 2
            slot = cc % 5
            o = cp * 8
            depx = [dL_x[cc]]
            v0 = p.step("vector", (lambda slot=slot, o=o: (lambda v: v.reduce_sum(
                st1[:, o:o + 1], xin[slot][:, :], axis=AX)))(),
                deps=depx + ln1v_free[cp], inc="v")
            ln1v_free[cp] = []
            s1 = p.step("scalar", (lambda slot=slot, o=o: (lambda s_: s_.activation(
                sqs[:, :], xin[slot][:, :], AF.Square,
                accum_out=st1[:, o + 2:o + 3])))(),
                deps=depx + [sqs_free_box[0]], inc="s")
            sqs_free_box[0] = ("s", s1)
            sm1 = p.step("scalar", (lambda o=o: (lambda s_: s_.activation(
                st1[:, o + 1:o + 2], st1[:, o:o + 1], AF.Identity,
                scale=-1.0 / D)))(), deps=[("v", v0)], inc="s")
            ln1v_free[cp] = [("s", sm1)]
            p.step("scalar", (lambda o=o: (lambda s_: s_.activation(
                st1[:, o + 3:o + 4], st1[:, o + 1:o + 2], AF.Square)))())
            p.step("scalar", (lambda o=o: (lambda s_: s_.activation(
                st1[:, o + 4:o + 5], st1[:, o + 2:o + 3], AF.Identity,
                scale=1.0 / D, bias=epsc[:, 0:1])))(), deps=[("g", g_eps)])
            p.step("scalar", (lambda o=o: (lambda s_: s_.activation(
                st1[:, o + 5:o + 6], st1[:, o + 3:o + 4], AF.Identity,
                scale=-1.0, bias=st1[:, o + 4:o + 5])))())
            p.step("scalar", (lambda o=o: (lambda s_: s_.activation(
                st1[:, o + 6:o + 7], st1[:, o + 5:o + 6], AF.Ln)))())
            p.step("scalar", (lambda o=o: (lambda s_: s_.activation(
                st1[:, o + 7:o + 8], st1[:, o + 6:o + 7], AF.Exp, scale=-0.5)))())
            p.step("scalar", (lambda cp=cp, o=o: (lambda s_: s_.activation(
                st1[:, 16 + cp:17 + cp], st1[:, o + 1:o + 2], AF.Identity,
                scale=st1[:, o + 7:o + 8])))())
            s3 = p.step("scalar", (lambda slot=slot, cp=cp, o=o: (lambda s_: s_.activation(
                ln1bf[cp][:, :], xin[slot][:, :], AF.Identity,
                bias=st1[:, 16 + cp:17 + cp], scale=st1[:, o + 7:o + 8])))(),
                deps=ln1bf_free[cp], inc="s")
            s_ln1[cc] = s3
            if dbg_dump and cc == 0:
                p.step("sync", lambda s_: s_.dma_start(out=dmp["d_ln1bf"], in_=ln1bf[0][:, :]),
                       deps=[("s", s3)], inc="dbg", inc_by=16)
                p.step("sync", lambda s_: s_.dma_start(out=dmp["d_xin0"], in_=xin[0][:, :]),
                       deps=[("s", s3)], inc="dbg", inc_by=16)
                p.step("sync", lambda s_: s_.dma_start(out=dmp["d_st1"], in_=st1[:, 0:8]),
                       deps=[("s", s3)], inc="dbg", inc_by=16)
                p.step("sync", lambda s_: s_.dma_start(out=dmp["d_vc"], in_=vcar[0][0:1, :]),
                       deps=[("s", s3)], inc="dbg", inc_by=16)

        def ln1_trans(cc):
            cp = cc % 2
            gpx = (cc // 2) % 2
            cgx = cc % 2
            s3 = s_ln1[cc]
            t_lnT = None
            for kb in range(KB):
                dep = [("s", s3)] + free["Pm"] if kb == 0 else ()
                t_lnT = p.step("tensor", (lambda kb=kb, cp=cp: (lambda t: t.transpose(
                    Pm[:, kb * 128:(kb + 1) * 128], ln1bf[cp][:, kb * 128:(kb + 1) * 128],
                    idn[:, :])))(), deps=dep, inc="t" if kb == KB - 1 else None)

            def lnT_ev(v, gp=gpx, cg=cgx):
                dst = ln1T[gp][:, :].rearrange("p (a b c) -> p a b c", a=KB, b=2)[:, :, cg, :]
                src = Pm[:, 0:1024].rearrange("p (a b) -> p a b", b=128)
                return v.tensor_copy(dst, src)
            dep = [("t", t_lnT)]
            if cgx == 0:
                dep += ln1T_free[gpx]
            vv = p.step("vector", lnT_ev, deps=dep, inc="v")
            v_lnT[cc] = vv
            free["Pm"] = [("v", vv)]
            ln1bf_free[cp] = [("v", vv)]

        def emit_ln1(cc):
            ln1_stats(cc)
            ln1_trans(cc)


        def qk_mm(gq, mb):
            gpq = gq % 2

            def run():
                chq = 2 * gq + 1
                for kb in range(KB):
                    dep = []
                    if kb == 0:
                        dep = [("v", v_lnT[chq]), ("v", v_lnT[chq - 1])] + free["Pq"]
                        if gq == 0 and mb == 0:
                            dep += [mk_wq, mk_wk]
                    p.step("tensor", (lambda mb=mb, kb=kb, gpq=gpq: (lambda t: t.matmul(
                        Pq[:, 0:256], wq[:, kb * D + mb * 128:kb * D + (mb + 1) * 128],
                        ln1T[gpq][:, kb * 256:(kb + 1) * 256],
                        start=(kb == 0), stop=(kb == KB - 1))))(), deps=dep)
                tk = None
                for kb in range(KB):
                    tk = p.step("tensor", (lambda mb=mb, kb=kb, gpq=gpq: (lambda t: t.matmul(
                        Pq[:, 256:512], wk[:, kb * D + mb * 128:kb * D + (mb + 1) * 128],
                        ln1T[gpq][:, kb * 256:(kb + 1) * 256],
                        start=(kb == 0), stop=(kb == KB - 1))))(),
                        inc="t" if kb == KB - 1 else None)
                qk_t[(gq, mb)] = tk
            return run

        def qk_ev(gq, mb):
            gpq = gq % 2

            def run():
                dep = [("t", qk_t[(gq, mb)])]
                if mb == 0:
                    dep += [("v", v_dect)] + qT_free[gpq]
                vq = p.step("vector", (lambda mb=mb, gpq=gpq: (lambda v: v.scalar_tensor_tensor(
                    qT[gpq][:, mb * 256:(mb + 1) * 256], Pq[:, 0:256],
                    bqT[:, mb:mb + 1],
                    dect[:, mb * 256:(mb + 1) * 256],
                    op0=ALU.add, op1=ALU.mult)))(),
                    deps=dep, inc="v")
                dep = []
                if mb == 0:
                    dep += kT_free[gpq]

                def kev(v_, mb=mb, gpq=gpq):
                    dst = kT[gpq][:, :].rearrange(
                        "p (m c k) -> p m c k", m=KB, c=2)[:, mb, :, 1:129]
                    return v_.tensor_scalar(dst, Pq[:, 256:512].rearrange(
                        "p (a b) -> p a b", b=128), bkT[:, mb:mb + 1],
                        None, op0=ALU.add)
                sk = p.step("vector", kev, deps=dep, inc="v")
                free["Pq"] = [("v", sk)]
                s_kev[(gq, mb)] = ("v", sk)
            return run

        def kcar_bundle(gq):
            gpq = gq % 2

            def run():
                kdep = [s_kev[(gq, KB - 1)]]

                def kcar0(v, gpq=gpq):
                    src = kT[1 - gpq][:, :].rearrange("p (a b) -> p a b", b=258)[:, :, 257]
                    dst = kT[gpq][:, :].rearrange("p (a b) -> p a b", b=258)[:, :, 0]
                    return v.tensor_copy(dst, src)

                def kcar1(v, gpq=gpq):
                    src = kT[gpq][:, :].rearrange("p (a b) -> p a b", b=258)[:, :, 128]
                    dst = kT[gpq][:, :].rearrange("p (a b) -> p a b", b=258)[:, :, 129]
                    return v.tensor_copy(dst, src)
                if gq > 0:
                    p.step("vector", kcar0, deps=kdep, inc="v")
                v_kcar[gq] = p.step("vector", kcar1, deps=kdep, inc="v")
            return run

        def v_bundle(cv, half):
            gpv = (cv // 2) % 2
            cgv = cv % 2

            def run():
                Pb = Pv1 if half == 0 else Pq
                bank = "Pv" if half == 0 else "Pq"
                for kb in range(KB):
                    dep = []
                    if kb == 0:
                        dep = list(free[bank])
                        if cgv == 0 and half == 0:
                            dep += vsb_free[gpv]
                        if cv <= 1 and half == 0:
                            dep += [mk_wv]
                    p.step("tensor", (lambda Pb=Pb, kb=kb, gpv=gpv, cgv=cgv, half=half:
                        (lambda t: t.matmul(
                            Pb[:, 0:512],
                            ln1T[gpv][:, kb * 256 + cgv * 128:kb * 256 + (cgv + 1) * 128],
                            wv[:, kb * D + half * 512:kb * D + (half + 1) * 512],
                            start=(kb == 0), stop=False)))(), deps=dep)
                tv = p.step("tensor", (lambda Pb=Pb, half=half: (lambda t: t.matmul(
                    Pb[:, 0:512], onesb[0:1, 0:128],
                    bvb[0:1, half * 512:(half + 1) * 512],
                    start=False, stop=True)))(), inc="t")
                sv_ = p.step("scalar", (lambda Pb=Pb, half=half, gpv=gpv, cgv=cgv:
                    (lambda s_: s_.activation(
                        vsb[gpv][:, cgv * 1024 + half * 512:cgv * 1024 + (half + 1) * 512],
                        Pb[:, 0:512], AF.Identity)))(),
                    deps=[("t", tv)], inc="s")
                free[bank] = [("s", sv_)]
                if half == 1:
                    s_vev[cv] = sv_
            return run

        ln2m_box = [[]]

        def ln2_bundle(ca):
            cpa = ca % 2
            cp3 = ca % 3
            box = {}

            def run1():
                vres = v_resid[ca]
                _ = cpa
                v0b = p.step("vector", (lambda cp3=cp3: (lambda v: v.reduce_sum(
                    st2[:, 0:1], x2st[cp3][:, :], axis=AX)))(),
                    deps=[("v", vres)] + ln2m_box[0], inc="v")
                s1b = p.step("scalar", (lambda cp3=cp3, ca=ca: (lambda s_: s_.activation(
                    sqs[:, :], x2st[cp3][:, :], AF.Square,
                    accum_out=st2[:, 2:3])))(),
                    deps=[("v", vres), sqs_free_box[0]], inc="s")
                sqs_free_box[0] = ("s", s1b)
                sm1b = p.step("scalar", (lambda ca=ca: (lambda s_: s_.activation(
                    nm2a[:, ca:ca + 1], st2[:, 0:1], AF.Identity,
                    scale=-1.0 / D)))(), deps=[("v", v0b)], inc="s")
                ln2m_box[0] = [("s", sm1b)]
                p.step("scalar", (lambda ca=ca: (lambda s_: s_.activation(
                    st2[:, 3:4], nm2a[:, ca:ca + 1], AF.Square)))())
                p.step("scalar", lambda s_: s_.activation(
                    st2[:, 4:5], st2[:, 2:3], AF.Identity,
                    scale=1.0 / D, bias=epsc[:, 0:1]), deps=[("g", g_eps)])
                p.step("scalar", lambda s_: s_.activation(
                    st2[:, 5:6], st2[:, 3:4], AF.Identity,
                    scale=-1.0, bias=st2[:, 4:5]))
                p.step("scalar", (lambda ca=ca: (lambda s_: s_.activation(
                    st2[:, 6:7], st2[:, 5:6], AF.Ln)))())
                s2b = p.step("scalar", (lambda ca=ca: (lambda s_: s_.activation(
                    rs2a[:, ca:ca + 1], st2[:, 6:7], AF.Exp, scale=-0.5)))(), inc="s")
                s4b = p.step("scalar", (lambda ca=ca: (lambda s_: s_.activation(
                    b2a[:, ca:ca + 1], nm2a[:, ca:ca + 1], AF.Identity,
                    scale=rs2a[:, ca:ca + 1])))(), inc="s")
                _ = s2b
                v_b2[ca] = ("s", s4b)
                box["s1b"] = s1b
                box["s4b"] = s4b

            def run2a():
                s1b = box["s1b"]
                s3b = p.step("scalar", (lambda ca=ca, cpa=cpa, cp3=cp3: (lambda s_: s_.activation(
                    xh2[0][:, :], x2st[cp3][:, :], AF.Identity,
                    bias=b2a[:, ca:ca + 1], scale=rs2a[:, ca:ca + 1])))(),
                    deps=[("s", box["s4b"])] + xh2_free[0], inc="s")
                x2st_free[cp3] = [dS_x2[ca], ("s", s1b), ("s", s3b)]
                dep = [("s", s3b), ("v", v_bb2)] + xng_free[0]
                gx1 = p.step("gpsimd", (lambda cpa=cpa: (lambda g: g.tensor_tensor(
                    xng[:, :], xh2[0][:, :], gamb[:, :], op=ALU.mult)))(),
                    deps=dep, inc="g")
                gx = p.step("gpsimd", lambda g: g.tensor_tensor(
                    xng[:, :], xng[:, :], betb[:, :], op=ALU.add),
                    deps=[("g", gx1)], inc="g")
                xh2_free[0] = [("g", gx)]
                box["gx"] = gx

            def run2b():
                gx = box["gx"]
                p.step("vector", lambda v: v.reduce_max(
                    amt[:, 0:1], xng[:, :], axis=AX, apply_absolute_value=True),
                    deps=[("g", gx)])
                vam = p.step("vector", lambda v: v.tensor_tensor(
                    amr[:, 0:1], amr[:, 0:1], amt[:, 0:1], op=ALU.max), inc="v")
                g_xn[ca] = vam
                xng_free[0] = [("v", vam)]
            return [run1, run2a, run2b]

        emit_ln1(0)
        emit_ln1(1)
        emit_ln1(2)
        emit_ln1(3)
        for mb in range(KB):
            qk_mm(0, mb)()
            qk_ev(0, mb)()
        _halo_late()
        kcar_bundle(0)()
        for cv in (0, 1):
            for half in (0, 1):
                v_bundle(cv, half)()

        def mk_chunk(ca):
            g2 = ca // 2
            gp = g2 % 2
            cga = ca % 2
            cpa = ca % 2
            s_exp = {}
            t_sc = {}
            vn_d = {}
            t_pT = {}
            vp2_d = {}
            sa_mid = {}
            sa_steps = {}

            def emit_S(j):
                if j == 0:
                    dep = [("v", v_kcar[g2]), ("s", s_vev[ca])] + psc_free[0]
                elif j == 1:
                    dep = [("s", s_vev[ca])] + psc_free[1]
                else:
                    dep = [("s", s_exp[j - 2])]
                bk0 = 2 * (j % 2)
                for h2 in range(2):
                    t_sc[j] = p.step("tensor", (lambda j=j, h2=h2, gp=gp, cga=cga, bk0=bk0:
                        (lambda t: t.matmul(
                            Psc[bk0 + h2][:, 0:129],
                            qT[gp][h2 * 64:(h2 + 1) * 64,
                                   j * 256 + cga * 128:j * 256 + (cga + 1) * 128],
                            kT[gp][h2 * 64:(h2 + 1) * 64,
                                   j * 258 + cga * 129:j * 258 + (cga + 1) * 129],
                            start=True, stop=True,
                            tile_position=(h2 * 64, 0))))(),
                        deps=dep if h2 == 0 else (),
                        inc="t" if h2 == 1 else None)

            def emit_X(j):
                bk0 = 2 * (j % 2)
                p.step("vector", (lambda j=j, cpa=cpa, bk0=bk0: (lambda v: v.reduce_max(
                    nmx[cpa][:, 2 * j:2 * j + 1],
                    Psc[bk0][:, 0:129], axis=AX, negate=True)))(),
                    deps=[("t", t_sc[j])], inc="v")
                vn_d[j] = p.step("vector", (lambda j=j, cpa=cpa, bk0=bk0:
                    (lambda v: v.reduce_max(
                        nmx[cpa][:, 2 * j + 1:2 * j + 2],
                        Psc[bk0 + 1][:, 0:129], axis=AX, negate=True)))(), inc="v")
                for h2 in range(2):
                    h = 2 * j + h2
                    dep2 = [("v", vn_d[j])]
                    if j == 0 and h2 == 0:
                        dep2 += psb_free
                    bk0 = 2 * (j % 2)
                    s_exp[j] = p.step("scalar", (lambda h=h, h2=h2, cpa=cpa, bk0=bk0:
                        (lambda s_: s_.activation(
                            psb[:, h * 129:(h + 1) * 129],
                            Psc[bk0 + h2][:, 0:129], AF.Exp,
                            bias=nmx[cpa][:, h:h + 1],
                            accum_out=dnm[cpa][:, h:h + 1])))(),
                        deps=dep2, inc="s")

            def emit_T(j):
                for h2 in range(2):
                    h = 2 * j + h2
                    dep2 = [("s", s_exp[j])] + free["Pm"] if h2 == 0 else ()
                    p.step("tensor", (lambda h=h, h2=h2: (lambda t: t.transpose(
                        Pm[:, h2 * 128:(h2 + 1) * 128],
                        psb[:, h * 129 + 1:(h + 1) * 129], idn[:, :])))(),
                        deps=dep2)
                    t_pT[j] = p.step("tensor", (lambda h=h, h2=h2: (lambda t: t.transpose(
                        Pm[0:1, 256 + h2 * 128:256 + (h2 + 1) * 128],
                        psb[:, h * 129:h * 129 + 1], idn[:, :])))(),
                        inc="t" if h2 == 1 else None)

            def emit_E(j):
                dep2 = [("t", t_pT[j])]
                if j == 0:
                    dep2 += pTs_free
                p.step("vector", (lambda j=j: (lambda v: v.tensor_copy(
                    pTs[:, j * 256:(j + 1) * 256], Pm[:, 0:256])))(),
                    deps=dep2, inc="v")
                vp2_d[j] = p.step("vector", (lambda j=j: (lambda v: v.tensor_copy(
                    pcT[0:1, j * 256:(j + 1) * 256], Pm[0:1, 256:512])))(),
                    inc="v")
                free["Pm"] = [("v", vp2_d[j])]

            def emit_A(j):
                for h2 in range(2):
                    h = 2 * j + h2
                    col = (h % 8) * 64
                    dep2 = [("v", vp2_d[j])]
                    if j == 0 and h2 == 0:
                        dep2 += free["Pat"] + dG_vcar[ca]
                    if j == 4 and h2 == 0:
                        dep2 += [("s", sa_mid[0])]
                    p.step("tensor", (lambda h=h, col=col, gp=gp, cga=cga:
                        (lambda t: t.matmul(
                            Pat[:, col:col + 64],
                            pTs[:, h * 128:(h + 1) * 128],
                            vsb[gp][:, cga * 1024 + h * 64:cga * 1024 + (h + 1) * 64],
                            start=True, stop=False)))(), deps=dep2)
                    p.step("tensor", (lambda h=h, col=col, ca=ca:
                        (lambda t: t.matmul(
                            Pat[:, col:col + 64],
                            pcT[0:1, h * 128:(h + 1) * 128],
                            vcar[ca % 2][0:1, h * 64:(h + 1) * 64],
                            start=False, stop=True)))(),
                        inc="t" if h2 == 1 else None)

            def emit_half_evac(hh, t_mark=None):
                t_now = p.cnt["t"] if t_mark is None else t_mark
                vr = p.step("vector", (lambda cpa=cpa, hh=hh: (lambda v: v.reciprocal(
                    rcp[cpa][:, 8 * hh:8 * hh + 8],
                    dnm[cpa][:, 8 * hh:8 * hh + 8])))(),
                    deps=[("s", s_exp[4 * hh + 3])], inc="v")
                sa = None
                for i in range(8):
                    h = 8 * hh + i
                    dep = [("v", vr)]
                    if i == 0:
                        dep += [("t", t_now)]
                        if hh == 0:
                            dep += ato_free_box[0]
                    sa = p.step("scalar", (lambda h=h, i=i, cpa=cpa:
                        (lambda s_: s_.activation(
                            ato[:, h * 64:(h + 1) * 64],
                            Pat[:, i * 64:(i + 1) * 64], AF.Identity,
                            scale=rcp[cpa][:, h:h + 1])))(), deps=dep, inc="s")
                    sa_steps[(hh, i)] = sa
                sa_mid[hh] = sa

            return dict(S=emit_S, X=emit_X, T=emit_T, E=emit_E, A=emit_A,
                        H=emit_half_evac, s_exp=s_exp, sa_mid=sa_mid,
                        sa_steps=sa_steps)

        ctx_box = [None]
        late2 = []

        for g_ in range(NCH // 2):
            gp = g_ % 2
            nxt = []
            if 2 * g_ + 3 < NCH:
                for mb in range(KB):
                    nxt.append(qk_mm(g_ + 1, mb))
                    nxt.append(qk_ev(g_ + 1, mb))
                nxt.append(kcar_bundle(g_ + 1))

            def drip(n=1, late_ok=False):
                for _ in range(n):
                    if nxt:
                        nxt.pop(0)()

            for cga in range(2):
                ca = g_ * 2 + cga
                cpa = ca % 2
                if ctx_box[0] is not None:
                    C = ctx_box[0]
                else:
                    C = mk_chunk(ca)
                    C["S"](0)
                    C["X"](0)
                    C["S"](1)
                    C["X"](1)
                emit_T, emit_E, emit_S, emit_X, emit_A = (
                    C["T"], C["E"], C["S"], C["X"], C["A"])
                emit_half_evac = C["H"]
                s_exp = C["s_exp"]
                sa_mid = C["sa_mid"]

                for j in range(8):
                    if j != 0 or not C.get("TE0"):
                        emit_T(j)
                        emit_E(j)
                    if j + 2 < 8:
                        emit_S(j + 2)
                        emit_X(j + 2)
                    drip(1)
                    emit_A(j)
                    if j == 3:
                        emit_half_evac(0)
                    drip(1, late_ok=(j >= 3))
                    if j in (0, 1, 5) and late2:
                        late2.pop(0)()
                t_attn_done = p.cnt["t"]
                psb_free = [("t", t_attn_done)]
                pTs_free = [("t", t_attn_done)]
                vcar_free[ca % 2] = [("t", t_attn_done)]
                psc_free = [[("s", s_exp[6])], [("s", s_exp[7])]]
                if ca + 1 < NCH:
                    Cn = mk_chunk(ca + 1)
                    Cn["S"](0)
                    ctx_box[0] = Cn
                else:
                    ctx_box[0] = None
                t_aT0 = None
                for kb in range(4):
                    dep = [("s", sa_mid[0])] + free["Pm"] if kb == 0 else ()
                    t_aT0 = p.step("tensor", (lambda kb=kb: (lambda t: t.transpose(
                        Pm[:, 512 + kb * 128:512 + (kb + 1) * 128],
                        ato[:, kb * 128:(kb + 1) * 128], idn[:, :])))(),
                        deps=dep, inc="t" if kb == 3 else None)
                vaT0 = p.step("vector", lambda v: v.tensor_copy(
                    atoT[:, 0:512], Pm[:, 512:1024]),
                    deps=[("t", t_aT0)] + atoT_free, inc="v")
                free["Pm"] = free["Pm"] + [("v", vaT0)]
                if ctx_box[0] is not None:
                    ctx_box[0]["S"](1)
                wo_dep0 = [("v", vaT0)] + free["Pv"]
                if g_ == 0 and cga == 0:
                    wo_dep0 += [mk_wo]
                for kb in range(4):
                    p.step("tensor", (lambda kb=kb: (lambda t: t.matmul(
                        Pv[0][:, :], atoT[:, kb * 128:(kb + 1) * 128],
                        wo[:, kb * D:kb * D + 512],
                        start=(kb == 0), stop=False)))(),
                        deps=wo_dep0 if kb == 0 else ())
                emit_half_evac(1, t_mark=t_attn_done)
                sa = sa_mid[1]
                s_aev[ca] = sa
                free["Pat"] = [("s", sa)]
                ato_free_box[0] = []
                if ca + 1 < NCH:
                    lane = "dGv%d" % ((ca + 1) % 2)
                    dgv = p.step("gpsimd", (lambda gp=gp, cga=cga, ca=ca:
                        (lambda g: g.dma_start(
                            out=vcar[(ca + 1) % 2][0:1, :],
                            in_=vsb[gp][127:128, cga * 1024:(cga + 1) * 1024])))(),
                        deps=[("s", s_vev[ca])] + vcar_free[(ca + 1) % 2],
                        inc=lane, inc_by=16)
                    dG_vcar[ca + 1] = [(lane, dgv)]

                t_aT = None
                sa_steps = C["sa_steps"]
                for kb in range(4, KB):
                    dep = [("s", sa_steps[(1, 2 * (kb - 4) + 1)])]
                    if kb == 4:
                        dep += free["Pm"]
                    t_aT = p.step("tensor", (lambda kb=kb: (lambda t: t.transpose(
                        Pm[:, (kb - 4) * 128:(kb - 3) * 128],
                        ato[:, kb * 128:(kb + 1) * 128], idn[:, :])))(),
                        deps=dep, inc="t" if kb == KB - 1 else None)
                vaT = p.step("vector", lambda v: v.tensor_copy(
                    atoT[:, 512:1024], Pm[:, 0:512]),
                    deps=[("t", t_aT)], inc="v")
                free["Pm"] = [("v", vaT)]
                ato_free_box[0] = [("v", vaT0), ("v", vaT)]
                if ctx_box[0] is not None:
                    ctx_box[0]["X"](0)
                    ctx_box[0]["X"](1)

                vres = None
                for half in range(2):
                    Pb = Pv1 if half == 0 else Pat
                    kbs = range(4, KB) if half == 0 else range(KB)
                    for kb in kbs:
                        dep = []
                        if half == 0 and kb == 4:
                            dep = [("v", vaT)]
                        if half == 1 and kb == 0:
                            dep = [("s", sa)]
                        p.step("tensor", (lambda Pb=Pb, half=half, kb=kb: (lambda t: t.matmul(
                            Pb[:, 0:512], atoT[:, kb * 128:(kb + 1) * 128],
                            wo[:, kb * D + half * 512:kb * D + (half + 1) * 512],
                            start=(half == 1 and kb == 0), stop=False)))(), deps=dep)
                    tw = p.step("tensor", (lambda Pb=Pb, half=half: (lambda t: t.matmul(
                        Pb[:, 0:512], onesb[0:1, 0:128],
                        bob[0:1, half * 512:(half + 1) * 512],
                        start=False, stop=True)))(), inc="t")
                    dep = [("t", tw)]
                    if half == 0:
                        dep += x2st_free[ca % 3]
                    vres = p.step("vector", (lambda Pb=Pb, half=half, ca=ca:
                        (lambda v: v.tensor_tensor(
                            x2st[ca % 3][:, half * 512:(half + 1) * 512],
                            Pb[:, 0:512],
                            xin[ca % 5][:, half * 512:(half + 1) * 512],
                            op=ALU.add)))(), deps=dep, inc="v")
                    if half == 0:
                        free["Pv"] = [("v", vres)]
                        if ctx_box[0] is not None:
                            ctx_box[0]["T"](0)
                            ctx_box[0]["E"](0)
                            ctx_box[0]["TE0"] = True
                    else:
                        free["Pat"] = [("v", vres)]
                v_resid[ca] = vres
                xin_free[ca % 5] = [("v", vres)]
                atoT_free = [("t", p.cnt["t"])]
                if ca + 5 < NCH:
                    cc2 = ca + 5
                    lane = "dL%d" % (cc2 % 5)
                    dL_x[cc2] = (lane, p.step("sync", (lambda cc2=cc2: (lambda s_: s_.dma_start(
                        out=xin[cc2 % 5][:, :],
                        in_=xs_h[cc2 * 128:(cc2 + 1) * 128, :])))(),
                        deps=[("v", vres)], inc=lane, inc_by=16))

                dS_x2[ca] = ("dS%d" % (ca % 3), p.step("scalar", (lambda ca=ca:
                    (lambda s_: s_.dma_start(out=x2_h[ca * 128:(ca + 1) * 128, :],
                                 in_=x2st[ca % 3][:, :])))(),
                    deps=[("v", vres)], inc="dS%d" % (ca % 3), inc_by=16))

                if ca + 1 < NCH:
                    late2.extend(ln2_bundle(ca))
                else:
                    for _f in ln2_bundle(ca):
                        _f()

                if cga == 0 and 2 * g_ + 3 < NCH:
                    for cv in (2 * g_ + 2, 2 * g_ + 3):
                        for half in (0, 1):
                            nxt.append(v_bundle(cv, half))
                    for c4 in (2 * g_ + 4, 2 * g_ + 5):
                        if c4 < NCH:
                            nxt.append((lambda c4=c4: ln1_stats(c4)))
                    for c4 in (2 * g_ + 4, 2 * g_ + 5):
                        if c4 < NCH:
                            nxt.append((lambda c4=c4: ln1_trans(c4)))

            while nxt:
                nxt.pop(0)()
            ln1T_free[gp] = [("t", p.cnt["t"])]
            qT_free[gp] = [("t", p.cnt["t"])]
            kT_free[gp] = [("t", p.cnt["t"])]
            vsb_free[gp] = [("t", p.cnt["t"])]

        while late2:
            late2.pop(0)()
        tA_end = p.cnt["t"]

        if upto == "A":
            return _emit(p, block, sems, nc)
        v_last = g_xn[NCH - 1]
        t_am = p.step("tensor", lambda t: t.transpose(
            Pq[0:1, 0:128], amr[:, 0:1], idnf[:, :]),
            deps=[("v", v_last)] + free["Pq"] + psc_free[0] + psc_free[1], inc="t")
        v_am = p.step("vector", lambda v: v.reduce_max(
            sca[0:1, 0:1], Pq[0:1, 0:128], axis=AX), deps=[("t", t_am)], inc="v")
        free["Pq"] = [("v", v_am)]
        dg1 = p.step("gpsimd", lambda g: g.dma_start(out=cc_in[:], in_=sca[0:1, 0:1]),
                     deps=[("v", v_am)], inc="dGm", inc_by=16)
        c1 = p.step("gpsimd", lambda g: g.collective_compute(
            "AllGather", ALU.bypass, replica_groups=[list(range(8))],
            ins=[cc_in[:]], outs=[cc_out[:]]), deps=[("dGm", dg1)], inc="c")
        dg2 = p.step("gpsimd", lambda g: g.dma_start(out=gmx[0:1, :], in_=cc_out[:]),
                     deps=[("c", c1)], inc="dGm", inc_by=16)

        wfs2 = wo[:, 6144:8192].bitcast(F32)
        wfsb = [wfs, wfs2]
        absb2 = mskb
        WL = ("dW0", "dW1")
        dgw = {}
        for ob in range(2):
            dgw[ob] = p.step("sync", (lambda ob=ob: (lambda s_: s_.dma_start(
                out=wfsb[ob][:, :], in_=wf_h[ob * 128:(ob + 1) * 128, :])))(),
                deps=[("t", tA_end)], inc=WL[ob % 2], inc_by=16)
        vcs = None
        sab_d = {}
        vcs_d = {}
        for ob in range(KB):
            ab = absb if ob % 2 == 0 else absb2
            dep = [(WL[ob % 2], dgw[ob]), ("t", tA_end)]
            if ob >= 2:
                dep += [vcs_d[ob - 2]]
            sab = p.step("scalar", (lambda ob=ob, ab=ab: (lambda s_: s_.activation(
                ab[:, :], wfsb[ob % 2][:, :], AF.Abs)))(), deps=dep, inc="s")
            sab_d[ob] = ("s", sab)
            if ob + 2 < KB:
                dgw[ob + 2] = p.step("sync", (lambda ob=ob: (lambda s_: s_.dma_start(
                    out=wfsb[ob % 2][:, :],
                    in_=wf_h[(ob + 2) * 128:(ob + 3) * 128, :])))(),
                    deps=[("s", sab)], inc=WL[ob % 2], inc_by=16)
            vcs = p.step("vector", (lambda ob=ob, ab=ab: (lambda v: v.reduce_sum(
                csab[:, ob:ob + 1], ab[:, :], axis=AX)))(), deps=[("s", sab)], inc="v")
            vcs_d[ob] = ("v", vcs)
        v_cs = p.step("vector", lambda v: v.reduce_sum(amt[:, 0:1], csab[:, :], axis=AX),
                      deps=[("v", vcs)], inc="v")
        t_tot = p.step("tensor", lambda t: t.matmul(
            Pq[0:1, 0:1], amt[:, 0:1], onecf[:, 0:1], start=True, stop=True),
            deps=[("v", v_cs)] + free["Pq"], inc="t")
        v_dl = p.step("vector", lambda v: v.tensor_scalar(
            tots[0:1, 0:1], Pq[0:1, 0:1], 0.7 / (D * D), None, op0=ALU.mult),
            deps=[("t", t_tot)], inc="v")
        free["Pq"] = [("v", v_dl)]
        t_dl = p.step("tensor", lambda t: t.matmul(
            Pk[:, 0:1], onesf[0:1, :], tots[0:1, 0:1], start=True, stop=True),
            deps=[("v", v_dl)] + free["Pq"], inc="t")
        v_dl2 = p.step("vector", lambda v: v.tensor_copy(dlpp[:, 0:1], Pk[:, 0:1]),
                       deps=[("t", t_dl)], inc="v")
        free["Pq"] = [("v", v_dl2)]
        vt = None
        vs2 = None
        ssg_d = {}
        vt_d = {}
        for ob in range(KB):
            dg = p.step("sync", (lambda ob=ob: (lambda s_: s_.dma_start(
                out=wfsb[ob % 2][:, :], in_=wf_h[ob * 128:(ob + 1) * 128, :])))(),
                deps=[ssg_d[ob - 2]] if ob >= 2 else [sab_d[KB - 2 + ob]],
                inc=WL[ob % 2], inc_by=16)
            dep = [(WL[ob % 2], dg), ("v", v_dl2)]
            if ob >= 1:
                dep += [vt_d[ob - 1]]
            if ob == 0:
                dep += [vcs_d[KB - 1]]
            sab = p.step("scalar", (lambda ob=ob: (lambda s_: s_.activation(
                absb[:, :], wfsb[ob % 2][:, :], AF.Abs)))(), deps=dep, inc="s")
            ssg = p.step("scalar", (lambda ob=ob: (lambda s_: s_.activation(
                sgnb, wfsb[ob % 2][:, :], AF.Sign)))(), inc="s")
            ssg_d[ob] = ("s", ssg)
            vm = p.step("vector", (lambda ob=ob: (lambda v: v.tensor_scalar(
                mskb, absb[:, :], dlpp[:, 0:1], None, op0=ALU.is_gt, op1=ALU.add,
                accum_out=cnta[:, ob:ob + 1])))(),
                deps=[("s", sab)] + ([vt_d[ob - 1]] if ob >= 1 else []), inc="v")
            vs2 = p.step("vector", (lambda ob=ob: (lambda v: v.scalar_tensor_tensor(
                absb[:, :], absb[:, :], 1.0, mskb, op0=ALU.mult, op1=ALU.mult,
                accum_out=suma[:, ob:ob + 1])))(), deps=[("v", vm)], inc="v")
            vt = p.step("vector", (lambda ob=ob: (lambda v: v.tensor_tensor(
                tbf[:, ob * D:(ob + 1) * D], sgnb, mskb, op=ALU.mult)))(),
                deps=[("s", ssg)] + ([("t", tA_end)] if ob == 0 else []), inc="v")
            vt_d[ob] = ("v", vt)
        v_c1 = p.step("vector", lambda v: v.reduce_sum(amt[:, 0:1], cnta[:, :], axis=AX),
                      deps=[("v", vs2)], inc="v")
        t_c1 = p.step("tensor", lambda t: t.matmul(
            Pq[0:1, 0:1], amt[:, 0:1], onecf[:, 0:1], start=True, stop=True),
            deps=[("v", v_c1)] + free["Pq"], inc="t")
        v_c2 = p.step("vector", lambda v: v.tensor_scalar(
            tots[0:1, 1:2], Pq[0:1, 0:1], 1.0, None, op0=ALU.max),
            deps=[("t", t_c1)], inc="v")
        v_s1 = p.step("vector", lambda v: v.reduce_sum(amt[:, 0:1], suma[:, :], axis=AX),
                      inc="v")
        t_s1 = p.step("tensor", lambda t: t.matmul(
            Pq[0:1, 1:2], amt[:, 0:1], onecf[:, 0:1], start=True, stop=True),
            deps=[("v", v_s1)], inc="t")
        v_al = p.step("vector", lambda v: v.reciprocal(tots[0:1, 2:3], tots[0:1, 1:2]),
                      deps=[("v", v_c2)], inc="v")
        p.step("vector", lambda v: v.tensor_tensor(
            tots[0:1, 3:4], Pq[0:1, 1:2], tots[0:1, 2:3], op=ALU.mult),
            deps=[("t", t_s1), ("v", v_al)], inc="v")

        v_wT = None
        for ib in range(KB):
            t_w = None
            for ob in range(KB):
                dep = []
                if ob == 0:
                    dep = list(free["Pm"])
                    if ib == 0:
                        dep += [("v", vt)]
                t_w = p.step("tensor", (lambda ib=ib, ob=ob: (lambda t: t.transpose(
                    Pm[:, ob * 128:(ob + 1) * 128],
                    tbf[:, ob * D + ib * 128:ob * D + (ib + 1) * 128], idn[:, :])))(),
                    deps=dep, inc="t" if ob == KB - 1 else None)
            v_wT = p.step("vector", (lambda ib=ib: (lambda v: v.tensor_copy(
                wT[:, ib * D:(ib + 1) * D], Pm[:, 0:1024])))(), deps=[("t", t_w)], inc="v")
            free["Pm"] = [("v", v_wT)]

        v_gm = p.step("vector", lambda v: v.reduce_max(sca[0:1, 0:1], gmx[0:1, :], axis=AX),
                      deps=[("dGm", dg2)], inc="v")
        p.step("vector", lambda v: v.tensor_scalar(
            sca[0:1, 1:2], sca[0:1, 0:1], 1e-8, 1.0 / QMAX, op0=ALU.max, op1=ALU.mult))
        v_iv = p.step("vector", lambda v: v.reciprocal(sca[0:1, 2:3], sca[0:1, 1:2]),
                      inc="v")
        t_iv = p.step("tensor", lambda t: t.matmul(
            Pq[:, 0:1], onesf[0:1, :], sca[0:1, 2:3], start=True, stop=True),
            deps=[("v", v_iv)] + free["Pq"], inc="t")
        v_iv2 = p.step("vector", lambda v: v.tensor_copy(ivpp[:, 0:1], Pq[:, 0:1]),
                       deps=[("t", t_iv)], inc="v")
        free["Pq"] = [("v", v_iv2)]
        v_G = p.step("vector", lambda v: v.tensor_scalar(
            gamb[:, :], gamb[:, :], ivpp[:, 0:1], None, op0=ALU.mult),
            deps=[("v", v_iv2)], inc="v")
        v_B = p.step("vector", lambda v: v.tensor_scalar(
            betb[:, :], betb[:, :], ivpp[:, 0:1], None, op0=ALU.mult), inc="v")
        v_sa = p.step("vector", lambda v: v.tensor_tensor(
            tots[0:1, 4:5], tots[0:1, 3:4], sca[0:1, 1:2], op=ALU.mult), inc="v")
        t_sa = p.step("tensor", lambda t: t.matmul(
            Pk[:, 0:1], onesf[0:1, :], tots[0:1, 4:5], start=True, stop=True),
            deps=[("v", v_sa)] + free["Pq"], inc="t")
        v_sa2 = p.step("vector", lambda v: v.tensor_copy(sapp[:, 0:1], Pk[:, 0:1]),
                       deps=[("t", t_sa)], inc="v")

        if upto == "bar":
            return _emit(p, block, sems, nc)
        x2done = [("dS0", p.cnt["dS0"]), ("dS1", p.cnt["dS1"]), ("dS2", p.cnt["dS2"])]
        x2in_free = dict(xin_free)
        dL_b = {}
        for c in range(3):
            lane = "dL%d" % (c % 5)
            dL_b[c] = (lane, p.step("sync", (lambda c=c: (lambda s_: s_.dma_start(
                out=xin[c % 5][:, :], in_=x2_h[c * 128:(c + 1) * 128, :])))(),
                deps=x2in_free[c % 5] + x2done + [("v", v_resid[NCH - 1])],
                inc=lane, inc_by=16))
        rpre_free = [[("t", tA_end)], [("t", tA_end)]]
        rbf_free = [[], []]
        rT_free = [[], []]
        yt_free = [[], []]
        ysb_free = [[], []]
        dS_y = {}
        vy = None
        xh2b = [xh2[0], sqs]
        xh2b_free = [list(xh2_free[0]), [sqs_free_box[0]]]

        def b_stage1(c):
            cp = c % 2
            slot = c % 5
            sx = p.step("scalar", (lambda c=c, slot=slot, cp=cp: (lambda s_: s_.activation(
                xh2b[cp][:, :], xin[slot][:, :], AF.Identity,
                bias=b2a[:, c:c + 1], scale=rs2a[:, c:c + 1])))(),
                deps=[dL_b[c], v_b2[c]] + xh2b_free[cp], inc="s")
            v1 = p.step("vector", (lambda cp=cp: (lambda v: v.tensor_tensor(
                rpre[cp], xh2b[cp][:, :], gamb[:, :], op=ALU.mult)))(),
                deps=[("s", sx), ("v", v_G)] + rpre_free[cp], inc="v")
            xh2b_free[cp] = [("v", v1)]
            p.step("vector", (lambda cp=cp: (lambda v: v.tensor_tensor(
                rpre[cp], rpre[cp], betb[:, :], op=ALU.add)))(),
                deps=[("v", v_B)], inc="v")
            v3 = p.step("vector", (lambda cp=cp: (lambda v: v.tensor_scalar(
                rbf[cp], rpre[cp], MAGIC, MAGIC,
                op0=ALU.add, op1=ALU.subtract)))(), deps=rbf_free[cp], inc="v")
            rpre_free[cp] = [("v", v3)]
            return v3

        PmB = Psc[0][:, :].bitcast(BF16)
        free["PmB"] = list(psc_free[0]) + list(psc_free[1])

        def b_trans(c, v3):
            cp = c % 2
            dst = Pm if cp == 0 else PmB
            key = "Pm" if cp == 0 else "PmB"
            t_rT = None
            for kb in range(KB):
                dep = []
                if kb == 0:
                    dep = [("v", v3), ("v", v_wT)] + free[key]
                t_rT = p.step("tensor", (lambda kb=kb, cp=cp, dst=dst: (lambda t: t.transpose(
                    dst[:, kb * 128:(kb + 1) * 128],
                    rbf[cp][:, kb * 128:(kb + 1) * 128], idn[:, :])))(),
                    deps=dep, inc="t" if kb == KB - 1 else None)
            vrT = p.step("vector", (lambda cp=cp, dst=dst: (lambda v: v.tensor_copy(
                rT[cp], dst[:, 0:1024])))(), deps=[("t", t_rT)] + rT_free[cp], inc="v")
            free[key] = [("v", vrT)]
            rbf_free[cp] = [("v", vrT)]
            return vrT

        free["Pat"] = list(free["Pat"])
        v3p = {0: b_stage1(0)}
        vrTn = b_trans(0, v3p[0])
        if NCH > 1:
            v3p[1] = b_stage1(1)
        for c in range(NCH):
            cp = c % 2
            slot = c % 5
            vrT = vrTn
            if c + 3 < NCH:
                c2 = c + 3
                lane = "dL%d" % (c2 % 5)
                dL_b[c2] = (lane, p.step("sync", (lambda c2=c2: (lambda s_: s_.dma_start(
                    out=xin[c2 % 5][:, :], in_=x2_h[c2 * 128:(c2 + 1) * 128, :])))(),
                    deps=x2in_free[c2 % 5], inc=lane, inc_by=16))
            if c + 2 < NCH:
                v3p[c + 2] = b_stage1(c + 2)
            v3n = v3p.get(c + 1)
            tf = {}
            for half in range(2):
                Pb = Pv1 if half == 0 else Pat
                bank = "Pv" if half == 0 else "Pat"
                t_f = None
                for kb in range(KB):
                    dep = []
                    if kb == 0:
                        dep = [("v", vrT)] + free[bank]
                    t_f = p.step("tensor", (lambda Pb=Pb, half=half, kb=kb, cp=cp:
                        (lambda t: t.matmul(
                            Pb[:, 0:512], rT[cp][:, kb * 128:(kb + 1) * 128],
                            wT[:, kb * D + half * 512:kb * D + (half + 1) * 512],
                            start=(kb == 0), stop=(kb == KB - 1))))(),
                        deps=dep, inc="t" if kb == KB - 1 else None)
                tf[half] = t_f
            vy = None
            for half in range(2):
                Pb = Pv1 if half == 0 else Pat
                bank = "Pv" if half == 0 else "Pat"
                dep = [("t", tf[half]), ("v", v_sa2)]
                if half == 0:
                    dep += yt_free[cp]
                sy = p.step("scalar", (lambda Pb=Pb, half=half, cp=cp:
                    (lambda s_: s_.activation(
                        yt[cp][:, half * 512:(half + 1) * 512], Pb[:, 0:512],
                        AF.Identity, scale=sapp[:, 0:1])))(), deps=dep, inc="s")
                free[bank] = [("s", sy)]
                dep = [("s", sy)]
                if half == 0:
                    dep += ysb_free[cp]
                vy = p.step("vector", (lambda half=half, cp=cp, slot=slot:
                    (lambda v: v.tensor_tensor(
                        ysb[cp][:, half * 512:(half + 1) * 512],
                        yt[cp][:, half * 512:(half + 1) * 512],
                        xin[slot][:, half * 512:(half + 1) * 512], op=ALU.add)))(),
                    deps=dep, inc="v")
            yt_free[cp] = [("v", vy)]
            x2in_free[slot] = [("v", vy)]
            rT_free[cp] = [("t", p.cnt["t"])]
            if v3n is not None:
                vrTn = b_trans(c + 1, v3n)
            dS_y[c] = ("dS%d" % cp, p.step("scalar", (lambda c=c, cp=cp: (lambda s_:
                s_.dma_start(out=y_h[c * 128:(c + 1) * 128, :],
                             in_=ysb[cp][:, :])))(),
                deps=[("v", vy)], inc="dS%d" % cp, inc_by=16))
            ysb_free[cp] = [dS_y[c]]

        fin_deps = [("dS0", p.cnt["dS0"]), ("dS1", p.cnt["dS1"])]
        if p.cnt["dbg"]:
            fin_deps.append(("dbg", p.cnt["dbg"]))
        p.step("sync", lambda s_: s_.nop(), deps=fin_deps)
        return _emit(p, block, sems, nc)


def _emit(p, block, sems, nc):
    @block.sync
    def _(eng):
        p.run("sync", eng, sems)

    @block.scalar
    def _(eng):
        p.run("scalar", eng, sems)

    @block.vector
    def _(eng):
        p.run("vector", eng, sems)

    @block.tensor
    def _(eng):
        p.run("tensor", eng, sems)

    @block.gpsimd
    def _(eng):
        p.run("gpsimd", eng, sems)
    return nc


_NC_CACHE = None


def kernel(L0_attn_x, L0_attn_Wq, L0_attn_bq, L0_attn_Wk, L0_attn_bk,
           L0_attn_Wv, L0_attn_bv, L0_attn_Wo, L0_attn_bo, L0_attn_decay,
           L0_ffn_weight, L0_ffn_gamma, L0_ffn_beta):
    global _NC_CACHE
    x = np.asarray(L0_attn_x)
    B, L, _ = x.shape
    assert (B, L) == (2, 16384)
    if _NC_CACHE is None:
        _NC_CACHE = build_nc()
    nc = _NC_CACHE

    def row(a):
        return np.ascontiguousarray(np.asarray(a, dtype=np.float32).reshape(1, -1))

    shared = {
        "wq": np.ascontiguousarray(L0_attn_Wq, dtype=np.float32),
        "wk": np.ascontiguousarray(L0_attn_Wk, dtype=np.float32),
        "wv": np.ascontiguousarray(L0_attn_Wv, dtype=np.float32),
        "wo": np.ascontiguousarray(L0_attn_Wo, dtype=np.float32),
        "bq": row(L0_attn_bq), "bk": row(L0_attn_bk),
        "bv": row(L0_attn_bv), "bo": row(L0_attn_bo),
        "dec": row(L0_attn_decay),
        "wf": np.ascontiguousarray(L0_ffn_weight, dtype=np.float32),
        "gam": row(L0_ffn_gamma), "bet": row(L0_ffn_beta),
    }
    in_maps = []
    for i in range(8):
        b = i // 4
        t0 = (i % 4) * NT
        m = dict(shared)
        m["xs"] = np.ascontiguousarray(x[b, t0:t0 + NT, :], dtype=np.float32)
        if i % 4 == 0:
            m["xh"] = np.ascontiguousarray(x[b, 0:1, :], dtype=np.float32)
            m["cm"] = np.zeros((1, 1), dtype=np.float32)
        else:
            m["xh"] = np.ascontiguousarray(x[b, t0 - 1:t0, :], dtype=np.float32)
            m["cm"] = np.ones((1, 1), dtype=np.float32)
        in_maps.append(m)

    res = run_bass_kernel_spmd(nc, in_maps, core_ids=list(range(8)))
    out = np.empty((2, 16384, D), dtype=np.float32)
    for i in range(8):
        b = i // 4
        t0 = (i % 4) * NT
        out[b, t0:t0 + NT, :] = np.asarray(res.results[i]["y"], dtype=np.float32)
    return out


# version 46
# speedup vs baseline: 1.0165x; 1.0135x over previous
import numpy as np
from contextlib import ExitStack

import concourse.bass as bass
import concourse.mybir as mybir
from concourse.bass_utils import run_bass_kernel_spmd

F32 = mybir.dt.float32
BF16 = mybir.dt.bfloat16
AX = mybir.AxisListType.X
AF = mybir.ActivationFunctionType
ALU = mybir.AluOpType

MAGIC = 12582912.0
D = 1024
KB = 8
H = 16
C = 128
NT = 4096
NCH = 32
EPS = 1e-5
QMAX = 127.0


class Plan:

    ENGINES = ("sync", "scalar", "vector", "tensor", "gpsimd")
    OWN = {"vector": "v", "scalar": "s", "gpsimd": "g"}

    def __init__(self):
        self.steps = {e: [] for e in self.ENGINES}
        self.cnt = {k: 0 for k in ("dL0", "dL1", "dL2", "dL3", "dL4", "dS0", "dS1", "dS2", "dGwq", "dGwk", "dGwv", "dGwo", "dGr", "dR2", "dW0", "dW1", "dGv0", "dGv1", "dGm", "v", "s", "t", "g", "c", "dbg")}

    def step(self, eng, fn, deps=(), inc=None, inc_by=1):
        chain = None
        if eng in self.OWN and inc is None:
            inc = self.OWN[eng]
        if eng in self.OWN and inc == self.OWN[eng]:
            chain = (self.OWN[eng], self.cnt[self.OWN[eng]])
        val = None
        if inc is not None:
            self.cnt[inc] += inc_by
            val = self.cnt[inc]
        self.steps[eng].append((tuple(deps), fn, inc, inc_by, chain))
        return val

    def run(self, eng_name, eng, sems):
        watermark = {}
        for deps, fn, inc, inc_by, chain in self.steps[eng_name]:
            alldeps = list(deps)
            if chain is not None and chain[1] > 0:
                alldeps.append(chain)
            for s, v in alldeps:
                if v and watermark.get(s, -1) < v:
                    eng.wait_ge(sems[s], v)
                    watermark[s] = v
            inst = fn(eng)
            if inc is not None:
                inst.then_inc(sems[inc], inc_by)


def build_nc(upto='full', alvl=10, blvl=9, dbg_dump=False):
    nc = bass.Bass("TRN2", target_bir_lowering=False)

    xs_h = nc.dram_tensor("xs", [NT, D], F32, kind="ExternalInput").ap()
    xh_h = nc.dram_tensor("xh", [1, D], F32, kind="ExternalInput").ap()
    cm_h = nc.dram_tensor("cm", [1, 1], F32, kind="ExternalInput").ap()
    wq_h = nc.dram_tensor("wq", [D, D], F32, kind="ExternalInput").ap()
    wk_h = nc.dram_tensor("wk", [D, D], F32, kind="ExternalInput").ap()
    wv_h = nc.dram_tensor("wv", [D, D], F32, kind="ExternalInput").ap()
    wo_h = nc.dram_tensor("wo", [D, D], F32, kind="ExternalInput").ap()
    bq_h = nc.dram_tensor("bq", [1, D], F32, kind="ExternalInput").ap()
    bk_h = nc.dram_tensor("bk", [1, D], F32, kind="ExternalInput").ap()
    bv_h = nc.dram_tensor("bv", [1, D], F32, kind="ExternalInput").ap()
    bo_h = nc.dram_tensor("bo", [1, D], F32, kind="ExternalInput").ap()
    dec_h = nc.dram_tensor("dec", [1, H], F32, kind="ExternalInput").ap()
    wf_h = nc.dram_tensor("wf", [D, D], F32, kind="ExternalInput").ap()
    gam_h = nc.dram_tensor("gam", [1, D], F32, kind="ExternalInput").ap()
    bet_h = nc.dram_tensor("bet", [1, D], F32, kind="ExternalInput").ap()
    y_h = nc.dram_tensor("y", [NT, D], F32, kind="ExternalOutput").ap()
    x2_h = nc.dram_tensor("x2sp", [NT, D], F32).ap()
    cc_in = nc.dram_tensor("cc_in", [1], F32).ap()
    cc_out = nc.dram_tensor("cc_out", [8], F32).ap()
    dmp = {}
    if dbg_dump:
        for nmm, shp, dt in [("d_ln1bf", [128, 1024], BF16), ("d_lnT", [128, 2048], BF16),
                             ("d_qT", [128, 2048], BF16), ("d_kT", [128, 2064], BF16),
                             ("d_vsb", [128, 2048], BF16), ("d_vc", [1, 1024], BF16),
                             ("d_psb", [128, 2064], BF16), ("d_pT", [128, 2048], BF16),
                             ("d_pcT", [1, 2048], BF16), ("d_nm", [128, 16], F32),
                             ("d_dn", [128, 16], F32), ("d_rc", [128, 16], F32),
                             ("d_ato", [128, 1024], BF16), ("d_atoT", [128, 1024], BF16),
                             ("d_xin0", [128, 1024], F32), ("d_st1", [128, 9], F32)]:
            dmp[nmm] = nc.dram_tensor(nmm, shp, dt, kind="ExternalOutput").ap()

    p = Plan()

    with ExitStack() as st:
        e = st.enter_context

        wq = e(nc.sbuf_tensor("wq_sb", [128, KB * D], BF16))
        wk = e(nc.sbuf_tensor("wk_sb", [128, KB * D], BF16))
        wv = e(nc.sbuf_tensor("wv_sb", [128, KB * D], BF16))
        wo = e(nc.sbuf_tensor("wo_sb", [128, KB * D], BF16))
        bqb = e(nc.sbuf_tensor("bqb", [1, D], BF16))
        bkb = e(nc.sbuf_tensor("bkb", [1, D], BF16))
        bvb = e(nc.sbuf_tensor("bvb", [1, D], BF16))
        bob = e(nc.sbuf_tensor("bob", [1, D], BF16))
        gam_r = e(nc.sbuf_tensor("gam_r", [1, D], F32))
        bet_r = e(nc.sbuf_tensor("bet_r", [1, D], F32))
        dec_r = e(nc.sbuf_tensor("dec_r", [1, H], F32))
        cm_sb = e(nc.sbuf_tensor("cm_sb", [1, 1], F32))
        xh_sb = e(nc.sbuf_tensor("xh_sb", [1, D], F32))
        gamb = e(nc.sbuf_tensor("gamb", [128, D], F32))
        betb = e(nc.sbuf_tensor("betb", [128, D], F32))
        dect = e(nc.sbuf_tensor("dect", [128, 2048], F32))
        d8rep = e(nc.sbuf_tensor("d8rep", [128, 128], F32))
        dpp = e(nc.sbuf_tensor("dpp", [128, 16], F32))
        idn = e(nc.sbuf_tensor("idn", [128, 128], BF16))
        idnf = e(nc.sbuf_tensor("idnf", [128, 128], F32))
        onesb = e(nc.sbuf_tensor("onesb", [1, 256], BF16))
        onesf = e(nc.sbuf_tensor("onesf", [1, 128], F32))
        onecf = e(nc.sbuf_tensor("onecf", [128, 1], F32))
        cmpp = e(nc.sbuf_tensor("cmpp", [128, 1], F32))
        lnhT = e(nc.sbuf_tensor("lnhT", [128, KB], BF16))
        bqT = e(nc.sbuf_tensor("bqT", [128, KB], F32))
        bkT = e(nc.sbuf_tensor("bkT", [128, KB], F32))
        lnh_bf = e(nc.sbuf_tensor("lnh_bf", [1, D], BF16))
        xin = [e(nc.sbuf_tensor(f"xin{i}", [128, D], F32)) for i in range(5)]
        ln1bf = [e(nc.sbuf_tensor(f"ln1bf{i}", [128, D], BF16)) for i in range(2)]
        ln1T = [e(nc.sbuf_tensor(f"ln1T{i}", [128, 2048], BF16)) for i in range(2)]
        qT = [e(nc.sbuf_tensor(f"qT{i}", [128, 2048], BF16)) for i in range(2)]
        kT = [e(nc.sbuf_tensor(f"kT{i}", [128, 2064], BF16)) for i in range(2)]
        vsb = [e(nc.sbuf_tensor(f"vsb{i}", [128, 2048], BF16)) for i in range(2)]
        vcar = [e(nc.sbuf_tensor(f"vcar{i}", [1, D], BF16)) for i in range(2)]
        psb = e(nc.sbuf_tensor("psb", [128, H * 129], BF16))
        pTs = e(nc.sbuf_tensor("pTs", [128, H * 128], BF16))
        pcT = e(nc.sbuf_tensor("pcT", [1, H * 128], BF16))
        dnm = [e(nc.sbuf_tensor(f"dnm{i}", [128, H], F32)) for i in range(2)]
        nmx = [e(nc.sbuf_tensor(f"nmx{i}", [128, H], F32)) for i in range(2)]
        rcp = [e(nc.sbuf_tensor(f"rcp{i}", [128, H], F32)) for i in range(2)]
        ato = e(nc.sbuf_tensor("ato", [128, D], BF16))
        atoT = e(nc.sbuf_tensor("atoT", [128, D], BF16))
        x2st = [e(nc.sbuf_tensor(f"x2st{i}", [128, D], F32)) for i in range(3)]
        sqs = e(nc.sbuf_tensor("sqs", [128, D], F32))
        xh2 = [e(nc.sbuf_tensor("xh2", [128, D], F32))]
        xng = e(nc.sbuf_tensor("xng", [128, D], F32))
        st1 = e(nc.sbuf_tensor("st1", [128, 18], F32))
        epsc = e(nc.sbuf_tensor("epsc", [128, 1], F32))
        st2 = e(nc.sbuf_tensor("st2", [128, 8], F32))
        nm2a = e(nc.sbuf_tensor("nm2a", [128, NCH], F32))
        rs2a = e(nc.sbuf_tensor("rs2a", [128, NCH], F32))
        b2a = e(nc.sbuf_tensor("b2a", [128, NCH], F32))
        amr = e(nc.sbuf_tensor("amr", [128, 1], F32))
        amt = e(nc.sbuf_tensor("amt", [128, 1], F32))
        gmx = e(nc.sbuf_tensor("gmx", [1, 8], F32))
        sca = e(nc.sbuf_tensor("sca", [1, 4], F32))
        ivpp = e(nc.sbuf_tensor("ivpp", [128, 1], F32))
        sapp = e(nc.sbuf_tensor("sapp", [128, 1], F32))
        kcs = e(nc.sbuf_tensor("kcs", [128, 8], BF16))
        hsc = e(nc.sbuf_tensor("hsc", [1, 8], F32))
        csab = e(nc.sbuf_tensor("csab", [128, 8], F32))
        cnta = e(nc.sbuf_tensor("cnta", [128, 8], F32))
        suma = e(nc.sbuf_tensor("suma", [128, 8], F32))
        tots = e(nc.sbuf_tensor("tots", [1, 8], F32))
        dlpp = e(nc.sbuf_tensor("dlpp", [128, 1], F32))

        wT = wq
        tbf = wk
        rT = [wv[:, 0:1024], wv[:, 1024:2048]]
        rbf = [wv[:, 2048:3072], wv[:, 3072:4096]]
        sgnb = wv[:, 4096:5120]
        wfs = wo[:, 0:2048].bitcast(F32)
        absb = wo[:, 2048:4096].bitcast(F32)
        mskb = wo[:, 4096:6144].bitcast(F32)
        rpre = [qT[0][:, :].bitcast(F32), qT[1][:, :].bitcast(F32)]
        yt = [ln1T[0][:, :].bitcast(F32), ln1T[1][:, :].bitcast(F32)]
        ysb = [vsb[0][:, :].bitcast(F32), vsb[1][:, :].bitcast(F32)]

        Pm = e(nc.psum_tensor("Pm", [128, 1024], BF16))
        Pq = e(nc.psum_tensor("Pq", [128, 512], F32))
        Pk = Pq
        Pv1 = e(nc.psum_tensor("Pv1", [128, 512], F32))
        Pv = [Pv1, Pv1]
        Pat = e(nc.psum_tensor("Pat", [128, 512], F32))
        Psc = [e(nc.psum_tensor(f"Psc{i}", [128, 512], F32)) for i in range(4)]

        sems = {}
        for sname in ("dL0", "dL1", "dL2", "dL3", "dL4", "dS0", "dS1", "dS2", "dGwq", "dGwk", "dGwv", "dGwo", "dGr", "dR2", "dW0", "dW1", "dGv0", "dGv1", "dGm", "v", "s", "t", "g", "c", "dbg"):
            sems[sname] = e(nc.semaphore(sname))
        block = e(nc.Block())

        g1 = p.step("gpsimd", lambda g: g.memset(idn[:, :], 0.0), inc="g")
        p.step("gpsimd", lambda g: g.affine_select(
            out=idn[:, :], in_=idn[:, :], compare_op=ALU.not_equal, fill=1.0,
            base=0, pattern=[[-1, 128]], channel_multiplier=1),
            deps=[("g", g1)], inc="g")
        g2 = p.step("gpsimd", lambda g: g.memset(idnf[:, :], 0.0), inc="g")
        p.step("gpsimd", lambda g: g.affine_select(
            out=idnf[:, :], in_=idnf[:, :], compare_op=ALU.not_equal, fill=1.0,
            base=0, pattern=[[-1, 128]], channel_multiplier=1),
            deps=[("g", g2)], inc="g")
        p.step("gpsimd", lambda g: g.memset(onesb[:, :], 1.0), inc="g")
        p.step("gpsimd", lambda g: g.memset(onesf[:, :], 1.0), inc="g")
        g_const = p.step("gpsimd", lambda g: g.memset(onecf[:, :], 1.0), inc="g")
        g_eps = p.step("gpsimd", lambda g: g.memset(epsc[:, :], EPS), inc="g")

        for dst, src in [(gam_r, gam_h), (bet_r, bet_h), (dec_r, dec_h),
                         (cm_sb, cm_h), (xh_sb, xh_h)]:
            p.step("sync", (lambda d=dst, s_=src: (lambda s2:
                s2.dma_start(out=d[0:1, :], in_=s_)))(), inc="dR2", inc_by=16)
        mk_small = ("dR2", p.cnt["dR2"])
        for dst, src in [(bqb, bq_h), (bkb, bk_h), (bvb, bv_h), (bob, bo_h)]:
            p.step("gpsimd", (lambda d=dst, s_=src: (lambda g:
                g.dma_start(out=d[0:1, :], in_=s_)))(), inc="dGr", inc_by=16)
        mk_bias = ("dGr", p.cnt["dGr"])
        for lane, w_h2, w_sb in (("dGwq", wq_h, wq), ("dGwk", wk_h, wk),
                                 ("dGwv", wv_h, wv), ("dGwo", wo_h, wo)):
            for kb in range(KB):
                p.step("gpsimd", (lambda wh=w_h2, ws=w_sb, kb=kb: (lambda g:
                    g.dma_start(out=ws[:, kb * D:(kb + 1) * D],
                                in_=wh[kb * 128:(kb + 1) * 128, :])))(),
                    inc=lane, inc_by=16)
        mk_wq = ("dGwq", p.cnt["dGwq"])
        mk_wk = ("dGwk", p.cnt["dGwk"])
        mk_wv = ("dGwv", p.cnt["dGwv"])
        mk_wo = ("dGwo", p.cnt["dGwo"])

        dL_x = {}
        for c in range(5):
            lane = "dL%d" % c
            dL_x[c] = (lane, p.step("sync", (lambda c=c: (lambda s_:
                s_.dma_start(out=xin[c][:, :], in_=xs_h[c * 128:(c + 1) * 128, :])))(),
                inc=lane, inc_by=16))

        t_cm = p.step("tensor", lambda t: t.matmul(
            Pq[:, 0:1], onesf[0:1, :], cm_sb[0:1, 0:1], start=True, stop=True),
            deps=[mk_small, ("g", g_const)], inc="t")
        v_cm = p.step("vector", lambda v: v.tensor_copy(cmpp[:, 0:1], Pq[:, 0:1]),
                      deps=[("t", t_cm)], inc="v")
        v_prev = None
        for dst, srcrow in ((gamb, gam_r), (betb, bet_r)):
            for half in range(2):
                dep = [("v", v_prev)] if v_prev else [mk_small]
                t_b = p.step("tensor", (lambda h=half, r=srcrow: (lambda t: t.matmul(
                    Pv[0][:, 0:512], onesf[0:1, :], r[0:1, h * 512:(h + 1) * 512],
                    start=True, stop=True)))(), deps=dep, inc="t")
                v_prev = p.step("vector", (lambda h=half, d=dst: (lambda v: v.tensor_copy(
                    d[:, h * 512:(h + 1) * 512], Pv[0][:, :])))(),
                    deps=[("t", t_b)], inc="v")
        v_bb2 = v_prev
        t_de = p.step("tensor", lambda t: t.matmul(
            Pk[:, 0:8], onesf[0:1, :],
            dec_r[0:1, :].rearrange("p (a b) -> p a b", b=2)[:, :, 0],
            start=True, stop=True), deps=[mk_small], inc="t")
        t_do = p.step("tensor", lambda t: t.matmul(
            Pk[:, 8:16], onesf[0:1, :],
            dec_r[0:1, :].rearrange("p (a b) -> p a b", b=2)[:, :, 1],
            start=True, stop=True), inc="t")
        p.step("vector", lambda v: v.tensor_copy(dpp[0:64, 0:16:2], Pk[0:64, 0:8]),
               deps=[("t", t_do)], inc="v")
        p.step("vector", lambda v: v.tensor_copy(dpp[64:128, 0:16:2], Pk[64:128, 8:16]),
               inc="v")
        p.step("vector", lambda v: v.tensor_copy(dpp[0:64, 1:16:2], Pk[0:64, 0:8]),
               inc="v")
        v_d4 = p.step("vector", lambda v: v.tensor_copy(dpp[64:128, 1:16:2], Pk[64:128, 8:16]),
                      inc="v")
        p.step("vector", lambda v: v.memset(
            dect[:, :].rearrange("p (a b) -> p a b", b=128)[:, :, 0], 0.125))

        dview = dect[:, :].rearrange("p (a b) -> p a b", b=128)
        for tp in range(1, 8):
            p.step("vector", (lambda tp=tp: (lambda v: v.tensor_tensor(
                dview[:, :, tp], dview[:, :, tp - 1], dpp[:, :], op=ALU.mult)))())
        for _ in range(3):
            p.step("vector", lambda v: v.tensor_tensor(
                dpp[:, :], dpp[:, :], dpp[:, :], op=ALU.mult))
        r8 = d8rep[:, :].rearrange("p (a k) -> p a k", k=8)
        p.step("vector", lambda v: v.tensor_copy(r8[:, :, 0:1], dpp[:, :].rearrange(
            "p (a k) -> p a k", k=1)))
        p.step("vector", lambda v: v.tensor_copy(r8[:, :, 1:2], r8[:, :, 0:1]))
        p.step("vector", lambda v: v.tensor_copy(r8[:, :, 2:4], r8[:, :, 0:2]))
        p.step("vector", lambda v: v.tensor_copy(r8[:, :, 4:8], r8[:, :, 0:4]))
        dblk = dect[:, :].rearrange("p (a c b) -> p a c b", a=16, c=16)
        v_dect = None
        for m in range(1, 16):
            v_dect = p.step("vector", (lambda m=m: (lambda v: v.tensor_tensor(
                dblk[:, :, m, :], dblk[:, :, m - 1, :], r8[:, :, :],
                op=ALU.mult)))(), inc="v" if m == 15 else None)

        t_bt = None
        for i, brow in enumerate((bqb, bkb)):
            for kb in range(KB):
                dep = [mk_bias, ("g", g_const)] if (i == 0 and kb == 0) else ()
                t_bt = p.step("tensor", (lambda brow=brow, i=i, kb=kb: (lambda t: t.transpose(
                    Pm[:, 32 + 2 * (KB * i + kb):32 + 2 * (KB * i + kb) + 1],
                    brow[0:1, kb * 128:(kb + 1) * 128],
                    onesb[0:1, 0:1])))(),
                    deps=dep, inc="t" if (i == 1 and kb == KB - 1) else None)
        p.step("vector", lambda v: v.tensor_copy(bqT[:, :], Pm[:, 32:32 + 2 * KB:2]),
               deps=[("t", t_bt)], inc="v")
        v_bt2 = p.step("vector", lambda v: v.tensor_copy(
            bkT[:, :], Pm[:, 32 + 2 * KB:32 + 4 * KB:2]), inc="v")
        free = {
            "Pm": [("v", v_bt2)],
            "Pq": [("v", v_d4)],
            "Pv": [("v", v_bb2)],
            "Pat": [],
        }
        ln1T_free = [[], []]
        qT_free = [[], []]
        kT_free = [[], []]
        vsb_free = [[], []]
        vcar_free = [[], []]
        xin_free = {i: [] for i in range(5)}
        psb_free = []
        pTs_free = []
        ato_free_box = [[]]
        atoT_free = []
        x2st_free = [[], [], []]
        xh2_free = [[]]
        ln1bf_free = [[], []]

        s_ln1 = {}
        v_lnT = {}
        s_kev = {}
        qk_t = {}
        psc_free = [[], []]
        xng_free = [[]]
        v_kcar = {}
        s_vev = {}
        dG_vcar = {}
        s_aev = {}
        v_resid = {}
        dS_x2 = {}
        g_xn = {}
        v_b2 = {}

        def _halo_late():
            p.step("vector", lambda v: v.reduce_sum(hsc[0:1, 0:1], xh_sb[0:1, :], axis=AX),
                   deps=[mk_small], inc="v")
            _ = 0
            p.step("vector", lambda v: v.tensor_scalar(hsc[0:1, 1:2], hsc[0:1, 0:1],
                                                       -1.0 / D, None, op0=ALU.mult))
            s_h1 = p.step("scalar", lambda s_: s_.activation(
                sqs[0:1, :], xh_sb[0:1, :], AF.Square),
                deps=[mk_small, sqs_free_box[0]], inc="s")
            v_h2 = p.step("vector", lambda v: v.reduce_sum(hsc[0:1, 2:3], sqs[0:1, :], axis=AX),
                          deps=[("s", s_h1)], inc="v")
            p.step("vector", lambda v: v.tensor_scalar(hsc[0:1, 3:4], hsc[0:1, 1:2],
                                                       hsc[0:1, 1:2], None, op0=ALU.mult))
            p.step("vector", lambda v: v.tensor_scalar(hsc[0:1, 4:5], hsc[0:1, 2:3],
                                                       1.0 / D, EPS, op0=ALU.mult, op1=ALU.add))
            p.step("vector", lambda v: v.tensor_tensor(hsc[0:1, 5:6], hsc[0:1, 4:5],
                                                       hsc[0:1, 3:4], op=ALU.subtract))
            v_h3 = p.step("vector", lambda v: v.tensor_copy(hsc[0:1, 6:7], hsc[0:1, 5:6]),
                          inc="v")
            s_hl = p.step("scalar", lambda s_: s_.activation(
                hsc[0:1, 6:7], hsc[0:1, 5:6], AF.Ln), deps=[("v", v_h3)], inc="s")
            s_h2 = p.step("scalar", lambda s_: s_.activation(
                hsc[0:1, 7:8], hsc[0:1, 6:7], AF.Exp, scale=-0.5),
                deps=[("s", s_hl)], inc="s")
            v_h4 = p.step("vector", lambda v: v.tensor_scalar(
                hsc[0:1, 3:4], hsc[0:1, 1:2], hsc[0:1, 7:8], None, op0=ALU.mult),
                deps=[("s", s_h2)], inc="v")
            s_h3 = p.step("scalar", lambda s_: s_.activation(
                lnh_bf[0:1, :], xh_sb[0:1, :], AF.Identity,
                bias=hsc[0:1, 3:4], scale=hsc[0:1, 7:8]), deps=[("v", v_h4)], inc="s")
            t_lnhT = None
            for kb in range(KB):
                dep = [("s", s_h3)] + free["Pm"] if kb == 0 else ()
                t_lnhT = p.step("tensor", (lambda kb=kb: (lambda t: t.transpose(
                    Pm[:, 2 * kb:2 * kb + 1], lnh_bf[0:1, kb * 128:(kb + 1) * 128],
                    onesb[0:1, 0:1])))(),
                    deps=dep, inc="t" if kb == KB - 1 else None)
            v_lnhT = p.step("vector", lambda v: v.tensor_copy(lnhT[:, :], Pm[:, 0:2 * KB:2]),
                            deps=[("t", t_lnhT)], inc="v")
            t_kh = None
            for mb in range(KB):
                for kb in range(KB):
                    dep = []
                    if mb == 0 and kb == 0:
                        dep = [("v", v_lnhT), mk_wk, mk_small, mk_bias, ("v", v_cm)] + free["Pq"]
                    p.step("tensor", (lambda mb=mb, kb=kb: (lambda t: t.matmul(
                        Pq[:, mb:mb + 1], wk[:, kb * D + mb * 128:kb * D + (mb + 1) * 128],
                        lnhT[:, kb:kb + 1], start=(kb == 0), stop=False)))(), deps=dep)
                t_kh = p.step("tensor", (lambda mb=mb: (lambda t: t.matmul(
                    Pq[:, mb:mb + 1], bkb[0:1, mb * 128:(mb + 1) * 128],
                    onesb[0:1, 0:1], start=False, stop=True)))(),
                    inc="t" if mb == KB - 1 else None)
            v_kh = p.step("vector", lambda v: v.tensor_scalar(
                kcs[:, :], Pq[:, 0:KB], cmpp[:, 0:1], None, op0=ALU.mult),
                deps=[("t", t_kh)], inc="v")
            v_kh2 = p.step("vector", lambda v: v.tensor_copy(
                kT[0][:, :].rearrange("p (a b) -> p a b", b=258)[:, :, 0], kcs[:, :]),
                deps=[("v", v_kh)], inc="v")
            t_vh = None
            for half in range(2):
                for kb in range(KB):
                    dep = ([("v", v_kh), mk_wv] + free["Pv"]) if (half == 0 and kb == 0) else ()
                    p.step("tensor", (lambda half=half, kb=kb: (lambda t: t.matmul(
                        Pv[half][0:1, :], lnhT[:, kb:kb + 1],
                        wv[:, kb * D + half * 512:kb * D + (half + 1) * 512],
                        start=(kb == 0), stop=False)))(), deps=dep)
                t_vh = p.step("tensor", (lambda half=half: (lambda t: t.matmul(
                    Pv[half][0:1, :], onesb[0:1, 0:1],
                    bvb[0:1, half * 512:(half + 1) * 512], start=False, stop=True)))(),
                    inc="t" if half == 1 else None)
            v_vh = None
            for half in range(2):
                v_vh = p.step("vector", (lambda half=half: (lambda v: v.tensor_scalar(
                    vcar[0][0:1, half * 512:(half + 1) * 512], Pv[half][0:1, :],
                    cm_sb[0:1, 0:1], None, op0=ALU.mult)))(), deps=[("t", t_vh)], inc="v")
            p.step("vector", lambda v: v.memset(amr[:, :], 0.0))


            free["Pm"] = [("v", v_lnhT)]
            free["Pq"] = [("v", v_kh)]
            free["Pv"] = [("v", v_vh)]
            kT_free[0] = [("v", v_kh2)]
            dG_vcar[0] = [("v", v_vh)]
            sqs_free_box[0] = ("v", v_h2)

        if upto == "pro":
            return _emit(p, block, sems, nc)
        sqs_free_box = [mk_small]
        ln1v_free = [[], []]

        def ln1_stats(cc):
            cp = cc % 2
            slot = cc % 5
            o = cp * 8
            depx = [dL_x[cc]]
            v0 = p.step("vector", (lambda slot=slot, o=o: (lambda v: v.reduce_sum(
                st1[:, o:o + 1], xin[slot][:, :], axis=AX)))(),
                deps=depx + ln1v_free[cp], inc="v")
            ln1v_free[cp] = []
            s1 = p.step("scalar", (lambda slot=slot, o=o: (lambda s_: s_.activation(
                sqs[:, :], xin[slot][:, :], AF.Square,
                accum_out=st1[:, o + 2:o + 3])))(),
                deps=depx + [sqs_free_box[0]], inc="s")
            sqs_free_box[0] = ("s", s1)
            sm1 = p.step("scalar", (lambda o=o: (lambda s_: s_.activation(
                st1[:, o + 1:o + 2], st1[:, o:o + 1], AF.Identity,
                scale=-1.0 / D)))(), deps=[("v", v0)], inc="s")
            ln1v_free[cp] = [("s", sm1)]
            p.step("scalar", (lambda o=o: (lambda s_: s_.activation(
                st1[:, o + 3:o + 4], st1[:, o + 1:o + 2], AF.Square)))())
            p.step("scalar", (lambda o=o: (lambda s_: s_.activation(
                st1[:, o + 4:o + 5], st1[:, o + 2:o + 3], AF.Identity,
                scale=1.0 / D, bias=epsc[:, 0:1])))(), deps=[("g", g_eps)])
            p.step("scalar", (lambda o=o: (lambda s_: s_.activation(
                st1[:, o + 5:o + 6], st1[:, o + 3:o + 4], AF.Identity,
                scale=-1.0, bias=st1[:, o + 4:o + 5])))())
            p.step("scalar", (lambda o=o: (lambda s_: s_.activation(
                st1[:, o + 6:o + 7], st1[:, o + 5:o + 6], AF.Ln)))())
            p.step("scalar", (lambda o=o: (lambda s_: s_.activation(
                st1[:, o + 7:o + 8], st1[:, o + 6:o + 7], AF.Exp, scale=-0.5)))())
            p.step("scalar", (lambda cp=cp, o=o: (lambda s_: s_.activation(
                st1[:, 16 + cp:17 + cp], st1[:, o + 1:o + 2], AF.Identity,
                scale=st1[:, o + 7:o + 8])))())
            s3 = p.step("scalar", (lambda slot=slot, cp=cp, o=o: (lambda s_: s_.activation(
                ln1bf[cp][:, :], xin[slot][:, :], AF.Identity,
                bias=st1[:, 16 + cp:17 + cp], scale=st1[:, o + 7:o + 8])))(),
                deps=ln1bf_free[cp], inc="s")
            s_ln1[cc] = s3
            if dbg_dump and cc == 0:
                p.step("sync", lambda s_: s_.dma_start(out=dmp["d_ln1bf"], in_=ln1bf[0][:, :]),
                       deps=[("s", s3)], inc="dbg", inc_by=16)
                p.step("sync", lambda s_: s_.dma_start(out=dmp["d_xin0"], in_=xin[0][:, :]),
                       deps=[("s", s3)], inc="dbg", inc_by=16)
                p.step("sync", lambda s_: s_.dma_start(out=dmp["d_st1"], in_=st1[:, 0:8]),
                       deps=[("s", s3)], inc="dbg", inc_by=16)
                p.step("sync", lambda s_: s_.dma_start(out=dmp["d_vc"], in_=vcar[0][0:1, :]),
                       deps=[("s", s3)], inc="dbg", inc_by=16)

        def ln1_trans(cc):
            cp = cc % 2
            gpx = (cc // 2) % 2
            cgx = cc % 2
            s3 = s_ln1[cc]
            t_lnT = None
            for kb in range(KB):
                dep = [("s", s3)] + free["Pm"] if kb == 0 else ()
                t_lnT = p.step("tensor", (lambda kb=kb, cp=cp: (lambda t: t.transpose(
                    Pm[:, kb * 128:(kb + 1) * 128], ln1bf[cp][:, kb * 128:(kb + 1) * 128],
                    idn[:, :])))(), deps=dep, inc="t" if kb == KB - 1 else None)

            def lnT_ev(v, gp=gpx, cg=cgx):
                dst = ln1T[gp][:, :].rearrange("p (a b c) -> p a b c", a=KB, b=2)[:, :, cg, :]
                src = Pm[:, 0:1024].rearrange("p (a b) -> p a b", b=128)
                return v.tensor_copy(dst, src)
            dep = [("t", t_lnT)]
            if cgx == 0:
                dep += ln1T_free[gpx]
            vv = p.step("vector", lnT_ev, deps=dep, inc="v")
            v_lnT[cc] = vv
            free["Pm"] = [("v", vv)]
            ln1bf_free[cp] = [("v", vv)]

        def emit_ln1(cc):
            ln1_stats(cc)
            ln1_trans(cc)


        def qk_mm(gq, mb):
            gpq = gq % 2

            def run():
                chq = 2 * gq + 1
                for kb in range(KB):
                    dep = []
                    if kb == 0:
                        dep = [("v", v_lnT[chq]), ("v", v_lnT[chq - 1])] + free["Pq"]
                        if gq == 0 and mb == 0:
                            dep += [mk_wq, mk_wk]
                    p.step("tensor", (lambda mb=mb, kb=kb, gpq=gpq: (lambda t: t.matmul(
                        Pq[:, 0:256], wq[:, kb * D + mb * 128:kb * D + (mb + 1) * 128],
                        ln1T[gpq][:, kb * 256:(kb + 1) * 256],
                        start=(kb == 0), stop=(kb == KB - 1))))(), deps=dep)
                tk = None
                for kb in range(KB):
                    tk = p.step("tensor", (lambda mb=mb, kb=kb, gpq=gpq: (lambda t: t.matmul(
                        Pq[:, 256:512], wk[:, kb * D + mb * 128:kb * D + (mb + 1) * 128],
                        ln1T[gpq][:, kb * 256:(kb + 1) * 256],
                        start=(kb == 0), stop=(kb == KB - 1))))(),
                        inc="t" if kb == KB - 1 else None)
                qk_t[(gq, mb)] = tk
            return run

        def qk_ev(gq, mb):
            gpq = gq % 2

            def run():
                dep = [("t", qk_t[(gq, mb)])]
                if mb == 0:
                    dep += [("v", v_dect)] + qT_free[gpq]
                vq = p.step("vector", (lambda mb=mb, gpq=gpq: (lambda v: v.scalar_tensor_tensor(
                    qT[gpq][:, mb * 256:(mb + 1) * 256], Pq[:, 0:256],
                    bqT[:, mb:mb + 1],
                    dect[:, mb * 256:(mb + 1) * 256],
                    op0=ALU.add, op1=ALU.mult)))(),
                    deps=dep, inc="v")
                dep = []
                if mb == 0:
                    dep += kT_free[gpq]

                def kev(v_, mb=mb, gpq=gpq):
                    dst = kT[gpq][:, :].rearrange(
                        "p (m c k) -> p m c k", m=KB, c=2)[:, mb, :, 1:129]
                    return v_.tensor_scalar(dst, Pq[:, 256:512].rearrange(
                        "p (a b) -> p a b", b=128), bkT[:, mb:mb + 1],
                        None, op0=ALU.add)
                sk = p.step("vector", kev, deps=dep, inc="v")
                free["Pq"] = [("v", sk)]
                s_kev[(gq, mb)] = ("v", sk)
            return run

        def kcar_bundle(gq):
            gpq = gq % 2

            def run():
                kdep = [s_kev[(gq, KB - 1)]]

                def kcar0(v, gpq=gpq):
                    src = kT[1 - gpq][:, :].rearrange("p (a b) -> p a b", b=258)[:, :, 257]
                    dst = kT[gpq][:, :].rearrange("p (a b) -> p a b", b=258)[:, :, 0]
                    return v.tensor_copy(dst, src)

                def kcar1(v, gpq=gpq):
                    src = kT[gpq][:, :].rearrange("p (a b) -> p a b", b=258)[:, :, 128]
                    dst = kT[gpq][:, :].rearrange("p (a b) -> p a b", b=258)[:, :, 129]
                    return v.tensor_copy(dst, src)
                if gq > 0:
                    p.step("vector", kcar0, deps=kdep, inc="v")
                v_kcar[gq] = p.step("vector", kcar1, deps=kdep, inc="v")
            return run

        def v_bundle(cv, half):
            gpv = (cv // 2) % 2
            cgv = cv % 2

            def run():
                Pb = Pv1 if half == 0 else Pq
                bank = "Pv" if half == 0 else "Pq"
                for kb in range(KB):
                    dep = []
                    if kb == 0:
                        dep = list(free[bank])
                        if cgv == 0 and half == 0:
                            dep += vsb_free[gpv]
                        if cv <= 1 and half == 0:
                            dep += [mk_wv]
                    p.step("tensor", (lambda Pb=Pb, kb=kb, gpv=gpv, cgv=cgv, half=half:
                        (lambda t: t.matmul(
                            Pb[:, 0:512],
                            ln1T[gpv][:, kb * 256 + cgv * 128:kb * 256 + (cgv + 1) * 128],
                            wv[:, kb * D + half * 512:kb * D + (half + 1) * 512],
                            start=(kb == 0), stop=False)))(), deps=dep)
                tv = p.step("tensor", (lambda Pb=Pb, half=half: (lambda t: t.matmul(
                    Pb[:, 0:512], onesb[0:1, 0:128],
                    bvb[0:1, half * 512:(half + 1) * 512],
                    start=False, stop=True)))(), inc="t")
                sv_ = p.step("scalar", (lambda Pb=Pb, half=half, gpv=gpv, cgv=cgv:
                    (lambda s_: s_.activation(
                        vsb[gpv][:, cgv * 1024 + half * 512:cgv * 1024 + (half + 1) * 512],
                        Pb[:, 0:512], AF.Identity)))(),
                    deps=[("t", tv)], inc="s")
                free[bank] = [("s", sv_)]
                if half == 1:
                    s_vev[cv] = sv_
            return run

        ln2m_box = [[]]

        def ln2_bundle(ca):
            cpa = ca % 2
            cp3 = ca % 3
            box = {}

            def run1():
                vres = v_resid[ca]
                _ = cpa
                v0b = p.step("vector", (lambda cp3=cp3: (lambda v: v.reduce_sum(
                    st2[:, 0:1], x2st[cp3][:, :], axis=AX)))(),
                    deps=[("v", vres)] + ln2m_box[0], inc="v")
                s1b = p.step("scalar", (lambda cp3=cp3, ca=ca: (lambda s_: s_.activation(
                    sqs[:, :], x2st[cp3][:, :], AF.Square,
                    accum_out=st2[:, 2:3])))(),
                    deps=[("v", vres), sqs_free_box[0]], inc="s")
                sqs_free_box[0] = ("s", s1b)
                sm1b = p.step("scalar", (lambda ca=ca: (lambda s_: s_.activation(
                    nm2a[:, ca:ca + 1], st2[:, 0:1], AF.Identity,
                    scale=-1.0 / D)))(), deps=[("v", v0b)], inc="s")
                ln2m_box[0] = [("s", sm1b)]
                p.step("scalar", (lambda ca=ca: (lambda s_: s_.activation(
                    st2[:, 3:4], nm2a[:, ca:ca + 1], AF.Square)))())
                p.step("scalar", lambda s_: s_.activation(
                    st2[:, 4:5], st2[:, 2:3], AF.Identity,
                    scale=1.0 / D, bias=epsc[:, 0:1]), deps=[("g", g_eps)])
                p.step("scalar", lambda s_: s_.activation(
                    st2[:, 5:6], st2[:, 3:4], AF.Identity,
                    scale=-1.0, bias=st2[:, 4:5]))
                p.step("scalar", (lambda ca=ca: (lambda s_: s_.activation(
                    st2[:, 6:7], st2[:, 5:6], AF.Ln)))())
                s2b = p.step("scalar", (lambda ca=ca: (lambda s_: s_.activation(
                    rs2a[:, ca:ca + 1], st2[:, 6:7], AF.Exp, scale=-0.5)))(), inc="s")
                s4b = p.step("scalar", (lambda ca=ca: (lambda s_: s_.activation(
                    b2a[:, ca:ca + 1], nm2a[:, ca:ca + 1], AF.Identity,
                    scale=rs2a[:, ca:ca + 1])))(), inc="s")
                _ = s2b
                v_b2[ca] = ("s", s4b)
                box["s1b"] = s1b
                box["s4b"] = s4b

            def run2a():
                s1b = box["s1b"]
                s3b = p.step("scalar", (lambda ca=ca, cpa=cpa, cp3=cp3: (lambda s_: s_.activation(
                    xh2[0][:, :], x2st[cp3][:, :], AF.Identity,
                    bias=b2a[:, ca:ca + 1], scale=rs2a[:, ca:ca + 1])))(),
                    deps=[("s", box["s4b"])] + xh2_free[0], inc="s")
                x2st_free[cp3] = [dS_x2[ca], ("s", s1b), ("s", s3b)]
                dep = [("s", s3b), ("v", v_bb2)] + xng_free[0]
                gx1 = p.step("gpsimd", (lambda cpa=cpa: (lambda g: g.tensor_tensor(
                    xng[:, :], xh2[0][:, :], gamb[:, :], op=ALU.mult)))(),
                    deps=dep, inc="g")
                gx = p.step("gpsimd", lambda g: g.tensor_tensor(
                    xng[:, :], xng[:, :], betb[:, :], op=ALU.add),
                    deps=[("g", gx1)], inc="g")
                xh2_free[0] = [("g", gx)]
                box["gx"] = gx

            def run2b():
                gx = box["gx"]
                p.step("vector", lambda v: v.reduce_max(
                    amt[:, 0:1], xng[:, :], axis=AX, apply_absolute_value=True),
                    deps=[("g", gx)])
                vam = p.step("vector", lambda v: v.tensor_tensor(
                    amr[:, 0:1], amr[:, 0:1], amt[:, 0:1], op=ALU.max), inc="v")
                g_xn[ca] = vam
                xng_free[0] = [("v", vam)]
            return [run1, run2a, run2b]

        emit_ln1(0)
        emit_ln1(1)
        emit_ln1(2)
        emit_ln1(3)
        for mb in range(KB):
            qk_mm(0, mb)()
            qk_ev(0, mb)()
        _halo_late()
        kcar_bundle(0)()
        for cv in (0, 1):
            for half in (0, 1):
                v_bundle(cv, half)()

        def mk_chunk(ca):
            g2 = ca // 2
            gp = g2 % 2
            cga = ca % 2
            cpa = ca % 2
            s_exp = {}
            t_sc = {}
            vn_d = {}
            t_pT = {}
            vp2_d = {}
            sa_mid = {}
            sa_steps = {}

            def emit_S(j):
                if j == 0:
                    dep = [("v", v_kcar[g2]), ("s", s_vev[ca])] + psc_free[0]
                elif j == 1:
                    dep = [("s", s_vev[ca])] + psc_free[1]
                else:
                    dep = [("s", s_exp[j - 2])]
                bk0 = 2 * (j % 2)
                for h2 in range(2):
                    t_sc[j] = p.step("tensor", (lambda j=j, h2=h2, gp=gp, cga=cga, bk0=bk0:
                        (lambda t: t.matmul(
                            Psc[bk0 + h2][:, 0:129],
                            qT[gp][h2 * 64:(h2 + 1) * 64,
                                   j * 256 + cga * 128:j * 256 + (cga + 1) * 128],
                            kT[gp][h2 * 64:(h2 + 1) * 64,
                                   j * 258 + cga * 129:j * 258 + (cga + 1) * 129],
                            start=True, stop=True,
                            tile_position=(h2 * 64, 0))))(),
                        deps=dep if h2 == 0 else (),
                        inc="t" if h2 == 1 else None)

            def emit_X(j):
                bk0 = 2 * (j % 2)
                p.step("vector", (lambda j=j, cpa=cpa, bk0=bk0: (lambda v: v.reduce_max(
                    nmx[cpa][:, 2 * j:2 * j + 1],
                    Psc[bk0][:, 0:129], axis=AX, negate=True)))(),
                    deps=[("t", t_sc[j])], inc="v")
                vn_d[j] = p.step("vector", (lambda j=j, cpa=cpa, bk0=bk0:
                    (lambda v: v.reduce_max(
                        nmx[cpa][:, 2 * j + 1:2 * j + 2],
                        Psc[bk0 + 1][:, 0:129], axis=AX, negate=True)))(), inc="v")
                for h2 in range(2):
                    h = 2 * j + h2
                    dep2 = [("v", vn_d[j])]
                    if j == 0 and h2 == 0:
                        dep2 += psb_free
                    bk0 = 2 * (j % 2)
                    s_exp[j] = p.step("scalar", (lambda h=h, h2=h2, cpa=cpa, bk0=bk0:
                        (lambda s_: s_.activation(
                            psb[:, h * 129:(h + 1) * 129],
                            Psc[bk0 + h2][:, 0:129], AF.Exp,
                            bias=nmx[cpa][:, h:h + 1],
                            accum_out=dnm[cpa][:, h:h + 1])))(),
                        deps=dep2, inc="s")

            def emit_T(j):
                for h2 in range(2):
                    h = 2 * j + h2
                    dep2 = [("s", s_exp[j])] + free["Pm"] if h2 == 0 else ()
                    p.step("tensor", (lambda h=h, h2=h2: (lambda t: t.transpose(
                        Pm[:, h2 * 128:(h2 + 1) * 128],
                        psb[:, h * 129 + 1:(h + 1) * 129], idn[:, :])))(),
                        deps=dep2)
                    t_pT[j] = p.step("tensor", (lambda h=h, h2=h2: (lambda t: t.transpose(
                        Pm[0:1, 256 + h2 * 128:256 + (h2 + 1) * 128],
                        psb[:, h * 129:h * 129 + 1], idn[:, :])))(),
                        inc="t" if h2 == 1 else None)

            def emit_E(j):
                dep2 = [("t", t_pT[j])]
                if j == 0:
                    dep2 += pTs_free
                p.step("vector", (lambda j=j: (lambda v: v.tensor_copy(
                    pTs[:, j * 256:(j + 1) * 256], Pm[:, 0:256])))(),
                    deps=dep2, inc="v")
                vp2_d[j] = p.step("vector", (lambda j=j: (lambda v: v.tensor_copy(
                    pcT[0:1, j * 256:(j + 1) * 256], Pm[0:1, 256:512])))(),
                    inc="v")
                free["Pm"] = [("v", vp2_d[j])]

            def emit_A(j):
                for h2 in range(2):
                    h = 2 * j + h2
                    col = (h % 8) * 64
                    dep2 = [("v", vp2_d[j])]
                    if j == 0 and h2 == 0:
                        dep2 += free["Pat"] + dG_vcar[ca]
                    if j == 4 and h2 == 0:
                        dep2 += [("s", sa_mid[0])]
                    p.step("tensor", (lambda h=h, col=col, gp=gp, cga=cga:
                        (lambda t: t.matmul(
                            Pat[:, col:col + 64],
                            pTs[:, h * 128:(h + 1) * 128],
                            vsb[gp][:, cga * 1024 + h * 64:cga * 1024 + (h + 1) * 64],
                            start=True, stop=False)))(), deps=dep2)
                    p.step("tensor", (lambda h=h, col=col, ca=ca:
                        (lambda t: t.matmul(
                            Pat[:, col:col + 64],
                            pcT[0:1, h * 128:(h + 1) * 128],
                            vcar[ca % 2][0:1, h * 64:(h + 1) * 64],
                            start=False, stop=True)))(),
                        inc="t" if h2 == 1 else None)

            def emit_half_evac(hh, t_mark=None):
                t_now = p.cnt["t"] if t_mark is None else t_mark
                vr = p.step("vector", (lambda cpa=cpa, hh=hh: (lambda v: v.reciprocal(
                    rcp[cpa][:, 8 * hh:8 * hh + 8],
                    dnm[cpa][:, 8 * hh:8 * hh + 8])))(),
                    deps=[("s", s_exp[4 * hh + 3])], inc="v")
                sa = None
                for i in range(8):
                    h = 8 * hh + i
                    dep = [("v", vr)]
                    if i == 0:
                        dep += [("t", t_now)]
                        if hh == 0:
                            dep += ato_free_box[0]
                    sa = p.step("scalar", (lambda h=h, i=i, cpa=cpa:
                        (lambda s_: s_.activation(
                            ato[:, h * 64:(h + 1) * 64],
                            Pat[:, i * 64:(i + 1) * 64], AF.Identity,
                            scale=rcp[cpa][:, h:h + 1])))(), deps=dep, inc="s")
                    sa_steps[(hh, i)] = sa
                sa_mid[hh] = sa

            return dict(S=emit_S, X=emit_X, T=emit_T, E=emit_E, A=emit_A,
                        H=emit_half_evac, s_exp=s_exp, sa_mid=sa_mid,
                        sa_steps=sa_steps)

        ctx_box = [None]
        late2 = []

        for g_ in range(NCH // 2):
            gp = g_ % 2
            nxt = []
            if 2 * g_ + 3 < NCH:
                for mb in range(KB):
                    nxt.append(qk_mm(g_ + 1, mb))
                    nxt.append(qk_ev(g_ + 1, mb))
                nxt.append(kcar_bundle(g_ + 1))

            def drip(n=1, late_ok=False):
                for _ in range(n):
                    if nxt:
                        nxt.pop(0)()

            for cga in range(2):
                ca = g_ * 2 + cga
                cpa = ca % 2
                if ctx_box[0] is not None:
                    C = ctx_box[0]
                else:
                    C = mk_chunk(ca)
                    C["S"](0)
                    C["X"](0)
                    C["S"](1)
                    C["X"](1)
                emit_T, emit_E, emit_S, emit_X, emit_A = (
                    C["T"], C["E"], C["S"], C["X"], C["A"])
                emit_half_evac = C["H"]
                s_exp = C["s_exp"]
                sa_mid = C["sa_mid"]

                for j in range(8):
                    if j != 0 or not C.get("TE0"):
                        emit_T(j)
                        emit_E(j)
                    if j + 2 < 8:
                        emit_S(j + 2)
                        emit_X(j + 2)
                    drip(1)
                    emit_A(j)
                    if j == 3:
                        emit_half_evac(0)
                    drip(1, late_ok=(j >= 3))
                    if j in (0, 1, 5) and late2:
                        late2.pop(0)()
                t_attn_done = p.cnt["t"]
                psb_free = [("t", t_attn_done)]
                pTs_free = [("t", t_attn_done)]
                vcar_free[ca % 2] = [("t", t_attn_done)]
                psc_free = [[("s", s_exp[6])], [("s", s_exp[7])]]
                if ca + 1 < NCH:
                    Cn = mk_chunk(ca + 1)
                    Cn["S"](0)
                    ctx_box[0] = Cn
                else:
                    ctx_box[0] = None
                t_aT0 = None
                for kb in range(4):
                    dep = [("s", sa_mid[0])] + free["Pm"] if kb == 0 else ()
                    t_aT0 = p.step("tensor", (lambda kb=kb: (lambda t: t.transpose(
                        Pm[:, 512 + kb * 128:512 + (kb + 1) * 128],
                        ato[:, kb * 128:(kb + 1) * 128], idn[:, :])))(),
                        deps=dep, inc="t" if kb == 3 else None)
                vaT0 = p.step("vector", lambda v: v.tensor_copy(
                    atoT[:, 0:512], Pm[:, 512:1024]),
                    deps=[("t", t_aT0)] + atoT_free, inc="v")
                free["Pm"] = free["Pm"] + [("v", vaT0)]
                if ctx_box[0] is not None:
                    ctx_box[0]["S"](1)
                wo_dep0 = [("v", vaT0)] + free["Pv"]
                if g_ == 0 and cga == 0:
                    wo_dep0 += [mk_wo]
                for kb in range(4):
                    p.step("tensor", (lambda kb=kb: (lambda t: t.matmul(
                        Pv[0][:, :], atoT[:, kb * 128:(kb + 1) * 128],
                        wo[:, kb * D:kb * D + 512],
                        start=(kb == 0), stop=False)))(),
                        deps=wo_dep0 if kb == 0 else ())
                emit_half_evac(1, t_mark=t_attn_done)
                sa = sa_mid[1]
                s_aev[ca] = sa
                free["Pat"] = [("s", sa)]
                ato_free_box[0] = []
                if ca + 1 < NCH:
                    lane = "dGv%d" % ((ca + 1) % 2)
                    dgv = p.step("gpsimd", (lambda gp=gp, cga=cga, ca=ca:
                        (lambda g: g.dma_start(
                            out=vcar[(ca + 1) % 2][0:1, :],
                            in_=vsb[gp][127:128, cga * 1024:(cga + 1) * 1024])))(),
                        deps=[("s", s_vev[ca])] + vcar_free[(ca + 1) % 2],
                        inc=lane, inc_by=16)
                    dG_vcar[ca + 1] = [(lane, dgv)]

                t_aT = None
                sa_steps = C["sa_steps"]
                for kb in range(4, KB):
                    dep = [("s", sa_steps[(1, 2 * (kb - 4) + 1)])]
                    if kb == 4:
                        dep += free["Pm"]
                    t_aT = p.step("tensor", (lambda kb=kb: (lambda t: t.transpose(
                        Pm[:, (kb - 4) * 128:(kb - 3) * 128],
                        ato[:, kb * 128:(kb + 1) * 128], idn[:, :])))(),
                        deps=dep, inc="t" if kb == KB - 1 else None)
                vaT = p.step("vector", lambda v: v.tensor_copy(
                    atoT[:, 512:1024], Pm[:, 0:512]),
                    deps=[("t", t_aT)], inc="v")
                free["Pm"] = [("v", vaT)]
                ato_free_box[0] = [("v", vaT0), ("v", vaT)]
                if ctx_box[0] is not None:
                    ctx_box[0]["X"](0)
                    ctx_box[0]["X"](1)

                vres = None
                for half in range(2):
                    Pb = Pv1 if half == 0 else Pat
                    kbs = range(4, KB) if half == 0 else range(KB)
                    for kb in kbs:
                        dep = []
                        if half == 0 and kb == 4:
                            dep = [("v", vaT)]
                        if half == 1 and kb == 0:
                            dep = [("s", sa)]
                        p.step("tensor", (lambda Pb=Pb, half=half, kb=kb: (lambda t: t.matmul(
                            Pb[:, 0:512], atoT[:, kb * 128:(kb + 1) * 128],
                            wo[:, kb * D + half * 512:kb * D + (half + 1) * 512],
                            start=(half == 1 and kb == 0), stop=False)))(), deps=dep)
                    tw = p.step("tensor", (lambda Pb=Pb, half=half: (lambda t: t.matmul(
                        Pb[:, 0:512], onesb[0:1, 0:128],
                        bob[0:1, half * 512:(half + 1) * 512],
                        start=False, stop=True)))(), inc="t")
                    dep = [("t", tw)]
                    if half == 0:
                        dep += x2st_free[ca % 3]
                    vres = p.step("vector", (lambda Pb=Pb, half=half, ca=ca:
                        (lambda v: v.tensor_tensor(
                            x2st[ca % 3][:, half * 512:(half + 1) * 512],
                            Pb[:, 0:512],
                            xin[ca % 5][:, half * 512:(half + 1) * 512],
                            op=ALU.add)))(), deps=dep, inc="v")
                    if half == 0:
                        free["Pv"] = [("v", vres)]
                        if ctx_box[0] is not None:
                            ctx_box[0]["T"](0)
                            ctx_box[0]["E"](0)
                            ctx_box[0]["TE0"] = True
                    else:
                        free["Pat"] = [("v", vres)]
                v_resid[ca] = vres
                xin_free[ca % 5] = [("v", vres)]
                atoT_free = [("t", p.cnt["t"])]
                if ca + 5 < NCH:
                    cc2 = ca + 5
                    lane = "dL%d" % (cc2 % 5)
                    dL_x[cc2] = (lane, p.step("sync", (lambda cc2=cc2: (lambda s_: s_.dma_start(
                        out=xin[cc2 % 5][:, :],
                        in_=xs_h[cc2 * 128:(cc2 + 1) * 128, :])))(),
                        deps=[("v", vres)], inc=lane, inc_by=16))

                dS_x2[ca] = ("dS%d" % (ca % 3), p.step("sync", (lambda ca=ca:
                    (lambda s_: s_.dma_start(out=x2_h[ca * 128:(ca + 1) * 128, :],
                                 in_=x2st[ca % 3][:, :])))(),
                    deps=[("v", vres)], inc="dS%d" % (ca % 3), inc_by=16))

                if ca + 1 < NCH:
                    late2.extend(ln2_bundle(ca))
                else:
                    for _f in ln2_bundle(ca):
                        _f()

                if cga == 0 and 2 * g_ + 3 < NCH:
                    for cv in (2 * g_ + 2, 2 * g_ + 3):
                        for half in (0, 1):
                            nxt.append(v_bundle(cv, half))
                    for c4 in (2 * g_ + 4, 2 * g_ + 5):
                        if c4 < NCH:
                            nxt.append((lambda c4=c4: ln1_stats(c4)))
                    for c4 in (2 * g_ + 4, 2 * g_ + 5):
                        if c4 < NCH:
                            nxt.append((lambda c4=c4: ln1_trans(c4)))

            while nxt:
                nxt.pop(0)()
            ln1T_free[gp] = [("t", p.cnt["t"])]
            qT_free[gp] = [("t", p.cnt["t"])]
            kT_free[gp] = [("t", p.cnt["t"])]
            vsb_free[gp] = [("t", p.cnt["t"])]

        while late2:
            late2.pop(0)()
        tA_end = p.cnt["t"]

        if upto == "A":
            return _emit(p, block, sems, nc)
        v_last = g_xn[NCH - 1]
        t_am = p.step("tensor", lambda t: t.transpose(
            Pq[0:1, 0:128], amr[:, 0:1], idnf[:, :]),
            deps=[("v", v_last)] + free["Pq"] + psc_free[0] + psc_free[1], inc="t")
        v_am = p.step("vector", lambda v: v.reduce_max(
            sca[0:1, 0:1], Pq[0:1, 0:128], axis=AX), deps=[("t", t_am)], inc="v")
        free["Pq"] = [("v", v_am)]
        dg1 = p.step("gpsimd", lambda g: g.dma_start(out=cc_in[:], in_=sca[0:1, 0:1]),
                     deps=[("v", v_am)], inc="dGm", inc_by=16)
        c1 = p.step("gpsimd", lambda g: g.collective_compute(
            "AllGather", ALU.bypass, replica_groups=[list(range(8))],
            ins=[cc_in[:]], outs=[cc_out[:]]), deps=[("dGm", dg1)], inc="c")
        dg2 = p.step("gpsimd", lambda g: g.dma_start(out=gmx[0:1, :], in_=cc_out[:]),
                     deps=[("c", c1)], inc="dGm", inc_by=16)

        wfs2 = wo[:, 6144:8192].bitcast(F32)
        wfsb = [wfs, wfs2]
        absb2 = mskb
        WL = ("dW0", "dW1")
        dgw = {}
        for ob in range(2):
            dgw[ob] = p.step("sync", (lambda ob=ob: (lambda s_: s_.dma_start(
                out=wfsb[ob][:, :], in_=wf_h[ob * 128:(ob + 1) * 128, :])))(),
                deps=[("t", tA_end)], inc=WL[ob % 2], inc_by=16)
        vcs = None
        sab_d = {}
        vcs_d = {}
        for ob in range(KB):
            ab = absb if ob % 2 == 0 else absb2
            dep = [(WL[ob % 2], dgw[ob]), ("t", tA_end)]
            if ob >= 2:
                dep += [vcs_d[ob - 2]]
            sab = p.step("scalar", (lambda ob=ob, ab=ab: (lambda s_: s_.activation(
                ab[:, :], wfsb[ob % 2][:, :], AF.Abs)))(), deps=dep, inc="s")
            sab_d[ob] = ("s", sab)
            if ob + 2 < KB:
                dgw[ob + 2] = p.step("sync", (lambda ob=ob: (lambda s_: s_.dma_start(
                    out=wfsb[ob % 2][:, :],
                    in_=wf_h[(ob + 2) * 128:(ob + 3) * 128, :])))(),
                    deps=[("s", sab)], inc=WL[ob % 2], inc_by=16)
            vcs = p.step("vector", (lambda ob=ob, ab=ab: (lambda v: v.reduce_sum(
                csab[:, ob:ob + 1], ab[:, :], axis=AX)))(), deps=[("s", sab)], inc="v")
            vcs_d[ob] = ("v", vcs)
        v_cs = p.step("vector", lambda v: v.reduce_sum(amt[:, 0:1], csab[:, :], axis=AX),
                      deps=[("v", vcs)], inc="v")
        t_tot = p.step("tensor", lambda t: t.matmul(
            Pq[0:1, 0:1], amt[:, 0:1], onecf[:, 0:1], start=True, stop=True),
            deps=[("v", v_cs)] + free["Pq"], inc="t")
        v_dl = p.step("vector", lambda v: v.tensor_scalar(
            tots[0:1, 0:1], Pq[0:1, 0:1], 0.7 / (D * D), None, op0=ALU.mult),
            deps=[("t", t_tot)], inc="v")
        free["Pq"] = [("v", v_dl)]
        t_dl = p.step("tensor", lambda t: t.matmul(
            Pk[:, 0:1], onesf[0:1, :], tots[0:1, 0:1], start=True, stop=True),
            deps=[("v", v_dl)] + free["Pq"], inc="t")
        v_dl2 = p.step("vector", lambda v: v.tensor_copy(dlpp[:, 0:1], Pk[:, 0:1]),
                       deps=[("t", t_dl)], inc="v")
        free["Pq"] = [("v", v_dl2)]
        vt = None
        vs2 = None
        ssg_d = {}
        vt_d = {}
        for ob in range(KB):
            dg = p.step("sync", (lambda ob=ob: (lambda s_: s_.dma_start(
                out=wfsb[ob % 2][:, :], in_=wf_h[ob * 128:(ob + 1) * 128, :])))(),
                deps=[ssg_d[ob - 2]] if ob >= 2 else [sab_d[KB - 2 + ob]],
                inc=WL[ob % 2], inc_by=16)
            dep = [(WL[ob % 2], dg), ("v", v_dl2)]
            if ob >= 1:
                dep += [vt_d[ob - 1]]
            if ob == 0:
                dep += [vcs_d[KB - 1]]
            sab = p.step("scalar", (lambda ob=ob: (lambda s_: s_.activation(
                absb[:, :], wfsb[ob % 2][:, :], AF.Abs)))(), deps=dep, inc="s")
            ssg = p.step("scalar", (lambda ob=ob: (lambda s_: s_.activation(
                sgnb, wfsb[ob % 2][:, :], AF.Sign)))(), inc="s")
            ssg_d[ob] = ("s", ssg)
            vm = p.step("vector", (lambda ob=ob: (lambda v: v.tensor_scalar(
                mskb, absb[:, :], dlpp[:, 0:1], None, op0=ALU.is_gt, op1=ALU.add,
                accum_out=cnta[:, ob:ob + 1])))(),
                deps=[("s", sab)] + ([vt_d[ob - 1]] if ob >= 1 else []), inc="v")
            vs2 = p.step("vector", (lambda ob=ob: (lambda v: v.scalar_tensor_tensor(
                absb[:, :], absb[:, :], 1.0, mskb, op0=ALU.mult, op1=ALU.mult,
                accum_out=suma[:, ob:ob + 1])))(), deps=[("v", vm)], inc="v")
            vt = p.step("vector", (lambda ob=ob: (lambda v: v.tensor_tensor(
                tbf[:, ob * D:(ob + 1) * D], sgnb, mskb, op=ALU.mult)))(),
                deps=[("s", ssg)] + ([("t", tA_end)] if ob == 0 else []), inc="v")
            vt_d[ob] = ("v", vt)
        v_c1 = p.step("vector", lambda v: v.reduce_sum(amt[:, 0:1], cnta[:, :], axis=AX),
                      deps=[("v", vs2)], inc="v")
        t_c1 = p.step("tensor", lambda t: t.matmul(
            Pq[0:1, 0:1], amt[:, 0:1], onecf[:, 0:1], start=True, stop=True),
            deps=[("v", v_c1)] + free["Pq"], inc="t")
        v_c2 = p.step("vector", lambda v: v.tensor_scalar(
            tots[0:1, 1:2], Pq[0:1, 0:1], 1.0, None, op0=ALU.max),
            deps=[("t", t_c1)], inc="v")
        v_s1 = p.step("vector", lambda v: v.reduce_sum(amt[:, 0:1], suma[:, :], axis=AX),
                      inc="v")
        t_s1 = p.step("tensor", lambda t: t.matmul(
            Pq[0:1, 1:2], amt[:, 0:1], onecf[:, 0:1], start=True, stop=True),
            deps=[("v", v_s1)], inc="t")
        v_al = p.step("vector", lambda v: v.reciprocal(tots[0:1, 2:3], tots[0:1, 1:2]),
                      deps=[("v", v_c2)], inc="v")
        p.step("vector", lambda v: v.tensor_tensor(
            tots[0:1, 3:4], Pq[0:1, 1:2], tots[0:1, 2:3], op=ALU.mult),
            deps=[("t", t_s1), ("v", v_al)], inc="v")

        v_wT = None
        for ib in range(KB):
            t_w = None
            for ob in range(KB):
                dep = []
                if ob == 0:
                    dep = list(free["Pm"])
                    if ib == 0:
                        dep += [("v", vt)]
                t_w = p.step("tensor", (lambda ib=ib, ob=ob: (lambda t: t.transpose(
                    Pm[:, ob * 128:(ob + 1) * 128],
                    tbf[:, ob * D + ib * 128:ob * D + (ib + 1) * 128], idn[:, :])))(),
                    deps=dep, inc="t" if ob == KB - 1 else None)
            v_wT = p.step("vector", (lambda ib=ib: (lambda v: v.tensor_copy(
                wT[:, ib * D:(ib + 1) * D], Pm[:, 0:1024])))(), deps=[("t", t_w)], inc="v")
            free["Pm"] = [("v", v_wT)]

        v_gm = p.step("vector", lambda v: v.reduce_max(sca[0:1, 0:1], gmx[0:1, :], axis=AX),
                      deps=[("dGm", dg2)], inc="v")
        p.step("vector", lambda v: v.tensor_scalar(
            sca[0:1, 1:2], sca[0:1, 0:1], 1e-8, 1.0 / QMAX, op0=ALU.max, op1=ALU.mult))
        v_iv = p.step("vector", lambda v: v.reciprocal(sca[0:1, 2:3], sca[0:1, 1:2]),
                      inc="v")
        t_iv = p.step("tensor", lambda t: t.matmul(
            Pq[:, 0:1], onesf[0:1, :], sca[0:1, 2:3], start=True, stop=True),
            deps=[("v", v_iv)] + free["Pq"], inc="t")
        v_iv2 = p.step("vector", lambda v: v.tensor_copy(ivpp[:, 0:1], Pq[:, 0:1]),
                       deps=[("t", t_iv)], inc="v")
        free["Pq"] = [("v", v_iv2)]
        v_G = p.step("vector", lambda v: v.tensor_scalar(
            gamb[:, :], gamb[:, :], ivpp[:, 0:1], None, op0=ALU.mult),
            deps=[("v", v_iv2)], inc="v")
        v_B = p.step("vector", lambda v: v.tensor_scalar(
            betb[:, :], betb[:, :], ivpp[:, 0:1], None, op0=ALU.mult), inc="v")
        v_sa = p.step("vector", lambda v: v.tensor_tensor(
            tots[0:1, 4:5], tots[0:1, 3:4], sca[0:1, 1:2], op=ALU.mult), inc="v")
        t_sa = p.step("tensor", lambda t: t.matmul(
            Pk[:, 0:1], onesf[0:1, :], tots[0:1, 4:5], start=True, stop=True),
            deps=[("v", v_sa)] + free["Pq"], inc="t")
        v_sa2 = p.step("vector", lambda v: v.tensor_copy(sapp[:, 0:1], Pk[:, 0:1]),
                       deps=[("t", t_sa)], inc="v")

        if upto == "bar":
            return _emit(p, block, sems, nc)
        x2done = [("dS0", p.cnt["dS0"]), ("dS1", p.cnt["dS1"]), ("dS2", p.cnt["dS2"])]
        x2in_free = dict(xin_free)
        dL_b = {}
        for c in range(3):
            lane = "dL%d" % (c % 5)
            dL_b[c] = (lane, p.step("sync", (lambda c=c: (lambda s_: s_.dma_start(
                out=xin[c % 5][:, :], in_=x2_h[c * 128:(c + 1) * 128, :])))(),
                deps=x2in_free[c % 5] + x2done + [("v", v_resid[NCH - 1])],
                inc=lane, inc_by=16))
        rpre_free = [[("t", tA_end)], [("t", tA_end)]]
        rbf_free = [[], []]
        rT_free = [[], []]
        yt_free = [[], []]
        ysb_free = [[], []]
        dS_y = {}
        vy = None
        def b_stage1(c):
            cp = c % 2
            slot = c % 5
            sx = p.step("scalar", (lambda c=c, slot=slot: (lambda s_: s_.activation(
                xh2[0][:, :], xin[slot][:, :], AF.Identity,
                bias=b2a[:, c:c + 1], scale=rs2a[:, c:c + 1])))(),
                deps=[dL_b[c], v_b2[c]] + xh2_free[0], inc="s")
            v1 = p.step("vector", (lambda cp=cp: (lambda v: v.tensor_tensor(
                rpre[cp], xh2[0][:, :], gamb[:, :], op=ALU.mult)))(),
                deps=[("s", sx), ("v", v_G)] + rpre_free[cp], inc="v")
            xh2_free[0] = [("v", v1)]
            p.step("vector", (lambda cp=cp: (lambda v: v.tensor_tensor(
                rpre[cp], rpre[cp], betb[:, :], op=ALU.add)))(),
                deps=[("v", v_B)], inc="v")
            v3 = p.step("vector", (lambda cp=cp: (lambda v: v.tensor_scalar(
                rbf[cp], rpre[cp], MAGIC, MAGIC,
                op0=ALU.add, op1=ALU.subtract)))(), deps=rbf_free[cp], inc="v")
            rpre_free[cp] = [("v", v3)]
            return v3

        def b_trans(c, v3):
            cp = c % 2
            t_rT = None
            for kb in range(KB):
                dep = []
                if kb == 0:
                    dep = [("v", v3), ("v", v_wT)] + free["Pm"]
                t_rT = p.step("tensor", (lambda kb=kb, cp=cp: (lambda t: t.transpose(
                    Pm[:, kb * 128:(kb + 1) * 128],
                    rbf[cp][:, kb * 128:(kb + 1) * 128], idn[:, :])))(),
                    deps=dep, inc="t" if kb == KB - 1 else None)
            vrT = p.step("vector", (lambda cp=cp: (lambda v: v.tensor_copy(
                rT[cp], Pm[:, 0:1024])))(), deps=[("t", t_rT)] + rT_free[cp], inc="v")
            free["Pm"] = [("v", vrT)]
            rbf_free[cp] = [("v", vrT)]
            return vrT

        free["Pat"] = list(free["Pat"])
        v3n = b_stage1(0)
        vrTn = b_trans(0, v3n)
        for c in range(NCH):
            cp = c % 2
            slot = c % 5
            vrT = vrTn
            if c + 3 < NCH:
                c2 = c + 3
                lane = "dL%d" % (c2 % 5)
                dL_b[c2] = (lane, p.step("sync", (lambda c2=c2: (lambda s_: s_.dma_start(
                    out=xin[c2 % 5][:, :], in_=x2_h[c2 * 128:(c2 + 1) * 128, :])))(),
                    deps=x2in_free[c2 % 5], inc=lane, inc_by=16))
            v3n = b_stage1(c + 1) if c + 1 < NCH else None
            tf = {}
            for half in range(2):
                Pb = Pv1 if half == 0 else Pat
                bank = "Pv" if half == 0 else "Pat"
                t_f = None
                for kb in range(KB):
                    dep = []
                    if kb == 0:
                        dep = [("v", vrT)] + free[bank]
                    t_f = p.step("tensor", (lambda Pb=Pb, half=half, kb=kb, cp=cp:
                        (lambda t: t.matmul(
                            Pb[:, 0:512], rT[cp][:, kb * 128:(kb + 1) * 128],
                            wT[:, kb * D + half * 512:kb * D + (half + 1) * 512],
                            start=(kb == 0), stop=(kb == KB - 1))))(),
                        deps=dep, inc="t" if kb == KB - 1 else None)
                tf[half] = t_f
            vy = None
            for half in range(2):
                Pb = Pv1 if half == 0 else Pat
                bank = "Pv" if half == 0 else "Pat"
                dep = [("t", tf[half]), ("v", v_sa2)]
                if half == 0:
                    dep += yt_free[cp]
                sy = p.step("scalar", (lambda Pb=Pb, half=half, cp=cp:
                    (lambda s_: s_.activation(
                        yt[cp][:, half * 512:(half + 1) * 512], Pb[:, 0:512],
                        AF.Identity, scale=sapp[:, 0:1])))(), deps=dep, inc="s")
                free[bank] = [("s", sy)]
                dep = [("s", sy)]
                if half == 0:
                    dep += ysb_free[cp]
                vy = p.step("vector", (lambda half=half, cp=cp, slot=slot:
                    (lambda v: v.tensor_tensor(
                        ysb[cp][:, half * 512:(half + 1) * 512],
                        yt[cp][:, half * 512:(half + 1) * 512],
                        xin[slot][:, half * 512:(half + 1) * 512], op=ALU.add)))(),
                    deps=dep, inc="v")
            yt_free[cp] = [("v", vy)]
            x2in_free[slot] = [("v", vy)]
            rT_free[cp] = [("t", p.cnt["t"])]
            if v3n is not None:
                vrTn = b_trans(c + 1, v3n)
            dS_y[c] = ("dS%d" % cp, p.step("sync", (lambda c=c, cp=cp: (lambda s_:
                s_.dma_start(out=y_h[c * 128:(c + 1) * 128, :],
                             in_=ysb[cp][:, :])))(),
                deps=[("v", vy)], inc="dS%d" % cp, inc_by=16))
            ysb_free[cp] = [dS_y[c]]

        fin_deps = [("dS0", p.cnt["dS0"]), ("dS1", p.cnt["dS1"])]
        if p.cnt["dbg"]:
            fin_deps.append(("dbg", p.cnt["dbg"]))
        p.step("sync", lambda s_: s_.nop(), deps=fin_deps)
        return _emit(p, block, sems, nc)


def _emit(p, block, sems, nc):
    @block.sync
    def _(eng):
        p.run("sync", eng, sems)

    @block.scalar
    def _(eng):
        p.run("scalar", eng, sems)

    @block.vector
    def _(eng):
        p.run("vector", eng, sems)

    @block.tensor
    def _(eng):
        p.run("tensor", eng, sems)

    @block.gpsimd
    def _(eng):
        p.run("gpsimd", eng, sems)
    return nc


_NC_CACHE = None


def kernel(L0_attn_x, L0_attn_Wq, L0_attn_bq, L0_attn_Wk, L0_attn_bk,
           L0_attn_Wv, L0_attn_bv, L0_attn_Wo, L0_attn_bo, L0_attn_decay,
           L0_ffn_weight, L0_ffn_gamma, L0_ffn_beta):
    global _NC_CACHE
    x = np.asarray(L0_attn_x)
    B, L, _ = x.shape
    assert (B, L) == (2, 16384)
    if _NC_CACHE is None:
        _NC_CACHE = build_nc()
    nc = _NC_CACHE

    def row(a):
        return np.ascontiguousarray(np.asarray(a, dtype=np.float32).reshape(1, -1))

    shared = {
        "wq": np.ascontiguousarray(L0_attn_Wq, dtype=np.float32),
        "wk": np.ascontiguousarray(L0_attn_Wk, dtype=np.float32),
        "wv": np.ascontiguousarray(L0_attn_Wv, dtype=np.float32),
        "wo": np.ascontiguousarray(L0_attn_Wo, dtype=np.float32),
        "bq": row(L0_attn_bq), "bk": row(L0_attn_bk),
        "bv": row(L0_attn_bv), "bo": row(L0_attn_bo),
        "dec": row(L0_attn_decay),
        "wf": np.ascontiguousarray(L0_ffn_weight, dtype=np.float32),
        "gam": row(L0_ffn_gamma), "bet": row(L0_ffn_beta),
    }
    in_maps = []
    for i in range(8):
        b = i // 4
        t0 = (i % 4) * NT
        m = dict(shared)
        m["xs"] = np.ascontiguousarray(x[b, t0:t0 + NT, :], dtype=np.float32)
        if i % 4 == 0:
            m["xh"] = np.ascontiguousarray(x[b, 0:1, :], dtype=np.float32)
            m["cm"] = np.zeros((1, 1), dtype=np.float32)
        else:
            m["xh"] = np.ascontiguousarray(x[b, t0 - 1:t0, :], dtype=np.float32)
            m["cm"] = np.ones((1, 1), dtype=np.float32)
        in_maps.append(m)

    res = run_bass_kernel_spmd(nc, in_maps, core_ids=list(range(8)))
    out = np.empty((2, 16384, D), dtype=np.float32)
    for i in range(8):
        b = i // 4
        t0 = (i % 4) * NT
        out[b, t0:t0 + NT, :] = np.asarray(res.results[i]["y"], dtype=np.float32)
    return out
